# version 04
# baseline (speedup 1.0000x reference)
import numpy as np
from contextlib import ExitStack

import concourse.bass as bass
import concourse.mybir as mybir
from concourse.bass_utils import run_bass_kernel_spmd

F32 = mybir.dt.float32
BF16 = mybir.dt.bfloat16
AF = mybir.ActivationFunctionType
ALU = mybir.AluOpType
AX = mybir.AxisListType.X

B, S, H = 4, 2048, 2048
NH, HD = 16, 128
NCH = H // 128
TKV, TQ = 2048, 1024
E, TOPK, I = 16, 4, 1536
NI = I // 128
MAGIC = 12582912.0
PI = float(np.pi)
INV2PI = float(1.0 / (2.0 * np.pi))


class Ctr:
    __slots__ = ("h", "v")

    def __init__(self, h):
        self.h = h
        self.v = 0

    def inc(self, ins, n=1):
        ins.then_inc(self.h, n)
        self.v += n
        return self.v


def build(debug=False):
    _sid = {}
    nc = bass.Bass("TRN2", target_bir_lowering=False,
                   dynamic_dma_scratch_size=16384)

    x_d = nc.dram_tensor("x", [TKV, H], F32, kind="ExternalInput").ap()
    wq_d = nc.dram_tensor("wq", [H, H], F32, kind="ExternalInput").ap()
    wk_d = nc.dram_tensor("wk", [H, H], F32, kind="ExternalInput").ap()
    wv_d = nc.dram_tensor("wv", [H, H], F32, kind="ExternalInput").ap()
    wo_d = nc.dram_tensor("wo", [H, H], F32, kind="ExternalInput").ap()
    gate_d = nc.dram_tensor("gate", [1, NH], F32, kind="ExternalInput").ap()
    gw_d = nc.dram_tensor("gw", [H, E], F32, kind="ExternalInput").ap()
    wg_d = nc.dram_tensor("wg", [E, H, I], F32, kind="ExternalInput").ap()
    wu_d = nc.dram_tensor("wu", [E, H, I], F32, kind="ExternalInput").ap()
    wd_d = nc.dram_tensor("wd", [E, I, H], F32, kind="ExternalInput").ap()
    sg_d = nc.dram_tensor("sg", [H, I], F32, kind="ExternalInput").ap()
    su_d = nc.dram_tensor("su", [H, I], F32, kind="ExternalInput").ap()
    sd_d = nc.dram_tensor("sd", [I, H], F32, kind="ExternalInput").ap()
    tb_d = nc.dram_tensor("tbase", [1, 4], F32, kind="ExternalInput").ap()
    out_d = nc.dram_tensor("out", [TQ, H], F32, kind="ExternalOutput").ap()
    if debug:
        dbg = {
            "d_xnT0": nc.dram_tensor("d_xnT0", [128, NCH * 1024], BF16,
                                     kind="ExternalOutput").ap(),
            "d_xnT1": nc.dram_tensor("d_xnT1", [128, NCH * 1024], BF16,
                                     kind="ExternalOutput").ap(),
            "d_comb": nc.dram_tensor("d_comb", [128, 128], F32,
                                     kind="ExternalOutput").ap(),
            "d_combF": nc.dram_tensor("d_combF", [16, 1024], BF16,
                                      kind="ExternalOutput").ap(),
            "d_kbf0": nc.dram_tensor("d_kbf0", [128, 2048], BF16,
                                     kind="ExternalOutput").ap(),
            "d_kbf1": nc.dram_tensor("d_kbf1", [128, 2048], BF16,
                                     kind="ExternalOutput").ap(),
            "d_kfbf0": nc.dram_tensor("d_kfbf0", [128, 2048], BF16,
                                      kind="ExternalOutput").ap(),
            "d_qbf0": nc.dram_tensor("d_qbf0", [128, 1024], BF16,
                                     kind="ExternalOutput").ap(),
            "d_qfbf0": nc.dram_tensor("d_qfbf0", [128, 1024], BF16,
                                      kind="ExternalOutput").ap(),
            "d_vab0": nc.dram_tensor("d_vab0", [128, NCH * 132], BF16,
                                     kind="ExternalOutput").ap(),
            "d_memaug0": nc.dram_tensor("d_memaug0", [128, 132], BF16,
                                        kind="ExternalOutput").ap(),
            "d_memaug1": nc.dram_tensor("d_memaug1", [128, 132], BF16,
                                        kind="ExternalOutput").ap(),
            "d_negM": nc.dram_tensor("d_negM", [128, 1], F32,
                                     kind="ExternalOutput").ap(),
            "d_mscr": nc.dram_tensor("d_mscr", [1, 8], F32,
                                     kind="ExternalOutput").ap(),
            "d_ptil": nc.dram_tensor("d_ptil", [128, NCH * 128], BF16,
                                     kind="ExternalOutput").ap(),
            "d_combined": nc.dram_tensor("d_combined", [128, 8 * 2048], BF16,
                                         kind="ExternalOutput").ap(),
            "d_ySBwo": nc.dram_tensor("d_ySBwo", [128, NCH * 1024], F32,
                                      kind="ExternalOutput").ap(),
            "d_h1sb": nc.dram_tensor("d_h1sb", [128, 12 * 512], BF16,
                                     kind="ExternalOutput").ap(),
        }

    es = ExitStack()
    sb = lambda n, s, d, **kw: es.enter_context(nc.sbuf_tensor(n, s, d, **kw))
    psn = lambda n, s, d: es.enter_context(nc.psum_tensor(n, s, d))
    mksem = lambda n: Ctr(es.enter_context(nc.semaphore(n)))

    s_pe = mksem("s_pe")
    s_act = mksem("s_act")
    s_dve = mksem("s_dve")
    s_pool = mksem("s_pool")
    s_ldx = mksem("s_ldx")
    s_wl = mksem("s_wl")
    s_st = mksem("s_st")
    s_dbg = mksem("s_dbg")
    s_ve = mksem("s_ve")
    s_ae = mksem("s_ae")

    def vedge(v, ins):
        s_ve.inc(ins)
        v.wait_ge(s_ve.h, s_ve.v)

    def aedge(sc, ins):
        s_ae.inc(ins)
        sc.wait_ge(s_ae.h, s_ae.v)

    xnT0 = sb("xnT0", [128, NCH * 1024], BF16)
    combF = sb("combF", [16, 1024], BF16)
    identf = sb("identf", [128, 128], F32)
    iota16 = sb("iota16", [16, 128], F32)
    sel_sb = sb("sel_sb", [16, 128], BF16)

    esW = ExitStack()
    sbW = lambda n, s, d: esW.enter_context(nc.sbuf_tensor(n, s, d))
    comb_all = sbW("comb_all", [128, 8 * 16], F32)
    g_bc = sbW("g_bc", [128, NH], F32)
    g1m_bc = sbW("g1m_bc", [128, NH], F32)
    identb = sbW("identb", [128, 128], BF16)
    onesb128 = sbW("onesb128", [128, 128], BF16)
    ones_row_b = sbW("ones_row_b", [1, 128], BF16)
    ones_col_b = sbW("ones_col_b", [128, 1], BF16)
    invf_tab = sbW("invf_tab", [128, 8], F32)
    tb_bc = sbW("tb_bc", [128, 4], F32)
    cst = sbW("cst", [128, 4], F32)
    C_EPS, C_HALFPI = 0, 1
    gate_sb = sbW("gate_sb", [1, NH], F32)
    gate_bf = sbW("gate_bf", [1, NH], BF16)
    tb_sb = sbW("tb_sb", [1, 4], F32)
    tb_bf = sbW("tb_bf", [1, 4], BF16)
    negM = sbW("negM", [128, 1], F32)
    mscr = sbW("mscr", [1, 8], F32)
    mbf = sbW("mbf", [1, 1], BF16)

    pA = psn("pA", [128, 512], F32)
    pB = psn("pB", [128, 512], F32)
    pC = psn("pC", [128, 512], F32)
    pD = psn("pD", [128, 512], F32)
    pE = psn("pE", [128, 512], F32)
    pF = psn("pF", [128, 512], F32)
    pT1 = psn("pT1", [128, 1024], BF16)
    pT2 = psn("pT2", [128, 1024], BF16)

    bank_free = {}

    def bank_wait(eng, bank):
        ent = bank_free.get(bank)
        if ent is not None:
            ctr, val = ent
            eng.wait_ge(ctr.h, val)

    def bank_consumed(bank, ctr):
        bank_free[bank] = (ctr, ctr.v)

    with nc.Block("init") as blk:
        @blk.sync
        def _(sy):
            sy.dma_start(out=gate_sb[:], in_=gate_d).then_inc(s_ldx.h, 16)
            sy.dma_start(out=tb_sb[:], in_=tb_d).then_inc(s_ldx.h, 16)
            s_ldx.v += 32

        @blk.vector
        def _(v):
            v.memset(ones_row_b[:], 1.0)
            v.memset(ones_col_b[:], 1.0)
            v.memset(onesb128[:], 1.0)
            v.memset(cst[:, C_EPS:C_EPS + 1], 1e-5)
            ins = v.memset(cst[:, C_HALFPI:C_HALFPI + 1], PI / 2.0)
            s_dve.inc(ins)

        @blk.gpsimd
        def _(g):
            g.wait_ge(s_dve.h, 1)
            g.affine_select(identb[:], onesb128[:], [[1, 128]], ALU.is_equal,
                            0.0, base=0, channel_multiplier=-1)
            g.iota(invf_tab[:], [[128, 8]], base=0, channel_multiplier=1,
                   allow_small_or_imprecise_dtypes=True)
            ins = g.iota(iota16[:], [[0, 128]], base=0, channel_multiplier=1,
                         allow_small_or_imprecise_dtypes=True)
            s_pool.inc(ins)

        @blk.scalar
        def _(sc):
            sc.wait_ge(s_pool.h, 1)
            sc.copy(identf[:], identb[:])
            sc.activation(invf_tab[:], invf_tab[:], AF.Exp,
                          scale=float(-np.log(10000.0) / 1024.0))
            sc.wait_ge(s_ldx.h, 32)
            aedge(sc, sc.activation(gate_sb[:], gate_sb[:], AF.Sigmoid))
            sc.copy(gate_bf[:], gate_sb[:])
            ins = sc.copy(tb_bf[:], tb_sb[:])
            s_act.inc(ins)

        @blk.tensor
        def _(t):
            t.wait_ge(s_act.h, 1)
            t.matmul(pC[:, 0:16], ones_row_b[:], gate_bf[:], start=True, stop=True)
            ins = t.matmul(pC[:, 16:20], ones_row_b[:], tb_bf[:],
                           start=True, stop=True)
            s_pe.inc(ins)

        @blk.scalar
        def _(sc):
            sc.wait_ge(s_pe.h, 1)
            sc.copy(g_bc[:], pC[:, 0:16])
            ins = sc.copy(tb_bc[:], pC[:, 16:20])
            s_act.inc(ins)

        @blk.vector
        def _(v):
            v.wait_ge(s_act.h, 2)
            ins = v.tensor_scalar(g1m_bc[:], g_bc[:], -1.0, 1.0, ALU.mult, ALU.add)
            s_dve.inc(ins)
        bank_consumed("pC", s_act)

    esCm = ExitStack()
    combined = esCm.enter_context(
        nc.sbuf_tensor("combined", [128, 8 * 2048], BF16))
    esX = ExitStack()
    xnT1 = esX.enter_context(nc.sbuf_tensor("xnT1", [128, NCH * 1024], BF16))
    iota512 = esX.enter_context(nc.sbuf_tensor("iota512", [128, 512], F32))
    gw_sb = esX.enter_context(nc.sbuf_tensor("gw_sb", [128, NCH * E], F32))

    esA = ExitStack()
    sbA = lambda n, s, d: esA.enter_context(nc.sbuf_tensor(n, s, d))
    xt = [sbA(f"xt{i}", [128, H], F32) for i in range(2)]
    xnf = [sbA(f"xnf{i}", [128, H], F32) for i in range(2)]
    xnb = [sbA(f"xnb{i}", [128, H], BF16) for i in range(2)]
    sqscr = sbA("sqscr", [128, H], F32)
    xnTf = sbA("xnTf", [128, NCH * 128], F32)
    lncol = sbA("lncol", [128, 8], F32)
    CL_SUM, CL_NEGMEAN, CL_M2, CL_VAR, CL_RSTD = 0, 1, 2, 3, 4
    CL_SQP = [5, 6]
    Ltile = sbA("Ltile", [128, 16], F32)
    topv = sbA("topv", [128, 8], F32)
    texp = sbA("texp", [128, 4], F32)
    probs = sbA("probs", [128, 4], F32)
    pcc = sbA("pcc", [128, 2], F32)
    maskt = sbA("maskt", [128, 16], F32)
    combsc = [sbA(f"combsc{i}", [128, 16], F32) for i in range(2)]

    with nc.Block("ginit") as blk:
        @blk.sync
        def _(sy):
            ins = sy.dma_start(out=gw_sb[:],
                               in_=gw_d.rearrange("(c p) f -> p c f", p=128))
            s_ldx.inc(ins, 16)

        @blk.gpsimd
        def _(g):
            ins = g.iota(iota512[:], [[1, 512]], base=0, channel_multiplier=0,
                         allow_small_or_imprecise_dtypes=True)
            s_pool.inc(ins)

    xt_free = [(s_dve, 0), (s_dve, 0)]
    xn_free = [(s_pe, 0), (s_pe, 0)]
    gw_ready = s_ldx.v
    iota_ready = s_pool.v

    for tt in range(16):
        pp = tt % 2
        dst_xnT = xnT0 if tt < 8 else xnT1
        _sid[f"A{tt}"] = nc.enter_named_scope(f"A{tt}", False)[0]
        toff = 128 * (tt % 8)
        with nc.Block(f"ln{tt}") as blk:
            @blk.sync
            def _(sy):
                ctr, val = xt_free[pp]
                sy.wait_ge(ctr.h, val)
                ins = sy.dma_start(out=xt[pp][:],
                                   in_=x_d[128 * tt:128 * (tt + 1), :])
                s_ldx.inc(ins, 16)
            x_ready = s_ldx.v

            @blk.scalar
            def _(sc, pp=pp):
                sc.wait_ge(s_ldx.h, x_ready)
                sq = CL_SQP[pp]
                ins = sc.activation(sqscr[:], xt[pp][:], AF.Square,
                                    accum_out=lncol[:, sq:sq + 1])
                s_act.inc(ins)
            sq_ready = s_act.v

            @blk.vector
            def _(v):
                v.wait_ge(s_ldx.h, x_ready)
                vedge(v, v.reduce_sum(lncol[:, CL_SUM:CL_SUM + 1], xt[pp][:],
                                      axis=AX))
                vedge(v, v.tensor_scalar(lncol[:, CL_NEGMEAN:CL_NEGMEAN + 1],
                                         lncol[:, CL_SUM:CL_SUM + 1],
                                         float(-1.0 / H), None, ALU.mult))
                vedge(v, v.tensor_tensor(lncol[:, CL_M2:CL_M2 + 1],
                                         lncol[:, CL_NEGMEAN:CL_NEGMEAN + 1],
                                         lncol[:, CL_NEGMEAN:CL_NEGMEAN + 1],
                                         ALU.mult))
                v.wait_ge(s_act.h, sq_ready)
                sq = CL_SQP[pp]
                ins = v.scalar_tensor_tensor(lncol[:, CL_VAR:CL_VAR + 1],
                                             lncol[:, sq:sq + 1],
                                             float(1.0 / H),
                                             lncol[:, CL_M2:CL_M2 + 1],
                                             ALU.mult, ALU.subtract)
                s_dve.inc(ins)
            var_ready = s_dve.v

            @blk.scalar
            def _(sc):
                sc.wait_ge(s_dve.h, var_ready)
                aedge(sc, sc.activation(lncol[:, CL_VAR:CL_VAR + 1],
                                        lncol[:, CL_VAR:CL_VAR + 1], AF.Ln,
                                        bias=cst[:, C_EPS:C_EPS + 1]))
                ins = sc.activation(lncol[:, CL_RSTD:CL_RSTD + 1],
                                    lncol[:, CL_VAR:CL_VAR + 1], AF.Exp,
                                    scale=-0.5)
                s_act.inc(ins)
            rstd_ready = s_act.v

            @blk.vector
            def _(v):
                v.wait_ge(s_act.h, rstd_ready)
                ctr, val = xn_free[pp]
                v.wait_ge(ctr.h, val)
                v.tensor_scalar(xnf[pp][:], xt[pp][:],
                                lncol[:, CL_NEGMEAN:CL_NEGMEAN + 1],
                                lncol[:, CL_RSTD:CL_RSTD + 1],
                                ALU.add, ALU.mult)
                ins = v.tensor_copy(xnb[pp][:], xnf[pp][:])
                s_dve.inc(ins)
            xn_ready = s_dve.v
            xt_free[pp] = (s_dve, s_dve.v)

            for g4 in range(4):
                bank = [pT1, pT2][g4 % 2]
                bname = ["pT1", "pT2"][g4 % 2]

                @blk.tensor
                def _(t, g4=g4, bank=bank, bname=bname):
                    t.wait_ge(s_dve.h, xn_ready)
                    bank_wait(t, bname)
                    last = None
                    for j in range(4):
                        ic = 4 * g4 + j
                        last = t.transpose(bank[:, 128 * j:128 * (j + 1)],
                                           xnb[pp][:, 128 * ic:128 * (ic + 1)],
                                           identb[:])
                    s_pe.inc(last)
                grp_done = s_pe.v

                @blk.scalar
                def _(sc, g4=g4, bank=bank, grp_done=grp_done):
                    sc.wait_ge(s_pe.h, grp_done)
                    dst3 = dst_xnT[:].rearrange("p (c f) -> p c f", f=1024)
                    ins = sc.copy(dst3[:, 4 * g4:4 * g4 + 4, toff:toff + 128],
                                  bank[:, 0:512].rearrange("p (c f) -> p c f",
                                                           f=128))
                    s_act.inc(ins)
                bank_consumed(bname, s_act)

            if tt < 8:
                for g4 in range(4):
                    bank = [pD, pE][g4 % 2]
                    bname = ["pD", "pE"][g4 % 2]

                    @blk.tensor
                    def _(t, g4=g4, bank=bank, bname=bname):
                        t.wait_ge(s_dve.h, xn_ready)
                        bank_wait(t, bname)
                        last = None
                        for j in range(4):
                            ic = 4 * g4 + j
                            last = t.transpose(bank[:, 128 * j:128 * (j + 1)],
                                               xnf[pp][:, 128 * ic:128 * (ic + 1)],
                                               identf[:])
                        s_pe.inc(last)
                    grp_done = s_pe.v

                    @blk.scalar
                    def _(sc, g4=g4, bank=bank, grp_done=grp_done):
                        sc.wait_ge(s_pe.h, grp_done)
                        ins = sc.copy(xnTf[:, 512 * g4:512 * (g4 + 1)], bank[:])
                        s_act.inc(ins)
                    bank_consumed(bname, s_act)
                xnTf_ready = s_act.v
                xn_free[pp] = (s_act, s_act.v)

                @blk.tensor
                def _(t):
                    t.wait_ge(s_act.h, xnTf_ready)
                    t.wait_ge(s_ldx.h, gw_ready)
                    bank_wait(t, "pC")
                    last = None
                    for ic in range(NCH):
                        last = t.matmul(pC[:, 0:16],
                                        xnTf[:, 128 * ic:128 * (ic + 1)],
                                        gw_sb[:, 16 * ic:16 * (ic + 1)],
                                        start=(ic == 0), stop=(ic == NCH - 1))
                    s_pe.inc(last)
                logits_done = s_pe.v

                @blk.vector
                def _(v):
                    v.wait_ge(s_pe.h, logits_done)
                    ins = v.tensor_copy(Ltile[:], pC[:, 0:16])
                    s_dve.inc(ins)
                bank_consumed("pC", s_dve)

                @blk.vector
                def _(v):
                    v.wait_ge(s_dve.h, s_dve.v)
                    vedge(v, v.max(out=topv[:], in_=Ltile[:]))
                    ins = v.tensor_scalar(texp[:], topv[:, 0:4], topv[:, 0:1],
                                          None, ALU.subtract)
                    s_dve.inc(ins)
                texp_ready = s_dve.v

                @blk.scalar
                def _(sc):
                    sc.wait_ge(s_dve.h, texp_ready)
                    ins = sc.activation(texp[:], texp[:], AF.Exp)
                    s_act.inc(ins)
                exp_ready = s_act.v

                @blk.vector
                def _(v):
                    v.wait_ge(s_act.h, exp_ready)
                    vedge(v, v.reduce_sum(pcc[:, 0:1], texp[:], axis=AX))
                    vedge(v, v.reciprocal(pcc[:, 1:2], pcc[:, 0:1]))
                    vedge(v, v.tensor_scalar(probs[:], texp[:], pcc[:, 1:2],
                                             None, ALU.mult))
                    v.memset(combsc[1][:], 0.0)
                    for r in range(TOPK):
                        vedge(v, v.tensor_scalar(maskt[:], Ltile[:],
                                                 topv[:, r:r + 1],
                                                 None, ALU.is_equal))
                        vedge(v, v.reduce_sum(pcc[:, 0:1], maskt[:], axis=AX))
                        vedge(v, v.reciprocal(pcc[:, 1:2], pcc[:, 0:1]))
                        vedge(v, v.tensor_tensor(pcc[:, 0:1], probs[:, r:r + 1],
                                                 pcc[:, 1:2], ALU.mult))
                        vedge(v, v.scalar_tensor_tensor(combsc[r % 2][:],
                                                        maskt[:],
                                                        pcc[:, 0:1],
                                                        combsc[1 - r % 2][:],
                                                        ALU.mult, ALU.add))
                    ins = v.tensor_copy(comb_all[:, 16 * tt:16 * (tt + 1)],
                                        combsc[(TOPK - 1) % 2][:])
                    s_dve.inc(ins)
                comb_ready = s_dve.v

                @blk.tensor
                def _(t):
                    t.wait_ge(s_dve.h, comb_ready)
                    bank_wait(t, "pD")
                    ins = t.transpose(pD[0:16, 0:128],
                                      comb_all[:, 16 * tt:16 * (tt + 1)],
                                      identf[:])
                    s_pe.inc(ins)
                combT_done = s_pe.v

                @blk.scalar
                def _(sc):
                    sc.wait_ge(s_pe.h, combT_done)
                    ins = sc.copy(combF[0:16, 128 * tt:128 * (tt + 1)],
                                  pD[0:16, 0:128])
                    s_act.inc(ins)
                bank_consumed("pD", s_act)
            else:
                xn_free[pp] = (s_act, s_act.v)
        nc.leave_named_scope(f"A{tt}", _sid[f"A{tt}"], False)

    if debug:
        with nc.Block("dbgA") as blk:
            @blk.sync
            def _(sy):
                sy.wait_ge(s_dve.h, s_dve.v)
                sy.wait_ge(s_act.h, s_act.v)
                sy.dma_start(out=dbg["d_xnT0"], in_=xnT0[:]).then_inc(s_dbg.h, 16)
                sy.dma_start(out=dbg["d_xnT1"], in_=xnT1[:]).then_inc(s_dbg.h, 16)
                sy.dma_start(out=dbg["d_comb"], in_=comb_all[:]).then_inc(s_dbg.h, 16)
                sy.dma_start(out=dbg["d_combF"], in_=combF[:]).then_inc(s_dbg.h, 16)
                s_dbg.v += 64

    esA.close()

    esB = ExitStack()
    sbB = lambda n, s, d: esB.enter_context(nc.sbuf_tensor(n, s, d))
    wslab = [sbB(f"wslab{i}", [128, 2048], BF16) for i in range(2)]
    WTq = sbB("WTq", [128, NCH * 256], BF16)
    WTk = sbB("WTk", [128, NCH * 256], BF16)
    WTv = sbB("WTv", [128, NCH * 256], BF16)
    WT = {"q": WTq, "k": WTk, "v": WTv}
    kqst = [sbB(f"kqst{i}", [128, 512], F32) for i in range(2)]
    ropeo = [sbB(f"ropeo{i}", [128, 512], F32) for i in range(2)]
    eluscr = [sbB(f"eluscr{i}", [128, 512], F32) for i in range(2)]
    ang = sbB("ang", [128, 512], F32)
    tscr = sbB("tscr", [128, 512], F32)
    tscr2 = sbB("tscr2", [128, 512], F32)
    sinb = sbB("sinb", [128, 512], F32)
    cosb = sbB("cosb", [128, 512], F32)
    k_bf = [sbB(f"k_bf{i}", [128, 2048], BF16) for i in range(2)]
    kf_bf = [sbB(f"kf_bf{i}", [128, 2048], BF16) for i in range(2)]
    q_bf = [sbB(f"q_bf{i}", [128, 1024], BF16) for i in range(2)]
    qf_bf = [sbB(f"qf_bf{i}", [128, 1024], BF16) for i in range(2)]
    v_stage = sbB("v_stage", [128, 512], BF16)
    sq_bf = sbB("sq_bf", [128, 512], BF16)
    vab = [sbB(f"vab{i}", [128, NCH * 132], BF16) for i in range(2)]
    kftok = [sbB(f"kftok{i}", [128, NCH * 128], BF16) for i in range(2)]
    ptil = sbB("ptil", [128, NCH * 128], BF16)
    mem_aug = [sbB(f"mem_aug{i}", [128, 132], BF16) for i in range(2)]
    cscr = sbB("cscr", [128, 128], F32)
    rcls = sbB("rcls", [128, 4], F32)

    slab_free = [(s_pe, 0)] * 2
    vst_free = (s_pe, 0)
    nload = 0
    nproj = 0
    ntr = 0
    ptil_free_pe = 0
    mbf_free_pe = 0

    def tbank():
        return [(pT1, "pT1"), (pT2, "pT2")][ntr % 2]

    def qslice(ic, tc):
        src = xnT0 if tc < 2 else xnT1
        return src[:, 1024 * ic + 512 * (tc % 2):1024 * ic + 512 * (tc % 2) + 512]

    W_SRC = {"q": wq_d, "k": wk_d, "v": wv_d}

    for p in range(8):
        _sid[f"P{p}"] = nc.enter_named_scope(f"P{p}", False)[0]
        with nc.Block(f"pair{p}") as blk:
            for wi, w in enumerate(("q", "k", "v")):
                for hh in range(2):
                    li = nload
                    nload += 1
                    slot = li % 2
                    hd = p + 8 * hh

                    @blk.gpsimd
                    def _(g, li=li, slot=slot, hd=hd, w=w):
                        ctr, val = slab_free[slot]
                        g.wait_ge(ctr.h, val)
                        ins = g.dma_start(out=wslab[slot][:],
                                          in_=W_SRC[w][128 * hd:128 * (hd + 1), :])
                        s_wl.inc(ins, 16)
                    ld_done = s_wl.v

                    for g8 in range(2):
                        bank, bname = tbank()
                        ntr += 1

                        @blk.tensor
                        def _(t, slot=slot, g8=g8, bank=bank, bname=bname,
                              ld_done=ld_done):
                            t.wait_ge(s_wl.h, ld_done)
                            bank_wait(t, bname)
                            last = None
                            for j in range(8):
                                ic = 8 * g8 + j
                                last = t.transpose(
                                    bank[:, 128 * j:128 * (j + 1)],
                                    wslab[slot][:, 128 * ic:128 * (ic + 1)],
                                    identb[:])
                            s_pe.inc(last)
                        tr_done = s_pe.v

                        @blk.scalar
                        def _(sc, w=w, hh=hh, g8=g8, bank=bank,
                              tr_done=tr_done):
                            sc.wait_ge(s_pe.h, tr_done)
                            dst3 = WT[w][:].rearrange("p (c f) -> p c f", f=256)
                            ins = sc.copy(
                                dst3[:, 8 * g8:8 * g8 + 8, 128 * hh:128 * hh + 128],
                                bank[:].rearrange("p (c f) -> p c f", f=128))
                            s_act.inc(ins)
                        bank_consumed(bname, s_act)
                    slab_free[slot] = (s_pe, s_pe.v)

            @blk.vector
            def _(v):
                v3a = vab[0][:].rearrange("p (c w) -> p c w", w=132)
                v3b = vab[1][:].rearrange("p (c w) -> p c w", w=132)
                v.memset(v3a[:, :, 128:129], 1.0)
                ins = v.memset(v3b[:, :, 128:129], 1.0)
                s_dve.inc(ins)
            vones_dve = s_dve.v

            kbf_ready = [0, 0, 0, 0]
            qbf_ready = [0, 0]

            for tc in range(4):
                @blk.vector
                def _(v, tc=tc, p=p):
                    vedge(v, v.tensor_scalar(ang[:], iota512[:],
                                             tb_bc[:, tc:tc + 1],
                                             invf_tab[:, p:p + 1],
                                             ALU.add, ALU.mult))
                    vedge(v, v.tensor_scalar(tscr[:], ang[:], INV2PI, None,
                                             ALU.mult))
                    vedge(v, v.tensor_scalar(tscr[:], tscr[:], MAGIC, MAGIC,
                                             ALU.add, ALU.subtract))
                    v.scalar_tensor_tensor(tscr[:], tscr[:], float(-2 * PI),
                                           ang[:], ALU.mult, ALU.add)
                    vedge(v, v.tensor_scalar(tscr2[:], ang[:], INV2PI, 0.25,
                                             ALU.mult, ALU.add))
                    vedge(v, v.tensor_scalar(tscr2[:], tscr2[:], MAGIC, MAGIC,
                                             ALU.add, ALU.subtract))
                    ins = v.scalar_tensor_tensor(tscr2[:], tscr2[:],
                                                 float(-2 * PI), ang[:],
                                                 ALU.mult, ALU.add)
                    s_dve.inc(ins)
                trig_dve = s_dve.v

                @blk.scalar
                def _(sc, trig_dve=trig_dve):
                    sc.wait_ge(s_dve.h, trig_dve)
                    sc.activation(sinb[:], tscr[:], AF.Sin)
                    ins = sc.activation(cosb[:], tscr2[:], AF.Sin,
                                        bias=cst[:, C_HALFPI:C_HALFPI + 1])
                    s_act.inc(ins)
                trig_act = s_act.v

                for hh in range(2):
                    bank, bname = [(pA, "pA"), (pB, "pB")][nproj % 2]
                    nproj += 1

                    @blk.tensor
                    def _(t, hh=hh, tc=tc, bank=bank, bname=bname):
                        bank_wait(t, bname)
                        last = None
                        for ic in range(NCH):
                            last = t.matmul(
                                bank[:],
                                WTv[:, 256 * ic + 128 * hh:256 * ic + 128 * hh + 128],
                                qslice(ic, tc),
                                start=(ic == 0), stop=(ic == NCH - 1))
                        s_pe.inc(last)
                    mm_done = s_pe.v

                    @blk.scalar
                    def _(sc, bank=bank, mm_done=mm_done):
                        ctr, val = vst_free
                        sc.wait_ge(ctr.h, val)
                        sc.wait_ge(s_pe.h, mm_done)
                        ins = sc.copy(v_stage[:], bank[:])
                        s_act.inc(ins)
                    bank_consumed(bname, s_act)
                    vst_act = s_act.v

                    bank2, bname2 = tbank()
                    ntr += 1

                    @blk.tensor
                    def _(t, hh=hh, tc=tc, bank2=bank2, bname2=bname2,
                          vst_act=vst_act, vones_dve=vones_dve):
                        t.wait_ge(s_act.h, vst_act)
                        t.wait_ge(s_dve.h, vones_dve)
                        bank_wait(t, bname2)
                        last = None
                        for j in range(4):
                            last = t.transpose(bank2[:, 128 * j:128 * (j + 1)],
                                               v_stage[:, 128 * j:128 * (j + 1)],
                                               identb[:])
                        s_pe.inc(last)
                    vtr_done = s_pe.v
                    vst_free = (s_pe, s_pe.v)

                    @blk.scalar
                    def _(sc, hh=hh, tc=tc, bank2=bank2, vtr_done=vtr_done):
                        sc.wait_ge(s_pe.h, vtr_done)
                        dst3 = vab[hh][:].rearrange("p (c w) -> p c w", w=132)
                        ins = sc.copy(dst3[:, 4 * tc:4 * tc + 4, 0:128],
                                      bank2[:, 0:512].rearrange(
                                          "p (c f) -> p c f", f=128))
                        s_act.inc(ins)
                    bank_consumed(bname2, s_act)

                for hh in range(2):
                    bank, bname = [(pA, "pA"), (pB, "pB")][nproj % 2]
                    nproj += 1

                    @blk.tensor
                    def _(t, hh=hh, tc=tc, bank=bank, bname=bname):
                        bank_wait(t, bname)
                        last = None
                        for ic in range(NCH):
                            last = t.matmul(
                                bank[:],
                                WTk[:, 256 * ic + 128 * hh:256 * ic + 128 * hh + 128],
                                qslice(ic, tc),
                                start=(ic == 0), stop=(ic == NCH - 1))
                        s_pe.inc(last)
                    mm_done = s_pe.v

                    @blk.scalar
                    def _(sc, hh=hh, bank=bank, mm_done=mm_done):
                        sc.wait_ge(s_pe.h, mm_done)
                        ins = sc.copy(kqst[hh][:], bank[:])
                        s_act.inc(ins)
                    bank_consumed(bname, s_act)
                kev_act = s_act.v

                @blk.vector
                def _(v, tc=tc, kev_act=kev_act):
                    v.wait_ge(s_act.h, kev_act)
                    v.tensor_tensor(tscr[:], kqst[0][:], cosb[:], ALU.mult)
                    v.tensor_tensor(tscr2[:], kqst[1][:], sinb[:], ALU.mult)
                    v.tensor_tensor(ropeo[0][:], tscr[:], tscr2[:], ALU.subtract)
                    v.tensor_tensor(tscr[:], kqst[1][:], cosb[:], ALU.mult)
                    v.tensor_tensor(tscr2[:], kqst[0][:], sinb[:], ALU.mult)
                    v.tensor_tensor(ropeo[1][:], tscr[:], tscr2[:], ALU.add)
                    for hh in range(2):
                        v.tensor_copy(k_bf[hh][:, 512 * tc:512 * (tc + 1)],
                                      ropeo[hh][:])
                    v.tensor_scalar(eluscr[0][:], ropeo[0][:], 0.0, None, ALU.min)
                    v.tensor_scalar(eluscr[1][:], ropeo[1][:], 0.0, None, ALU.min)
                    v.tensor_scalar(tscr[:], ropeo[0][:], 0.0, None, ALU.max)
                    ins = v.tensor_scalar(tscr2[:], ropeo[1][:], 0.0, None,
                                          ALU.max)
                    s_dve.inc(ins)
                rk_dve = s_dve.v

                @blk.scalar
                def _(sc, rk_dve=rk_dve):
                    sc.wait_ge(s_dve.h, rk_dve)
                    sc.activation(eluscr[0][:], eluscr[0][:], AF.Exp)
                    ins = sc.activation(eluscr[1][:], eluscr[1][:], AF.Exp)
                    s_act.inc(ins)
                ek_act = s_act.v

                @blk.vector
                def _(v, tc=tc, ek_act=ek_act):
                    v.wait_ge(s_act.h, ek_act)
                    v.tensor_tensor(kf_bf[0][:, 512 * tc:512 * (tc + 1)],
                                    eluscr[0][:], tscr[:], ALU.add)
                    ins = v.tensor_tensor(kf_bf[1][:, 512 * tc:512 * (tc + 1)],
                                          eluscr[1][:], tscr2[:], ALU.add)
                    s_dve.inc(ins)
                kbf_ready[tc] = s_dve.v

                if tc < 2:
                    for hh in range(2):
                        bank, bname = [(pA, "pA"), (pB, "pB")][nproj % 2]
                        nproj += 1

                        @blk.tensor
                        def _(t, hh=hh, tc=tc, bank=bank, bname=bname):
                            bank_wait(t, bname)
                            last = None
                            for ic in range(NCH):
                                last = t.matmul(
                                    bank[:],
                                    WTq[:, 256 * ic + 128 * hh:256 * ic + 128 * hh + 128],
                                    qslice(ic, tc),
                                    start=(ic == 0), stop=(ic == NCH - 1))
                            s_pe.inc(last)
                        mm_done = s_pe.v

                        @blk.scalar
                        def _(sc, hh=hh, bank=bank, mm_done=mm_done):
                            sc.wait_ge(s_pe.h, mm_done)
                            ins = sc.copy(kqst[hh][:], bank[:])
                            s_act.inc(ins)
                        bank_consumed(bname, s_act)
                    qev_act = s_act.v

                    @blk.vector
                    def _(v, tc=tc, qev_act=qev_act):
                        v.wait_ge(s_act.h, qev_act)
                        v.tensor_tensor(tscr[:], kqst[0][:], cosb[:], ALU.mult)
                        v.tensor_tensor(tscr2[:], kqst[1][:], sinb[:], ALU.mult)
                        v.tensor_tensor(ropeo[0][:], tscr[:], tscr2[:],
                                        ALU.subtract)
                        v.tensor_tensor(tscr[:], kqst[1][:], cosb[:], ALU.mult)
                        v.tensor_tensor(tscr2[:], kqst[0][:], sinb[:], ALU.mult)
                        v.tensor_tensor(ropeo[1][:], tscr[:], tscr2[:], ALU.add)
                        for hh in range(2):
                            v.tensor_copy(q_bf[hh][:, 512 * tc:512 * (tc + 1)],
                                          ropeo[hh][:])
                        v.tensor_scalar(eluscr[0][:], ropeo[0][:], 0.0, None,
                                        ALU.min)
                        v.tensor_scalar(eluscr[1][:], ropeo[1][:], 0.0, None,
                                        ALU.min)
                        v.tensor_scalar(tscr[:], ropeo[0][:], 0.0, None, ALU.max)
                        ins = v.tensor_scalar(tscr2[:], ropeo[1][:], 0.0, None,
                                              ALU.max)
                        s_dve.inc(ins)
                    rq_dve = s_dve.v

                    @blk.scalar
                    def _(sc, rq_dve=rq_dve):
                        sc.wait_ge(s_dve.h, rq_dve)
                        sc.activation(eluscr[0][:], eluscr[0][:], AF.Exp)
                        ins = sc.activation(eluscr[1][:], eluscr[1][:], AF.Exp)
                        s_act.inc(ins)
                    eq_act = s_act.v

                    @blk.vector
                    def _(v, tc=tc, eq_act=eq_act):
                        v.wait_ge(s_act.h, eq_act)
                        v.tensor_tensor(qf_bf[0][:, 512 * tc:512 * (tc + 1)],
                                        eluscr[0][:], tscr[:], ALU.add)
                        ins = v.tensor_tensor(qf_bf[1][:, 512 * tc:512 * (tc + 1)],
                                              eluscr[1][:], tscr2[:], ALU.add)
                        s_dve.inc(ins)
                    qbf_ready[tc] = s_dve.v

            for hh in range(2):
                hglob = p + 8 * hh

                for g8 in range(2):
                    bank, bname = tbank()
                    ntr += 1

                    @blk.tensor
                    def _(t, hh=hh, g8=g8, bank=bank, bname=bname):
                        t.wait_ge(s_dve.h, kbf_ready[3])
                        bank_wait(t, bname)
                        last = None
                        for j in range(8):
                            sc_i = 8 * g8 + j
                            last = t.transpose(
                                bank[:, 128 * j:128 * (j + 1)],
                                kf_bf[hh][:, 128 * sc_i:128 * (sc_i + 1)],
                                identb[:])
                        s_pe.inc(last)
                    tr_done = s_pe.v

                    @blk.scalar
                    def _(sc, hh=hh, g8=g8, bank=bank, tr_done=tr_done):
                        sc.wait_ge(s_pe.h, tr_done)
                        ins = sc.copy(
                            kftok[hh][:, 1024 * g8:1024 * (g8 + 1)], bank[:])
                        s_act.inc(ins)
                    bank_consumed(bname, s_act)
                kftok_act = s_act.v

                @blk.tensor
                def _(t, hh=hh, kftok_act=kftok_act):
                    t.wait_ge(s_act.h, kftok_act)
                    bank_wait(t, "pC")
                    last = None
                    for sc_i in range(NCH):
                        last = t.matmul(
                            pC[:, 0:129],
                            kftok[hh][:, 128 * sc_i:128 * (sc_i + 1)],
                            vab[hh][:, 132 * sc_i:132 * sc_i + 129],
                            start=(sc_i == 0), stop=(sc_i == NCH - 1))
                    s_pe.inc(last)
                mem_done = s_pe.v

                @blk.scalar
                def _(sc, hh=hh, mem_done=mem_done):
                    sc.wait_ge(s_pe.h, mem_done)
                    ins = sc.copy(mem_aug[hh][:, 0:129], pC[:, 0:129])
                    s_act.inc(ins)
                bank_consumed("pC", s_act)
                memaug_act = s_act.v

                prev_nrm_pe = 0
                prev_red_dve = 0
                for i6 in range(6):
                    is_q = i6 >= 4
                    tci = i6 - 4 if is_q else i6
                    src = q_bf[hh] if is_q else k_bf[hh]

                    @blk.scalar
                    def _(sc, src=src, tci=tci, i6=i6, prev_nrm_pe=prev_nrm_pe):
                        if i6 > 0:
                            sc.wait_ge(s_pe.h, prev_nrm_pe)
                        ins = sc.activation(sq_bf[:],
                                            src[:, 512 * tci:512 * (tci + 1)],
                                            AF.Square)
                        s_act.inc(ins)
                    sq_act = s_act.v

                    @blk.tensor
                    def _(t, sq_act=sq_act, i6=i6, prev_red_dve=prev_red_dve):
                        t.wait_ge(s_act.h, sq_act)
                        if i6 == 0:
                            bank_wait(t, "pE")
                        else:
                            t.wait_ge(s_dve.h, prev_red_dve)
                        ins = t.matmul(pE[0:1, 0:512], ones_col_b[:], sq_bf[:],
                                       start=True, stop=True)
                        s_pe.inc(ins)
                    nrm_done = s_pe.v
                    prev_nrm_pe = nrm_done

                    @blk.vector
                    def _(v, i6=i6, nrm_done=nrm_done):
                        v.wait_ge(s_pe.h, nrm_done)
                        ins = v.reduce_max(mscr[0:1, i6:i6 + 1], pE[0:1, 0:512],
                                           axis=AX)
                        s_dve.inc(ins)
                    prev_red_dve = s_dve.v
                bank_consumed("pE", s_dve)

                @blk.vector
                def _(v):
                    v.wait_ge(s_dve.h, s_dve.v)
                    vedge(v, v.reduce_max(mscr[0:1, 6:7], mscr[0:1, 0:4],
                                          axis=AX))
                    vedge(v, v.reduce_max(mscr[0:1, 7:8], mscr[0:1, 4:6],
                                          axis=AX))
                    ins = v.tensor_tensor(mscr[0:1, 6:7], mscr[0:1, 6:7],
                                          mscr[0:1, 7:8], ALU.mult)
                    s_dve.inc(ins)
                bnd_dve = s_dve.v

                @blk.scalar
                def _(sc, bnd_dve=bnd_dve):
                    sc.wait_ge(s_dve.h, bnd_dve)
                    ins = sc.activation(mscr[0:1, 6:7], mscr[0:1, 6:7], AF.Sqrt,
                                        scale=float(1.0 / 128.0))
                    s_act.inc(ins)
                bnd_act = s_act.v

                @blk.vector
                def _(v, bnd_act=bnd_act, mbf_free_pe=mbf_free_pe):
                    v.wait_ge(s_act.h, bnd_act)
                    if mbf_free_pe:
                        v.wait_ge(s_pe.h, mbf_free_pe)
                    ins = v.tensor_scalar(mbf[0:1, 0:1], mscr[0:1, 6:7],
                                          -1.02, -0.3, ALU.mult, ALU.add)
                    s_dve.inc(ins)
                mbf_dve = s_dve.v

                @blk.tensor
                def _(t, mbf_dve=mbf_dve):
                    t.wait_ge(s_dve.h, mbf_dve)
                    bank_wait(t, "pE")
                    ins = t.matmul(pE[:, 0:1], ones_row_b[:], mbf[0:1, 0:1],
                                   start=True, stop=True)
                    s_pe.inc(ins)
                mb_done = s_pe.v
                mbf_free_pe = mb_done

                @blk.scalar
                def _(sc, mb_done=mb_done):
                    sc.wait_ge(s_pe.h, mb_done)
                    ins = sc.copy(negM[:], pE[:, 0:1])
                    s_act.inc(ins)
                bank_consumed("pE", s_act)
                negM_act = s_act.v

                pe0 = s_pe.v
                act0 = s_act.v
                dve0 = s_dve.v
                for qt in range(8):
                    bankPV, bnPV = [(pD, "pD"), (pE, "pE")][qt % 2]
                    bankMO, bnMO = [(pF, "pF"), (pC, "pC")][qt % 2]

                    @blk.tensor
                    def _(t, hh=hh, qt=qt, bankPV=bankPV, bnPV=bnPV,
                          bankMO=bankMO, bnMO=bnMO, negM_act=negM_act,
                          memaug_act=memaug_act, pe0=pe0, act0=act0,
                          dve0=dve0):
                        for sc_i in range(NCH):
                            idx = 16 * qt + sc_i
                            bank = [pA, pB][sc_i % 2]
                            if idx == 0:
                                t.wait_ge(s_act.h, negM_act)
                                t.wait_ge(s_dve.h, max(kbf_ready[3],
                                                       qbf_ready[1]))
                                bank_wait(t, "pA")
                            elif idx == 1:
                                bank_wait(t, "pB")
                            else:
                                t.wait_ge(s_act.h, act0 + idx - 1)
                            ins = t.matmul(
                                bank[:, 0:128],
                                k_bf[hh][:, 128 * sc_i:128 * (sc_i + 1)],
                                q_bf[hh][:, 128 * qt:128 * (qt + 1)],
                                start=True, stop=True)
                            s_pe.inc(ins)
                        t.wait_ge(s_act.h, act0 + 16 * (qt + 1))
                        if qt < 2:
                            bank_wait(t, bnPV)
                        else:
                            t.wait_ge(s_dve.h, dve0 + qt - 1)
                        last = None
                        for sc_i in range(NCH):
                            last = t.matmul(
                                bankPV[:, 0:129],
                                ptil[:, 128 * sc_i:128 * (sc_i + 1)],
                                vab[hh][:, 132 * sc_i:132 * sc_i + 129],
                                start=(sc_i == 0), stop=(sc_i == NCH - 1))
                        s_pe.inc(last)
                        if qt < 2:
                            bank_wait(t, bnMO)
                        if qt == 0:
                            t.wait_ge(s_act.h, memaug_act)
                        ins = t.matmul(bankMO[:, 0:129],
                                       qf_bf[hh][:, 128 * qt:128 * (qt + 1)],
                                       mem_aug[hh][:, 0:129],
                                       start=True, stop=True)
                        s_pe.inc(ins)

                @blk.scalar
                def _(sc, hh=hh, pe0=pe0):
                    for qt in range(8):
                        for sc_i in range(NCH):
                            sc.wait_ge(s_pe.h, pe0 + 18 * qt + sc_i + 1)
                            bank = [pA, pB][sc_i % 2]
                            ins = sc.activation(
                                ptil[:, 128 * sc_i:128 * (sc_i + 1)],
                                bank[:, 0:128], AF.Exp, bias=negM[:],
                                scale=float(1.0 / np.sqrt(128.0)))
                            s_act.inc(ins)
                assert s_act.v == act0 + 128

                for qt in range(8):
                    bankPV = [pD, pE][qt % 2]
                    bankMO = [pF, pC][qt % 2]

                    @blk.vector
                    def _(v, hh=hh, hglob=hglob, qt=qt, bankPV=bankPV,
                          bankMO=bankMO, pe0=pe0):
                        v.wait_ge(s_pe.h, pe0 + 18 * qt + 18)
                        vedge(v, v.reciprocal(rcls[:, 0:1],
                                              bankPV[:, 128:129]))
                        vedge(v, v.reciprocal(rcls[:, 1:2],
                                              bankMO[:, 128:129]))
                        vedge(v, v.tensor_tensor(rcls[:, 2:3],
                                                 rcls[:, 0:1],
                                                 g1m_bc[:, hglob:hglob + 1],
                                                 ALU.mult))
                        vedge(v, v.tensor_tensor(rcls[:, 3:4],
                                                 rcls[:, 1:2],
                                                 g_bc[:, hglob:hglob + 1],
                                                 ALU.mult))
                        vedge(v, v.tensor_scalar(cscr[:], bankPV[:, 0:128],
                                                 rcls[:, 2:3], None,
                                                 ALU.mult))
                        ins = v.scalar_tensor_tensor(
                            combined[:, 2048 * qt + 128 * hglob:
                                     2048 * qt + 128 * hglob + 128],
                            bankMO[:, 0:128], rcls[:, 3:4], cscr[:],
                            ALU.mult, ALU.add)
                        s_dve.inc(ins)
                assert s_dve.v == dve0 + 8
                for bn in ("pA", "pB"):
                    bank_consumed(bn, s_act)
                for bn in ("pD", "pE", "pF", "pC"):
                    bank_consumed(bn, s_dve)
                ptil_free_pe = s_pe.v
        nc.leave_named_scope(f"P{p}", _sid[f"P{p}"], False)

    if debug:
        with nc.Block("dbgB") as blk:
            @blk.sync
            def _(sy):
                sy.wait_ge(s_dve.h, s_dve.v)
                sy.wait_ge(s_act.h, s_act.v)
                for nm, t in (("d_kbf0", k_bf[0]), ("d_kbf1", k_bf[1]),
                              ("d_kfbf0", kf_bf[0]), ("d_qbf0", q_bf[0]),
                              ("d_qfbf0", qf_bf[0]), ("d_vab0", vab[0]),
                              ("d_memaug0", mem_aug[0]),
                              ("d_memaug1", mem_aug[1]),
                              ("d_negM", negM), ("d_mscr", mscr),
                              ("d_ptil", ptil), ("d_combined", combined)):
                    sy.dma_start(out=dbg[nm], in_=t[:]).then_inc(s_dbg.h, 16)
                    s_dbg.v += 16
    dbg_pair_sdbg = s_dbg.v if debug else 0

    esB.close()
    esX.close()

    esFin = ExitStack()
    ySB = esFin.enter_context(
        nc.sbuf_tensor("ySB", [128, NCH * 1024], F32, side="right"))
    esWo = ExitStack()
    combT = esWo.enter_context(
        nc.sbuf_tensor("combT", [128, NCH * 1024], BF16, side="right"))
    woslab = [esWo.enter_context(nc.sbuf_tensor(f"woslab{i}", [128, 2048], BF16))
              for i in range(2)]
    WoT = esWo.enter_context(nc.sbuf_tensor("WoT", [128, NCH * 128], BF16))

    _sid["WO"] = nc.enter_named_scope("WO", False)[0]
    with nc.Block("wo") as blk:
        comb_dve = s_dve.v
        for qt in range(8):
            for g8 in range(2):
                bank, bname = tbank()
                ntr += 1

                @blk.tensor
                def _(t, qt=qt, g8=g8, bank=bank, bname=bname,
                      comb_dve=comb_dve):
                    t.wait_ge(s_dve.h, comb_dve)
                    bank_wait(t, bname)
                    last = None
                    for j in range(8):
                        ic = 8 * g8 + j
                        last = t.transpose(
                            bank[:, 128 * j:128 * (j + 1)],
                            combined[:, 2048 * qt + 128 * ic:
                                      2048 * qt + 128 * (ic + 1)],
                            identb[:])
                    s_pe.inc(last)
                tr_done = s_pe.v

                @blk.scalar
                def _(sc, qt=qt, g8=g8, bank=bank, tr_done=tr_done):
                    sc.wait_ge(s_pe.h, tr_done)
                    dst3 = combT[:].rearrange("p (c f) -> p c f", f=1024)
                    ins = sc.copy(dst3[:, 8 * g8:8 * g8 + 8, 128 * qt:128 * (qt + 1)],
                                  bank[:].rearrange("p (c f) -> p c f", f=128))
                    s_act.inc(ins)
                bank_consumed(bname, s_act)
        combT_act = s_act.v

        woslab_free = [(s_pe, 0), (s_pe, 0)]
        for jc in range(NCH):
            slot = jc % 2

            @blk.gpsimd
            def _(g, jc=jc, slot=slot):
                if debug and jc == 0:
                    g.wait_ge(s_dbg.h, dbg_pair_sdbg)
                ctr, val = woslab_free[slot]
                g.wait_ge(ctr.h, val)
                ins = g.dma_start(out=woslab[slot][:],
                                  in_=wo_d[128 * jc:128 * (jc + 1), :])
                s_wl.inc(ins, 16)
            ld_done = s_wl.v

            for g8 in range(2):
                bank, bname = tbank()
                ntr += 1

                @blk.tensor
                def _(t, slot=slot, g8=g8, bank=bank, bname=bname,
                      ld_done=ld_done):
                    t.wait_ge(s_wl.h, ld_done)
                    bank_wait(t, bname)
                    last = None
                    for j in range(8):
                        ic = 8 * g8 + j
                        last = t.transpose(bank[:, 128 * j:128 * (j + 1)],
                                           woslab[slot][:, 128 * ic:128 * (ic + 1)],
                                           identb[:])
                    s_pe.inc(last)
                tr_done = s_pe.v

                @blk.scalar
                def _(sc, g8=g8, bank=bank, tr_done=tr_done):
                    sc.wait_ge(s_pe.h, tr_done)
                    ins = sc.copy(WoT[:, 1024 * g8:1024 * (g8 + 1)], bank[:])
                    s_act.inc(ins)
                bank_consumed(bname, s_act)
            wot_act = s_act.v
            woslab_free[slot] = (s_pe, s_pe.v)

            for tch in range(2):
                bank, bname = [(pA, "pA"), (pB, "pB")][tch]

                @blk.tensor
                def _(t, jc=jc, tch=tch, bank=bank, bname=bname,
                      wot_act=wot_act, combT_act=combT_act):
                    t.wait_ge(s_act.h, max(wot_act, combT_act))
                    bank_wait(t, bname)
                    last = None
                    for ic in range(NCH):
                        last = t.matmul(
                            bank[:],
                            WoT[:, 128 * ic:128 * (ic + 1)],
                            combT[:, 1024 * ic + 512 * tch:
                                  1024 * ic + 512 * tch + 512],
                            start=(ic == 0), stop=(ic == NCH - 1))
                    s_pe.inc(last)
                mm_done = s_pe.v

                @blk.scalar
                def _(sc, jc=jc, tch=tch, bank=bank, mm_done=mm_done):
                    sc.wait_ge(s_pe.h, mm_done)
                    ins = sc.copy(ySB[:, 1024 * jc + 512 * tch:
                                      1024 * jc + 512 * tch + 512], bank[:])
                    s_act.inc(ins)
                bank_consumed(bname, s_act)
    nc.leave_named_scope("WO", _sid["WO"], False)
    ySB_wo_act = s_act.v
    if debug:
        with nc.Block("dbgWo") as blk:
            @blk.sync
            def _(sy):
                sy.wait_ge(s_act.h, s_act.v)
                sy.dma_start(out=dbg["d_ySBwo"], in_=ySB[:]).then_inc(s_dbg.h, 16)
                s_dbg.v += 16
    dbg_wo_sdbg = s_dbg.v if debug else 0

    esWo.close()
    esCm.close()
    esW.close()

    esC = ExitStack()
    wbuf = [esC.enter_context(nc.sbuf_tensor(f"wbuf{i}", [128, NCH * 768],
                                             BF16)) for i in range(2)]
    h1sb = esC.enter_context(nc.sbuf_tensor("h1sb", [128, 12 * 512], BF16))
    h2sc = [esC.enter_context(nc.sbuf_tensor(f"h2sc{i}", [128, 512], BF16))
            for i in range(2)]
    cSB = esC.enter_context(nc.sbuf_tensor("cSB", [128, 1024], BF16))
    NIH = 6

    wlist = []
    for ih in range(2):
        wlist.append((sg_d[:, 768 * ih:768 * (ih + 1)],
                      su_d[:, 768 * ih:768 * (ih + 1)],
                      sd_d[768 * ih:768 * (ih + 1), :], False, -1))
    for e in range(E):
        for ih in range(2):
            wlist.append((wg_d[e][:, 768 * ih:768 * (ih + 1)],
                          wu_d[e][:, 768 * ih:768 * (ih + 1)],
                          wd_d[e][768 * ih:768 * (ih + 1), :], True, e))

    def emit_wload(blk, src_ap, buf, kind, wait_pe):
        @blk.gpsimd
        def _(g, src_ap=src_ap, buf=buf, kind=kind, wait_pe=wait_pe):
            if wait_pe:
                g.wait_ge(s_pe.h, wait_pe)
            if kind == "gu":
                for grp in range(2):
                    ins = g.dma_start(
                        out=buf[:, 768 * 8 * grp:768 * 8 * (grp + 1)],
                        in_=src_ap[1024 * grp:1024 * (grp + 1), :].rearrange(
                            "(c p) f -> p c f", p=128))
                    s_wl.inc(ins, 16)
            else:
                ins = g.dma_start(
                    out=buf[:, 0:6 * 2048],
                    in_=src_ap.rearrange("(c p) f -> p c f", p=128))
                s_wl.inc(ins, 16)
        return s_wl.v

    dbg_h_sdbg = 0
    with nc.Block("ffnpre") as blk:
        ld_done = [None] * (3 * len(wlist))
        ld_done[0] = emit_wload(blk, wlist[0][0], wbuf[0], "gu", 0)
        ld_done[1] = emit_wload(blk, wlist[0][1], wbuf[1], "gu", 0)

    for ei in range(len(wlist)):
        srcg, srcu, srcd, routed, eidx = wlist[ei]
        gb = wbuf[ei % 2]
        ub = wbuf[(ei + 1) % 2]
        db = wbuf[ei % 2]

        _sid[f"F{ei}"] = nc.enter_named_scope(f"F{ei}", False)[0]
        with nc.Block(f"ffn{ei}") as blk:
            gate_start_wl = ld_done[3 * ei]
            peG0 = s_pe.v
            actG0 = s_act.v
            for tch in range(2):
                for nIc in range(NIH):
                    bank, bn = [(pA, "pA"), (pB, "pB")][nIc % 2]
                    idx = NIH * tch + nIc

                    @blk.tensor
                    def _(t, tch=tch, nIc=nIc, bank=bank, bn=bn, idx=idx,
                          gate_start_wl=gate_start_wl, gb=gb, actG0=actG0):
                        if idx == 0:
                            t.wait_ge(s_wl.h, gate_start_wl)
                            bank_wait(t, "pA")
                        elif idx == 1:
                            bank_wait(t, "pB")
                        else:
                            t.wait_ge(s_act.h, actG0 + idx - 1)
                        last = None
                        for hc in range(NCH):
                            last = t.matmul(
                                bank[:],
                                gb[:, 768 * hc + 128 * nIc:
                                   768 * hc + 128 * nIc + 128],
                                xnT0[:, 1024 * hc + 512 * tch:
                                     1024 * hc + 512 * tch + 512],
                                start=(hc == 0), stop=(hc == NCH - 1))
                        s_pe.inc(last)

            @blk.scalar
            def _(sc, ei=ei, peG0=peG0, dbg_h=dbg_h_sdbg if debug else 0):
                for tch in range(2):
                    for nIc in range(NIH):
                        idx = NIH * tch + nIc
                        sc.wait_ge(s_pe.h, peG0 + idx + 1)
                        if debug and ei == 1 and idx == 0:
                            sc.wait_ge(s_dbg.h, dbg_h)
                        bank = [pA, pB][nIc % 2]
                        ins = sc.activation(
                            h1sb[:, 512 * idx:512 * idx + 512],
                            bank[:], AF.Silu)
                        s_act.inc(ins)
            assert s_act.v == actG0 + 12
            for bn in ("pA", "pB"):
                bank_consumed(bn, s_act)
            gate_pe_done = s_pe.v
            ld_done[3 * ei + 2] = emit_wload(blk, srcd, db, "d", gate_pe_done)

            up_start_wl = ld_done[3 * ei + 1]
            if routed:
                @blk.vector
                def _(v, eidx=eidx):
                    ins = v.tensor_scalar(sel_sb[:], iota16[:], float(eidx),
                                          None, ALU.is_equal)
                    s_dve.inc(ins)
                sel_dve = s_dve.v
            else:
                sel_dve = 0

            for tch in range(2):
                dveT0 = s_dve.v
                roff = 1 if routed else 0

                @blk.tensor
                def _(t, tch=tch, dveT0=dveT0, roff=roff, routed=routed,
                      sel_dve=sel_dve, up_start_wl=up_start_wl, ub=ub):
                    if routed:
                        if tch == 0:
                            t.wait_ge(s_dve.h, sel_dve)
                            bank_wait(t, "pC")
                        else:
                            t.wait_ge(s_dve.h, dveT0 - roff - NIH + 1)
                        ins = t.matmul(pC[:], sel_sb[:],
                                       combF[0:16, 512 * tch:512 * (tch + 1)],
                                       start=True, stop=True)
                        s_pe.inc(ins)
                    for nIc in range(NIH):
                        bank = [pD, pE][nIc % 2]
                        if tch == 0 and nIc == 0:
                            t.wait_ge(s_wl.h, up_start_wl)
                        if tch == 0 and nIc < 2:
                            bank_wait(t, ["pD", "pE"][nIc])
                        elif nIc < 2:
                            t.wait_ge(s_dve.h, dveT0 - roff - NIH +
                                      (roff + nIc - 2 + 1) + NIH)
                        else:
                            t.wait_ge(s_dve.h, dveT0 + roff + nIc - 1)
                        last = None
                        for hc in range(NCH):
                            last = t.matmul(
                                bank[:],
                                ub[:, 768 * hc + 128 * nIc:
                                   768 * hc + 128 * nIc + 128],
                                xnT0[:, 1024 * hc + 512 * tch:
                                     1024 * hc + 512 * tch + 512],
                                start=(hc == 0), stop=(hc == NCH - 1))
                        s_pe.inc(last)
                peU = s_pe.v

                @blk.vector
                def _(v, tch=tch, routed=routed, peU=peU, roff=roff):
                    nmm = NIH + (1 if routed else 0)
                    base_pe = peU - nmm
                    if routed:
                        v.wait_ge(s_pe.h, base_pe + 1)
                        ins = v.tensor_copy(cSB[:, 512 * tch:512 * (tch + 1)],
                                            pC[:])
                        s_dve.inc(ins)
                    for nIc in range(NIH):
                        bank = [pD, pE][nIc % 2]
                        v.wait_ge(s_pe.h, base_pe + roff + nIc + 1)
                        hsl = h1sb[:, 512 * (NIH * tch + nIc):
                                   512 * (NIH * tch + nIc) + 512]
                        if routed:
                            v.tensor_tensor(hsl, hsl, bank[:], ALU.mult)
                            ins = v.tensor_tensor(
                                hsl, hsl, cSB[:, 512 * tch:512 * (tch + 1)],
                                ALU.mult)
                        else:
                            ins = v.tensor_tensor(hsl, hsl, bank[:], ALU.mult)
                        s_dve.inc(ins)
                assert s_dve.v == dveT0 + roff + NIH
            for bn in ("pD", "pE", "pC"):
                bank_consumed(bn, s_dve)
            up_pe_done = s_pe.v
            h_dve_done = s_dve.v
            if ei == 0 and debug:
                @blk.sync
                def _(sy, h_dve_done=h_dve_done):
                    sy.wait_ge(s_dve.h, h_dve_done)
                    sy.dma_start(out=dbg["d_h1sb"],
                                 in_=h1sb[:]).then_inc(s_dbg.h, 16)
                    s_dbg.v += 16
            dbg_h_sdbg = s_dbg.v if debug else 0
            if ei + 1 < len(wlist):
                ld_done[3 * (ei + 1)] = emit_wload(blk, wlist[ei + 1][0], ub,
                                                   "gu", up_pe_done)

            down_start_wl = ld_done[3 * ei + 2]
            for tch in range(2):
                for jc in range(NCH):
                    bank, bn = [(pF, "pF"), (pA, "pA")][jc % 2]

                    @blk.tensor
                    def _(t, tch=tch, jc=jc, bank=bank, bn=bn,
                          down_start_wl=down_start_wl, h_dve_done=h_dve_done,
                          db=db):
                        if tch == 0 and jc == 0:
                            t.wait_ge(s_wl.h, down_start_wl)
                            t.wait_ge(s_dve.h, h_dve_done)
                        bank_wait(t, bn)
                        last = None
                        for nIc in range(NIH):
                            last = t.matmul(
                                bank[:],
                                db[:, 2048 * nIc + 128 * jc:
                                   2048 * nIc + 128 * jc + 128],
                                h1sb[:, 512 * (NIH * tch + nIc):
                                     512 * (NIH * tch + nIc) + 512],
                                start=(nIc == 0), stop=(nIc == NIH - 1))
                        s_pe.inc(last)
                    dn_done = s_pe.v

                    @blk.vector
                    def _(v, tch=tch, jc=jc, bank=bank, dn_done=dn_done,
                          ei=ei):
                        v.wait_ge(s_pe.h, dn_done)
                        if ei == 0 and tch == 0 and jc == 0:
                            v.wait_ge(s_act.h, ySB_wo_act)
                            if debug:
                                v.wait_ge(s_dbg.h, dbg_wo_sdbg)
                        ysl = ySB[:, 1024 * jc + 512 * tch:
                                  1024 * jc + 512 * tch + 512]
                        ins = v.tensor_tensor(ysl, ysl, bank[:], ALU.add)
                        s_dve.inc(ins)
                    bank_consumed(bn, s_dve)
            down_pe_done = s_pe.v
            if ei + 1 < len(wlist):
                ld_done[3 * (ei + 1) + 1] = emit_wload(
                    blk, wlist[ei + 1][1], db, "gu", down_pe_done)
        nc.leave_named_scope(f"F{ei}", _sid[f"F{ei}"], False)
    esC.close()

    esD = ExitStack()
    xfin = [esD.enter_context(nc.sbuf_tensor(f"xfin{i}", [128, H], F32))
            for i in range(2)]
    ofin = [esD.enter_context(nc.sbuf_tensor(f"ofin{i}", [128, H], F32))
            for i in range(2)]
    xfin_free = [(s_dve, 0), (s_dve, 0)]
    ofin_free = [(s_st, 0), (s_st, 0)]
    ySB_full_dve = s_dve.v

    _sid["FIN"] = nc.enter_named_scope("FIN", False)[0]
    with nc.Block("fin") as blk:
        for qt in range(8):
            pp = qt % 2

            @blk.sync
            def _(sy, qt=qt, pp=pp):
                ctr, val = xfin_free[pp]
                sy.wait_ge(ctr.h, val)
                ins = sy.dma_start(out=xfin[pp][:],
                                   in_=x_d[128 * qt:128 * (qt + 1), :])
                s_ldx.inc(ins, 16)
            x_ready = s_ldx.v

            for g4 in range(4):
                bank, bname = [(pD, "pD"), (pE, "pE")][g4 % 2]

                @blk.tensor
                def _(t, qt=qt, g4=g4, bank=bank, bname=bname,
                      ySB_full_dve=ySB_full_dve):
                    if qt == 0 and g4 == 0:
                        t.wait_ge(s_dve.h, ySB_full_dve)
                    bank_wait(t, bname)
                    last = None
                    for j in range(4):
                        jc = 4 * g4 + j
                        last = t.transpose(
                            bank[:, 128 * j:128 * (j + 1)],
                            ySB[:, 1024 * jc + 128 * qt:
                                1024 * jc + 128 * (qt + 1)],
                            identf[:])
                    s_pe.inc(last)
                tr_done = s_pe.v

                @blk.vector
                def _(v, qt=qt, pp=pp, g4=g4, bank=bank, tr_done=tr_done,
                      x_ready=x_ready):
                    v.wait_ge(s_pe.h, tr_done)
                    v.wait_ge(s_ldx.h, x_ready)
                    if g4 == 0:
                        ctr, val = ofin_free[pp]
                        v.wait_ge(ctr.h, val)
                    ins = v.tensor_tensor(ofin[pp][:, 512 * g4:512 * (g4 + 1)],
                                          xfin[pp][:, 512 * g4:512 * (g4 + 1)],
                                          bank[:], ALU.add)
                    s_dve.inc(ins)
                bank_consumed(bname, s_dve)
            add_done = s_dve.v
            xfin_free[pp] = (s_dve, s_dve.v)

            @blk.sync
            def _(sy, qt=qt, pp=pp, add_done=add_done):
                sy.wait_ge(s_dve.h, add_done)
                ins = sy.dma_start(out=out_d[128 * qt:128 * (qt + 1), :],
                                   in_=ofin[pp][:])
                s_st.inc(ins, 16)
            ofin_free[pp] = (s_st, s_st.v)

        @blk.sync
        def _(sy):
            sy.wait_ge(s_st.h, s_st.v)
            if debug:
                sy.wait_ge(s_dbg.h, s_dbg.v)

    nc.leave_named_scope("FIN", _sid["FIN"], False)
    esD.close()
    esFin.close()
    es.close()
    return nc


_NC_CACHE = None


def kernel(L0_attn_hidden_states, L0_attn_Wq, L0_attn_Wk, L0_attn_Wv,
           L0_attn_Wo, L0_attn_gate, L0_ffn_gate_w, L0_ffn_Wg, L0_ffn_Wu,
           L0_ffn_Wd, L0_ffn_Sg, L0_ffn_Su, L0_ffn_Sd):
    global _NC_CACHE
    x = np.ascontiguousarray(np.asarray(L0_attn_hidden_states, dtype=np.float32))
    wq = np.ascontiguousarray(np.asarray(L0_attn_Wq, dtype=np.float32))
    wk = np.ascontiguousarray(np.asarray(L0_attn_Wk, dtype=np.float32))
    wv = np.ascontiguousarray(np.asarray(L0_attn_Wv, dtype=np.float32))
    wo = np.ascontiguousarray(np.asarray(L0_attn_Wo, dtype=np.float32))
    gate = np.ascontiguousarray(
        np.asarray(L0_attn_gate, dtype=np.float32).reshape(1, NH))
    gw = np.ascontiguousarray(np.asarray(L0_ffn_gate_w, dtype=np.float32))
    wg = np.ascontiguousarray(np.asarray(L0_ffn_Wg, dtype=np.float32))
    wu = np.ascontiguousarray(np.asarray(L0_ffn_Wu, dtype=np.float32))
    wd = np.ascontiguousarray(np.asarray(L0_ffn_Wd, dtype=np.float32))
    sg = np.ascontiguousarray(np.asarray(L0_ffn_Sg, dtype=np.float32))
    su = np.ascontiguousarray(np.asarray(L0_ffn_Su, dtype=np.float32))
    sd = np.ascontiguousarray(np.asarray(L0_ffn_Sd, dtype=np.float32))

    if _NC_CACHE is None:
        _NC_CACHE = build()
    nc = _NC_CACHE

    in_maps = []
    for c in range(8):
        b, half = c // 2, c % 2
        xb = x[b]
        if half:
            xb = np.ascontiguousarray(np.roll(xb, -1024, axis=0))
        tbase = np.array([[(512 * t + 1024 * half) % 2048 for t in range(4)]],
                         dtype=np.float32)
        in_maps.append(dict(x=xb, wq=wq, wk=wk, wv=wv, wo=wo, gate=gate,
                            gw=gw, wg=wg, wu=wu, wd=wd, sg=sg, su=su, sd=sd,
                            tbase=tbase))
    res = run_bass_kernel_spmd(nc, in_maps, core_ids=list(range(8)))
    out = np.empty((B, S, H), dtype=np.float32)
    for c in range(8):
        b, half = c // 2, c % 2
        out[b, 1024 * half:1024 * half + 1024] = res.results[c]["out"]
    return out


# version 05
# speedup vs baseline: 1.0679x; 1.0679x over previous
import numpy as np
from contextlib import ExitStack

import concourse.bass as bass
import concourse.mybir as mybir
from concourse.bass_utils import run_bass_kernel_spmd

F32 = mybir.dt.float32
BF16 = mybir.dt.bfloat16
AF = mybir.ActivationFunctionType
ALU = mybir.AluOpType
AX = mybir.AxisListType.X

B, S, H = 4, 2048, 2048
NH, HD = 16, 128
NCH = H // 128
TKV, TQ = 2048, 1024
E, TOPK, I = 16, 4, 1536
NI = I // 128
MAGIC = 12582912.0
PI = float(np.pi)
INV2PI = float(1.0 / (2.0 * np.pi))


class Ctr:
    __slots__ = ("h", "v")

    def __init__(self, h):
        self.h = h
        self.v = 0

    def inc(self, ins, n=1):
        ins.then_inc(self.h, n)
        self.v += n
        return self.v


def build(debug=False):
    _sid = {}
    nc = bass.Bass("TRN2", target_bir_lowering=False,
                   dynamic_dma_scratch_size=16384)

    x_d = nc.dram_tensor("x", [TKV, H], F32, kind="ExternalInput").ap()
    wq_d = nc.dram_tensor("wq", [H, H], F32, kind="ExternalInput").ap()
    wk_d = nc.dram_tensor("wk", [H, H], F32, kind="ExternalInput").ap()
    wv_d = nc.dram_tensor("wv", [H, H], F32, kind="ExternalInput").ap()
    wo_d = nc.dram_tensor("wo", [H, H], F32, kind="ExternalInput").ap()
    gate_d = nc.dram_tensor("gate", [1, NH], F32, kind="ExternalInput").ap()
    gw_d = nc.dram_tensor("gw", [H, E], F32, kind="ExternalInput").ap()
    wg_d = nc.dram_tensor("wg", [E, H, I], F32, kind="ExternalInput").ap()
    wu_d = nc.dram_tensor("wu", [E, H, I], F32, kind="ExternalInput").ap()
    wd_d = nc.dram_tensor("wd", [E, I, H], F32, kind="ExternalInput").ap()
    sg_d = nc.dram_tensor("sg", [H, I], F32, kind="ExternalInput").ap()
    su_d = nc.dram_tensor("su", [H, I], F32, kind="ExternalInput").ap()
    sd_d = nc.dram_tensor("sd", [I, H], F32, kind="ExternalInput").ap()
    tb_d = nc.dram_tensor("tbase", [1, 4], F32, kind="ExternalInput").ap()
    out_d = nc.dram_tensor("out", [TQ, H], F32, kind="ExternalOutput").ap()
    if debug:
        dbg = {
            "d_xnT0": nc.dram_tensor("d_xnT0", [128, NCH * 1024], BF16,
                                     kind="ExternalOutput").ap(),
            "d_xnT1": nc.dram_tensor("d_xnT1", [128, NCH * 1024], BF16,
                                     kind="ExternalOutput").ap(),
            "d_comb": nc.dram_tensor("d_comb", [128, 128], F32,
                                     kind="ExternalOutput").ap(),
            "d_combF": nc.dram_tensor("d_combF", [16, 1024], BF16,
                                      kind="ExternalOutput").ap(),
            "d_kbf0": nc.dram_tensor("d_kbf0", [128, 2048], BF16,
                                     kind="ExternalOutput").ap(),
            "d_kbf1": nc.dram_tensor("d_kbf1", [128, 2048], BF16,
                                     kind="ExternalOutput").ap(),
            "d_kfbf0": nc.dram_tensor("d_kfbf0", [128, 2048], BF16,
                                      kind="ExternalOutput").ap(),
            "d_qbf0": nc.dram_tensor("d_qbf0", [128, 1024], BF16,
                                     kind="ExternalOutput").ap(),
            "d_qfbf0": nc.dram_tensor("d_qfbf0", [128, 1024], BF16,
                                      kind="ExternalOutput").ap(),
            "d_vab0": nc.dram_tensor("d_vab0", [128, NCH * 132], BF16,
                                     kind="ExternalOutput").ap(),
            "d_memaug0": nc.dram_tensor("d_memaug0", [128, 132], BF16,
                                        kind="ExternalOutput").ap(),
            "d_memaug1": nc.dram_tensor("d_memaug1", [128, 132], BF16,
                                        kind="ExternalOutput").ap(),
            "d_negM": nc.dram_tensor("d_negM", [128, 1], F32,
                                     kind="ExternalOutput").ap(),
            "d_mscr": nc.dram_tensor("d_mscr", [1, 8], F32,
                                     kind="ExternalOutput").ap(),
            "d_ptil": nc.dram_tensor("d_ptil", [128, NCH * 128], BF16,
                                     kind="ExternalOutput").ap(),
            "d_combined": nc.dram_tensor("d_combined", [128, 8 * 2048], BF16,
                                         kind="ExternalOutput").ap(),
            "d_ySBwo": nc.dram_tensor("d_ySBwo", [128, NCH * 1024], F32,
                                      kind="ExternalOutput").ap(),
            "d_h1sb": nc.dram_tensor("d_h1sb", [128, 12 * 512], BF16,
                                     kind="ExternalOutput").ap(),
        }

    es = ExitStack()
    sb = lambda n, s, d, **kw: es.enter_context(nc.sbuf_tensor(n, s, d, **kw))
    psn = lambda n, s, d: es.enter_context(nc.psum_tensor(n, s, d))
    mksem = lambda n: Ctr(es.enter_context(nc.semaphore(n)))

    s_pe = mksem("s_pe")
    s_act = mksem("s_act")
    s_dve = mksem("s_dve")
    s_pool = mksem("s_pool")
    s_ldx = mksem("s_ldx")
    s_wl = mksem("s_wl")
    s_st = mksem("s_st")
    s_dbg = mksem("s_dbg")
    s_ve = mksem("s_ve")
    s_ae = mksem("s_ae")

    def vedge(v, ins):
        s_ve.inc(ins)
        v.wait_ge(s_ve.h, s_ve.v)

    def aedge(sc, ins):
        s_ae.inc(ins)
        sc.wait_ge(s_ae.h, s_ae.v)

    xnT0 = sb("xnT0", [128, NCH * 1024], BF16)
    combF = sb("combF", [16, 1024], BF16)
    identf = sb("identf", [128, 128], F32)
    iota16 = sb("iota16", [16, 128], F32)
    sel_sb = sb("sel_sb", [16, 128], BF16)

    esW = ExitStack()
    sbW = lambda n, s, d: esW.enter_context(nc.sbuf_tensor(n, s, d))
    comb_all = sbW("comb_all", [128, 8 * 16], F32)
    g_bc = sbW("g_bc", [128, NH], F32)
    g1m_bc = sbW("g1m_bc", [128, NH], F32)
    identb = sbW("identb", [128, 128], BF16)
    onesb128 = sbW("onesb128", [128, 128], BF16)
    ones_row_b = sbW("ones_row_b", [1, 128], BF16)
    ones_col_b = sbW("ones_col_b", [128, 1], BF16)
    invf_tab = sbW("invf_tab", [128, 8], F32)
    tb_bc = sbW("tb_bc", [128, 4], F32)
    cst = sbW("cst", [128, 4], F32)
    C_EPS, C_HALFPI = 0, 1
    gate_sb = sbW("gate_sb", [1, NH], F32)
    gate_bf = sbW("gate_bf", [1, NH], BF16)
    tb_sb = sbW("tb_sb", [1, 4], F32)
    tb_bf = sbW("tb_bf", [1, 4], BF16)
    negM = sbW("negM", [128, 1], F32)
    mscr = sbW("mscr", [1, 8], F32)
    mbf = sbW("mbf", [1, 1], BF16)

    pA = psn("pA", [128, 512], F32)
    pB = psn("pB", [128, 512], F32)
    pC = psn("pC", [128, 512], F32)
    pD = psn("pD", [128, 512], F32)
    pE = psn("pE", [128, 512], F32)
    pF = psn("pF", [128, 512], F32)
    pT1 = psn("pT1", [128, 1024], BF16)
    pT2 = psn("pT2", [128, 1024], BF16)

    bank_free = {}

    def bank_wait(eng, bank):
        ent = bank_free.get(bank)
        if ent is not None:
            ctr, val = ent
            eng.wait_ge(ctr.h, val)

    def bank_consumed(bank, ctr):
        bank_free[bank] = (ctr, ctr.v)

    with nc.Block("init") as blk:
        @blk.sync
        def _(sy):
            sy.dma_start(out=gate_sb[:], in_=gate_d).then_inc(s_ldx.h, 16)
            sy.dma_start(out=tb_sb[:], in_=tb_d).then_inc(s_ldx.h, 16)
            s_ldx.v += 32

        @blk.vector
        def _(v):
            v.memset(ones_row_b[:], 1.0)
            v.memset(ones_col_b[:], 1.0)
            v.memset(onesb128[:], 1.0)
            v.memset(cst[:, C_EPS:C_EPS + 1], 1e-5)
            ins = v.memset(cst[:, C_HALFPI:C_HALFPI + 1], PI / 2.0)
            s_dve.inc(ins)

        @blk.gpsimd
        def _(g):
            g.wait_ge(s_dve.h, 1)
            g.affine_select(identb[:], onesb128[:], [[1, 128]], ALU.is_equal,
                            0.0, base=0, channel_multiplier=-1)
            g.iota(invf_tab[:], [[128, 8]], base=0, channel_multiplier=1,
                   allow_small_or_imprecise_dtypes=True)
            ins = g.iota(iota16[:], [[0, 128]], base=0, channel_multiplier=1,
                         allow_small_or_imprecise_dtypes=True)
            s_pool.inc(ins)

        @blk.scalar
        def _(sc):
            sc.wait_ge(s_pool.h, 1)
            sc.copy(identf[:], identb[:])
            sc.activation(invf_tab[:], invf_tab[:], AF.Exp,
                          scale=float(-np.log(10000.0) / 1024.0))
            sc.wait_ge(s_ldx.h, 32)
            aedge(sc, sc.activation(gate_sb[:], gate_sb[:], AF.Sigmoid))
            sc.copy(gate_bf[:], gate_sb[:])
            ins = sc.copy(tb_bf[:], tb_sb[:])
            s_act.inc(ins)

        @blk.tensor
        def _(t):
            t.wait_ge(s_act.h, 1)
            t.matmul(pC[:, 0:16], ones_row_b[:], gate_bf[:], start=True, stop=True)
            ins = t.matmul(pC[:, 16:20], ones_row_b[:], tb_bf[:],
                           start=True, stop=True)
            s_pe.inc(ins)

        @blk.scalar
        def _(sc):
            sc.wait_ge(s_pe.h, 1)
            sc.copy(g_bc[:], pC[:, 0:16])
            ins = sc.copy(tb_bc[:], pC[:, 16:20])
            s_act.inc(ins)

        @blk.vector
        def _(v):
            v.wait_ge(s_act.h, 2)
            ins = v.tensor_scalar(g1m_bc[:], g_bc[:], -1.0, 1.0, ALU.mult, ALU.add)
            s_dve.inc(ins)
        bank_consumed("pC", s_act)

    esCm = ExitStack()
    combined = esCm.enter_context(
        nc.sbuf_tensor("combined", [128, 8 * 2048], BF16))
    esX = ExitStack()
    xnT1 = esX.enter_context(nc.sbuf_tensor("xnT1", [128, NCH * 1024], BF16))
    iota512 = esX.enter_context(nc.sbuf_tensor("iota512", [128, 512], F32))
    gw_sb = esX.enter_context(nc.sbuf_tensor("gw_sb", [128, NCH * E], F32))

    esA = ExitStack()
    sbA = lambda n, s, d: esA.enter_context(nc.sbuf_tensor(n, s, d))
    xt = [sbA(f"xt{i}", [128, H], F32) for i in range(2)]
    xnf = [sbA(f"xnf{i}", [128, H], F32) for i in range(2)]
    xnb = [sbA(f"xnb{i}", [128, H], BF16) for i in range(2)]
    sqscr = sbA("sqscr", [128, H], F32)
    xnTf = sbA("xnTf", [128, NCH * 128], F32)
    lncol = sbA("lncol", [128, 8], F32)
    CL_SUM, CL_NEGMEAN, CL_M2, CL_VAR, CL_RSTD = 0, 1, 2, 3, 4
    CL_SQP = [5, 6]
    Ltile = sbA("Ltile", [128, 16], F32)
    topv = sbA("topv", [128, 8], F32)
    texp = sbA("texp", [128, 4], F32)
    probs = sbA("probs", [128, 4], F32)
    pcc = sbA("pcc", [128, 2], F32)
    maskt = sbA("maskt", [128, 16], F32)
    combsc = [sbA(f"combsc{i}", [128, 16], F32) for i in range(2)]

    with nc.Block("ginit") as blk:
        @blk.sync
        def _(sy):
            ins = sy.dma_start(out=gw_sb[:],
                               in_=gw_d.rearrange("(c p) f -> p c f", p=128))
            s_ldx.inc(ins, 16)

        @blk.gpsimd
        def _(g):
            ins = g.iota(iota512[:], [[1, 512]], base=0, channel_multiplier=0,
                         allow_small_or_imprecise_dtypes=True)
            s_pool.inc(ins)

    xt_free = [(s_dve, 0), (s_dve, 0)]
    xn_free = [(s_pe, 0), (s_pe, 0)]
    gw_ready = s_ldx.v
    iota_ready = s_pool.v

    for tt in range(16):
        pp = tt % 2
        dst_xnT = xnT0 if tt < 8 else xnT1
        _sid[f"A{tt}"] = nc.enter_named_scope(f"A{tt}", False)[0]
        toff = 128 * (tt % 8)
        with nc.Block(f"ln{tt}") as blk:
            @blk.sync
            def _(sy):
                ctr, val = xt_free[pp]
                sy.wait_ge(ctr.h, val)
                ins = sy.dma_start(out=xt[pp][:],
                                   in_=x_d[128 * tt:128 * (tt + 1), :])
                s_ldx.inc(ins, 16)
            x_ready = s_ldx.v

            @blk.scalar
            def _(sc, pp=pp):
                sc.wait_ge(s_ldx.h, x_ready)
                sq = CL_SQP[pp]
                ins = sc.activation(sqscr[:], xt[pp][:], AF.Square,
                                    accum_out=lncol[:, sq:sq + 1])
                s_act.inc(ins)
            sq_ready = s_act.v

            @blk.vector
            def _(v):
                v.wait_ge(s_ldx.h, x_ready)
                vedge(v, v.reduce_sum(lncol[:, CL_SUM:CL_SUM + 1], xt[pp][:],
                                      axis=AX))
                vedge(v, v.tensor_scalar(lncol[:, CL_NEGMEAN:CL_NEGMEAN + 1],
                                         lncol[:, CL_SUM:CL_SUM + 1],
                                         float(-1.0 / H), None, ALU.mult))
                vedge(v, v.tensor_tensor(lncol[:, CL_M2:CL_M2 + 1],
                                         lncol[:, CL_NEGMEAN:CL_NEGMEAN + 1],
                                         lncol[:, CL_NEGMEAN:CL_NEGMEAN + 1],
                                         ALU.mult))
                v.wait_ge(s_act.h, sq_ready)
                sq = CL_SQP[pp]
                ins = v.scalar_tensor_tensor(lncol[:, CL_VAR:CL_VAR + 1],
                                             lncol[:, sq:sq + 1],
                                             float(1.0 / H),
                                             lncol[:, CL_M2:CL_M2 + 1],
                                             ALU.mult, ALU.subtract)
                s_dve.inc(ins)
            var_ready = s_dve.v

            @blk.scalar
            def _(sc):
                sc.wait_ge(s_dve.h, var_ready)
                aedge(sc, sc.activation(lncol[:, CL_VAR:CL_VAR + 1],
                                        lncol[:, CL_VAR:CL_VAR + 1], AF.Ln,
                                        bias=cst[:, C_EPS:C_EPS + 1]))
                ins = sc.activation(lncol[:, CL_RSTD:CL_RSTD + 1],
                                    lncol[:, CL_VAR:CL_VAR + 1], AF.Exp,
                                    scale=-0.5)
                s_act.inc(ins)
            rstd_ready = s_act.v

            @blk.vector
            def _(v):
                v.wait_ge(s_act.h, rstd_ready)
                ctr, val = xn_free[pp]
                v.wait_ge(ctr.h, val)
                v.tensor_scalar(xnf[pp][:], xt[pp][:],
                                lncol[:, CL_NEGMEAN:CL_NEGMEAN + 1],
                                lncol[:, CL_RSTD:CL_RSTD + 1],
                                ALU.add, ALU.mult)
                ins = v.tensor_copy(xnb[pp][:], xnf[pp][:])
                s_dve.inc(ins)
            xn_ready = s_dve.v
            xt_free[pp] = (s_dve, s_dve.v)

            for g4 in range(4):
                bank = [pT1, pT2][g4 % 2]
                bname = ["pT1", "pT2"][g4 % 2]

                @blk.tensor
                def _(t, g4=g4, bank=bank, bname=bname):
                    t.wait_ge(s_dve.h, xn_ready)
                    bank_wait(t, bname)
                    last = None
                    for j in range(4):
                        ic = 4 * g4 + j
                        last = t.transpose(bank[:, 128 * j:128 * (j + 1)],
                                           xnb[pp][:, 128 * ic:128 * (ic + 1)],
                                           identb[:])
                    s_pe.inc(last)
                grp_done = s_pe.v

                @blk.vector
                def _(v, g4=g4, bank=bank, grp_done=grp_done):
                    v.wait_ge(s_pe.h, grp_done)
                    dst3 = dst_xnT[:].rearrange("p (c f) -> p c f", f=1024)
                    ins = v.tensor_copy(dst3[:, 4 * g4:4 * g4 + 4,
                                             toff:toff + 128],
                                        bank[:, 0:512].rearrange(
                                            "p (c f) -> p c f", f=128))
                    s_dve.inc(ins)
                bank_consumed(bname, s_dve)

            if tt < 8:
                for g4 in range(4):
                    bank = [pD, pE][g4 % 2]
                    bname = ["pD", "pE"][g4 % 2]

                    @blk.tensor
                    def _(t, g4=g4, bank=bank, bname=bname):
                        t.wait_ge(s_dve.h, xn_ready)
                        bank_wait(t, bname)
                        last = None
                        for j in range(4):
                            ic = 4 * g4 + j
                            last = t.transpose(bank[:, 128 * j:128 * (j + 1)],
                                               xnf[pp][:, 128 * ic:128 * (ic + 1)],
                                               identf[:])
                        s_pe.inc(last)
                    grp_done = s_pe.v

                    @blk.vector
                    def _(v, g4=g4, bank=bank, grp_done=grp_done):
                        v.wait_ge(s_pe.h, grp_done)
                        ins = v.tensor_copy(xnTf[:, 512 * g4:512 * (g4 + 1)],
                                            bank[:])
                        s_dve.inc(ins)
                    bank_consumed(bname, s_dve)
                xnTf_ready = s_dve.v
                xn_free[pp] = (s_dve, s_dve.v)

                @blk.tensor
                def _(t):
                    t.wait_ge(s_dve.h, xnTf_ready)
                    t.wait_ge(s_ldx.h, gw_ready)
                    bank_wait(t, "pC")
                    last = None
                    for ic in range(NCH):
                        last = t.matmul(pC[:, 0:16],
                                        xnTf[:, 128 * ic:128 * (ic + 1)],
                                        gw_sb[:, 16 * ic:16 * (ic + 1)],
                                        start=(ic == 0), stop=(ic == NCH - 1))
                    s_pe.inc(last)
                logits_done = s_pe.v

                @blk.vector
                def _(v):
                    v.wait_ge(s_pe.h, logits_done)
                    ins = v.tensor_copy(Ltile[:], pC[:, 0:16])
                    s_dve.inc(ins)
                bank_consumed("pC", s_dve)

                @blk.vector
                def _(v):
                    v.wait_ge(s_dve.h, s_dve.v)
                    vedge(v, v.max(out=topv[:], in_=Ltile[:]))
                    ins = v.tensor_scalar(texp[:], topv[:, 0:4], topv[:, 0:1],
                                          None, ALU.subtract)
                    s_dve.inc(ins)
                texp_ready = s_dve.v

                @blk.scalar
                def _(sc):
                    sc.wait_ge(s_dve.h, texp_ready)
                    ins = sc.activation(texp[:], texp[:], AF.Exp)
                    s_act.inc(ins)
                exp_ready = s_act.v

                @blk.vector
                def _(v):
                    v.wait_ge(s_act.h, exp_ready)
                    vedge(v, v.reduce_sum(pcc[:, 0:1], texp[:], axis=AX))
                    vedge(v, v.reciprocal(pcc[:, 1:2], pcc[:, 0:1]))
                    vedge(v, v.tensor_scalar(probs[:], texp[:], pcc[:, 1:2],
                                             None, ALU.mult))
                    v.memset(combsc[1][:], 0.0)
                    for r in range(TOPK):
                        vedge(v, v.tensor_scalar(maskt[:], Ltile[:],
                                                 topv[:, r:r + 1],
                                                 None, ALU.is_equal))
                        vedge(v, v.reduce_sum(pcc[:, 0:1], maskt[:], axis=AX))
                        vedge(v, v.reciprocal(pcc[:, 1:2], pcc[:, 0:1]))
                        vedge(v, v.tensor_tensor(pcc[:, 0:1], probs[:, r:r + 1],
                                                 pcc[:, 1:2], ALU.mult))
                        vedge(v, v.scalar_tensor_tensor(combsc[r % 2][:],
                                                        maskt[:],
                                                        pcc[:, 0:1],
                                                        combsc[1 - r % 2][:],
                                                        ALU.mult, ALU.add))
                    ins = v.tensor_copy(comb_all[:, 16 * tt:16 * (tt + 1)],
                                        combsc[(TOPK - 1) % 2][:])
                    s_dve.inc(ins)
                comb_ready = s_dve.v

                @blk.tensor
                def _(t):
                    t.wait_ge(s_dve.h, comb_ready)
                    bank_wait(t, "pD")
                    ins = t.transpose(pD[0:16, 0:128],
                                      comb_all[:, 16 * tt:16 * (tt + 1)],
                                      identf[:])
                    s_pe.inc(ins)
                combT_done = s_pe.v

                @blk.vector
                def _(v):
                    v.wait_ge(s_pe.h, combT_done)
                    ins = v.tensor_copy(combF[0:16, 128 * tt:128 * (tt + 1)],
                                        pD[0:16, 0:128])
                    s_dve.inc(ins)
                bank_consumed("pD", s_dve)
            else:
                xn_free[pp] = (s_dve, s_dve.v)
        nc.leave_named_scope(f"A{tt}", _sid[f"A{tt}"], False)

    if debug:
        with nc.Block("dbgA") as blk:
            @blk.sync
            def _(sy):
                sy.wait_ge(s_dve.h, s_dve.v)
                sy.wait_ge(s_act.h, s_act.v)
                sy.dma_start(out=dbg["d_xnT0"], in_=xnT0[:]).then_inc(s_dbg.h, 16)
                sy.dma_start(out=dbg["d_xnT1"], in_=xnT1[:]).then_inc(s_dbg.h, 16)
                sy.dma_start(out=dbg["d_comb"], in_=comb_all[:]).then_inc(s_dbg.h, 16)
                sy.dma_start(out=dbg["d_combF"], in_=combF[:]).then_inc(s_dbg.h, 16)
                s_dbg.v += 64

    esA.close()

    esB = ExitStack()
    sbB = lambda n, s, d: esB.enter_context(nc.sbuf_tensor(n, s, d))
    wslab = [sbB(f"wslab{i}", [128, 2048], BF16) for i in range(2)]
    WTq = sbB("WTq", [128, NCH * 256], BF16)
    WTk = sbB("WTk", [128, NCH * 256], BF16)
    WTv = sbB("WTv", [128, NCH * 256], BF16)
    WT = {"q": WTq, "k": WTk, "v": WTv}
    kqst = [sbB(f"kqst{i}", [128, 512], F32) for i in range(2)]
    ropeo = [sbB(f"ropeo{i}", [128, 512], F32) for i in range(2)]
    eluscr = [sbB(f"eluscr{i}", [128, 512], F32) for i in range(2)]
    ang = sbB("ang", [128, 512], F32)
    tscr = sbB("tscr", [128, 512], F32)
    tscr2 = sbB("tscr2", [128, 512], F32)
    sinb = sbB("sinb", [128, 512], F32)
    cosb = sbB("cosb", [128, 512], F32)
    k_bf = [sbB(f"k_bf{i}", [128, 2048], BF16) for i in range(2)]
    kf_bf = [sbB(f"kf_bf{i}", [128, 2048], BF16) for i in range(2)]
    q_bf = [sbB(f"q_bf{i}", [128, 1024], BF16) for i in range(2)]
    qf_bf = [sbB(f"qf_bf{i}", [128, 1024], BF16) for i in range(2)]
    v_stage = sbB("v_stage", [128, 512], BF16)
    sq_bf = sbB("sq_bf", [128, 512], BF16)
    vab = [sbB(f"vab{i}", [128, NCH * 132], BF16) for i in range(2)]
    kftok = [sbB(f"kftok{i}", [128, NCH * 128], BF16) for i in range(2)]
    ptil = sbB("ptil", [128, NCH * 128], BF16)
    mem_aug = [sbB(f"mem_aug{i}", [128, 132], BF16) for i in range(2)]
    cscr = sbB("cscr", [128, 128], F32)
    rcls = sbB("rcls", [128, 4], F32)

    slab_free = [(s_pe, 0)] * 2
    vst_free = (s_pe, 0)
    nload = 0
    nproj = 0
    ntr = 0
    ptil_free_pe = 0
    mbf_free_pe = 0

    def tbank():
        return [(pT1, "pT1"), (pT2, "pT2")][ntr % 2]

    def qslice(ic, tc):
        src = xnT0 if tc < 2 else xnT1
        return src[:, 1024 * ic + 512 * (tc % 2):1024 * ic + 512 * (tc % 2) + 512]

    W_SRC = {"q": wq_d, "k": wk_d, "v": wv_d}

    for p in range(8):
        _sid[f"P{p}"] = nc.enter_named_scope(f"P{p}", False)[0]
        with nc.Block(f"pair{p}") as blk:
            for wi, w in enumerate(("q", "k", "v")):
                for hh in range(2):
                    li = nload
                    nload += 1
                    slot = li % 2
                    hd = p + 8 * hh

                    @blk.gpsimd
                    def _(g, li=li, slot=slot, hd=hd, w=w):
                        ctr, val = slab_free[slot]
                        g.wait_ge(ctr.h, val)
                        ins = g.dma_start(out=wslab[slot][:],
                                          in_=W_SRC[w][128 * hd:128 * (hd + 1), :])
                        s_wl.inc(ins, 16)
                    ld_done = s_wl.v

                    for g8 in range(2):
                        bank, bname = tbank()
                        ntr += 1

                        @blk.tensor
                        def _(t, slot=slot, g8=g8, bank=bank, bname=bname,
                              ld_done=ld_done):
                            t.wait_ge(s_wl.h, ld_done)
                            bank_wait(t, bname)
                            last = None
                            for j in range(8):
                                ic = 8 * g8 + j
                                last = t.transpose(
                                    bank[:, 128 * j:128 * (j + 1)],
                                    wslab[slot][:, 128 * ic:128 * (ic + 1)],
                                    identb[:])
                            s_pe.inc(last)
                        tr_done = s_pe.v

                        @blk.vector
                        def _(v, w=w, hh=hh, g8=g8, bank=bank,
                              tr_done=tr_done):
                            v.wait_ge(s_pe.h, tr_done)
                            dst3 = WT[w][:].rearrange("p (c f) -> p c f", f=256)
                            ins = v.tensor_copy(
                                dst3[:, 8 * g8:8 * g8 + 8,
                                     128 * hh:128 * hh + 128],
                                bank[:].rearrange("p (c f) -> p c f", f=128))
                            s_dve.inc(ins)
                        bank_consumed(bname, s_dve)
                    slab_free[slot] = (s_pe, s_pe.v)

            wt_ready_dve = s_dve.v

            @blk.vector
            def _(v):
                v3a = vab[0][:].rearrange("p (c w) -> p c w", w=132)
                v3b = vab[1][:].rearrange("p (c w) -> p c w", w=132)
                v.memset(v3a[:, :, 128:129], 1.0)
                ins = v.memset(v3b[:, :, 128:129], 1.0)
                s_dve.inc(ins)

            kbf_ready = [0, 0, 0, 0]
            qbf_ready = [0, 0]

            for tc in range(4):
                d0 = s_dve.v
                a0 = s_act.v

                @blk.vector
                def _(v, tc=tc, p=p):
                    vedge(v, v.tensor_scalar(ang[:], iota512[:],
                                             tb_bc[:, tc:tc + 1],
                                             invf_tab[:, p:p + 1],
                                             ALU.add, ALU.mult))
                    vedge(v, v.tensor_scalar(tscr[:], ang[:], INV2PI, None,
                                             ALU.mult))
                    vedge(v, v.tensor_scalar(tscr[:], tscr[:], MAGIC, MAGIC,
                                             ALU.add, ALU.subtract))
                    v.scalar_tensor_tensor(tscr[:], tscr[:], float(-2 * PI),
                                           ang[:], ALU.mult, ALU.add)
                    vedge(v, v.tensor_scalar(tscr2[:], ang[:], INV2PI, 0.25,
                                             ALU.mult, ALU.add))
                    vedge(v, v.tensor_scalar(tscr2[:], tscr2[:], MAGIC, MAGIC,
                                             ALU.add, ALU.subtract))
                    ins = v.scalar_tensor_tensor(tscr2[:], tscr2[:],
                                                 float(-2 * PI), ang[:],
                                                 ALU.mult, ALU.add)
                    s_dve.inc(ins)
                assert s_dve.v == d0 + 1

                @blk.scalar
                def _(sc, tc=tc, d0=d0):
                    sc.wait_ge(s_dve.h, d0 + 1)
                    sc.activation(sinb[:], tscr[:], AF.Sin)
                    ins = sc.activation(cosb[:], tscr2[:], AF.Sin,
                                        bias=cst[:, C_HALFPI:C_HALFPI + 1])
                    s_act.inc(ins)
                    sc.wait_ge(s_dve.h, d0 + 7)
                    sc.activation(eluscr[0][:], eluscr[0][:], AF.Exp)
                    ins = sc.activation(eluscr[1][:], eluscr[1][:], AF.Exp)
                    s_act.inc(ins)
                    if tc < 2:
                        sc.wait_ge(s_dve.h, d0 + 10)
                        sc.activation(eluscr[0][:], eluscr[0][:], AF.Exp)
                        ins = sc.activation(eluscr[1][:], eluscr[1][:], AF.Exp)
                        s_act.inc(ins)
                assert s_act.v == a0 + (3 if tc < 2 else 2)

                for hh in range(2):
                    bank, bname = [(pA, "pA"), (pB, "pB")][nproj % 2]
                    nproj += 1

                    @blk.tensor
                    def _(t, hh=hh, tc=tc, bank=bank, bname=bname,
                          wt_ready_dve=wt_ready_dve):
                        if tc == 0 and hh == 0:
                            t.wait_ge(s_dve.h, wt_ready_dve)
                        bank_wait(t, bname)
                        last = None
                        for ic in range(NCH):
                            last = t.matmul(
                                bank[:],
                                WTv[:, 256 * ic + 128 * hh:256 * ic + 128 * hh + 128],
                                qslice(ic, tc),
                                start=(ic == 0), stop=(ic == NCH - 1))
                        s_pe.inc(last)
                    mm_done = s_pe.v

                    @blk.vector
                    def _(v, bank=bank, mm_done=mm_done):
                        ctr, val = vst_free
                        v.wait_ge(ctr.h, val)
                        v.wait_ge(s_pe.h, mm_done)
                        ins = v.tensor_copy(v_stage[:], bank[:])
                        s_dve.inc(ins)
                    bank_consumed(bname, s_dve)
                    vst_dve = s_dve.v

                    bank2, bname2 = tbank()
                    ntr += 1

                    @blk.tensor
                    def _(t, bank2=bank2, bname2=bname2, vst_dve=vst_dve):
                        t.wait_ge(s_dve.h, vst_dve)
                        bank_wait(t, bname2)
                        last = None
                        for j in range(4):
                            last = t.transpose(bank2[:, 128 * j:128 * (j + 1)],
                                               v_stage[:, 128 * j:128 * (j + 1)],
                                               identb[:])
                        s_pe.inc(last)
                    vtr_done = s_pe.v
                    vst_free = (s_pe, s_pe.v)

                    @blk.vector
                    def _(v, hh=hh, tc=tc, bank2=bank2, vtr_done=vtr_done):
                        v.wait_ge(s_pe.h, vtr_done)
                        dst3 = vab[hh][:].rearrange("p (c w) -> p c w", w=132)
                        ins = v.tensor_copy(dst3[:, 4 * tc:4 * tc + 4, 0:128],
                                            bank2[:, 0:512].rearrange(
                                                "p (c f) -> p c f", f=128))
                        s_dve.inc(ins)
                    bank_consumed(bname2, s_dve)
                assert s_dve.v == d0 + 5

                kmm = []
                for hh in range(2):
                    bank, bname = [(pA, "pA"), (pB, "pB")][nproj % 2]
                    nproj += 1

                    @blk.tensor
                    def _(t, hh=hh, tc=tc, bank=bank, bname=bname):
                        bank_wait(t, bname)
                        last = None
                        for ic in range(NCH):
                            last = t.matmul(
                                bank[:],
                                WTk[:, 256 * ic + 128 * hh:256 * ic + 128 * hh + 128],
                                qslice(ic, tc),
                                start=(ic == 0), stop=(ic == NCH - 1))
                        s_pe.inc(last)
                    kmm.append((bank, bname, s_pe.v))

                @blk.vector
                def _(v, tc=tc, kmm=kmm, a0=a0):
                    v.wait_ge(s_pe.h, kmm[1][2])
                    v.tensor_copy(kqst[0][:], kmm[0][0][:])
                    ins = v.tensor_copy(kqst[1][:], kmm[1][0][:])
                    s_dve.inc(ins)
                    v.wait_ge(s_act.h, a0 + 1)
                    v.tensor_tensor(tscr[:], kqst[0][:], cosb[:], ALU.mult)
                    v.tensor_tensor(tscr2[:], kqst[1][:], sinb[:], ALU.mult)
                    v.tensor_tensor(ropeo[0][:], tscr[:], tscr2[:],
                                    ALU.subtract)
                    v.tensor_tensor(tscr[:], kqst[1][:], cosb[:], ALU.mult)
                    v.tensor_tensor(tscr2[:], kqst[0][:], sinb[:], ALU.mult)
                    v.tensor_tensor(ropeo[1][:], tscr[:], tscr2[:], ALU.add)
                    for hh in range(2):
                        v.tensor_copy(k_bf[hh][:, 512 * tc:512 * (tc + 1)],
                                      ropeo[hh][:])
                    v.tensor_scalar(eluscr[0][:], ropeo[0][:], 0.0, None,
                                    ALU.min)
                    v.tensor_scalar(eluscr[1][:], ropeo[1][:], 0.0, None,
                                    ALU.min)
                    v.tensor_scalar(tscr[:], ropeo[0][:], 0.0, None, ALU.max)
                    ins = v.tensor_scalar(tscr2[:], ropeo[1][:], 0.0, None,
                                          ALU.max)
                    s_dve.inc(ins)
                for bank, bname, _v in kmm:
                    bank_free[bname] = (s_dve, d0 + 6)

                @blk.vector
                def _(v, tc=tc, a0=a0):
                    v.wait_ge(s_act.h, a0 + 2)
                    v.tensor_tensor(kf_bf[0][:, 512 * tc:512 * (tc + 1)],
                                    eluscr[0][:], tscr[:], ALU.add)
                    ins = v.tensor_tensor(kf_bf[1][:, 512 * tc:512 * (tc + 1)],
                                          eluscr[1][:], tscr2[:], ALU.add)
                    s_dve.inc(ins)
                kbf_ready[tc] = s_dve.v
                assert s_dve.v == d0 + 8

                if tc < 2:
                    qmm = []
                    for hh in range(2):
                        bank, bname = [(pA, "pA"), (pB, "pB")][nproj % 2]
                        nproj += 1

                        @blk.tensor
                        def _(t, hh=hh, tc=tc, bank=bank, bname=bname):
                            bank_wait(t, bname)
                            last = None
                            for ic in range(NCH):
                                last = t.matmul(
                                    bank[:],
                                    WTq[:, 256 * ic + 128 * hh:256 * ic + 128 * hh + 128],
                                    qslice(ic, tc),
                                    start=(ic == 0), stop=(ic == NCH - 1))
                            s_pe.inc(last)
                        qmm.append((bank, bname, s_pe.v))

                    @blk.vector
                    def _(v, tc=tc, qmm=qmm, a0=a0):
                        v.wait_ge(s_pe.h, qmm[1][2])
                        v.tensor_copy(kqst[0][:], qmm[0][0][:])
                        ins = v.tensor_copy(kqst[1][:], qmm[1][0][:])
                        s_dve.inc(ins)
                        v.tensor_tensor(tscr[:], kqst[0][:], cosb[:], ALU.mult)
                        v.tensor_tensor(tscr2[:], kqst[1][:], sinb[:],
                                        ALU.mult)
                        v.tensor_tensor(ropeo[0][:], tscr[:], tscr2[:],
                                        ALU.subtract)
                        v.tensor_tensor(tscr[:], kqst[1][:], cosb[:], ALU.mult)
                        v.tensor_tensor(tscr2[:], kqst[0][:], sinb[:],
                                        ALU.mult)
                        v.tensor_tensor(ropeo[1][:], tscr[:], tscr2[:],
                                        ALU.add)
                        for hh in range(2):
                            v.tensor_copy(q_bf[hh][:, 512 * tc:512 * (tc + 1)],
                                          ropeo[hh][:])
                        v.tensor_scalar(eluscr[0][:], ropeo[0][:], 0.0, None,
                                        ALU.min)
                        v.tensor_scalar(eluscr[1][:], ropeo[1][:], 0.0, None,
                                        ALU.min)
                        v.tensor_scalar(tscr[:], ropeo[0][:], 0.0, None,
                                        ALU.max)
                        ins = v.tensor_scalar(tscr2[:], ropeo[1][:], 0.0,
                                              None, ALU.max)
                        s_dve.inc(ins)
                    for bank, bname, _v in qmm:
                        bank_free[bname] = (s_dve, d0 + 9)

                    @blk.vector
                    def _(v, tc=tc, a0=a0):
                        v.wait_ge(s_act.h, a0 + 3)
                        v.tensor_tensor(qf_bf[0][:, 512 * tc:512 * (tc + 1)],
                                        eluscr[0][:], tscr[:], ALU.add)
                        ins = v.tensor_tensor(
                            qf_bf[1][:, 512 * tc:512 * (tc + 1)],
                            eluscr[1][:], tscr2[:], ALU.add)
                        s_dve.inc(ins)
                    qbf_ready[tc] = s_dve.v
                    assert s_dve.v == d0 + 11

            for hh in range(2):
                hglob = p + 8 * hh

                for g8 in range(2):
                    bank, bname = tbank()
                    ntr += 1

                    @blk.tensor
                    def _(t, hh=hh, g8=g8, bank=bank, bname=bname):
                        t.wait_ge(s_dve.h, kbf_ready[3])
                        bank_wait(t, bname)
                        last = None
                        for j in range(8):
                            sc_i = 8 * g8 + j
                            last = t.transpose(
                                bank[:, 128 * j:128 * (j + 1)],
                                kf_bf[hh][:, 128 * sc_i:128 * (sc_i + 1)],
                                identb[:])
                        s_pe.inc(last)
                    tr_done = s_pe.v

                    @blk.vector
                    def _(v, hh=hh, g8=g8, bank=bank, tr_done=tr_done):
                        v.wait_ge(s_pe.h, tr_done)
                        ins = v.tensor_copy(
                            kftok[hh][:, 1024 * g8:1024 * (g8 + 1)], bank[:])
                        s_dve.inc(ins)
                    bank_consumed(bname, s_dve)
                kftok_dve = s_dve.v

                @blk.tensor
                def _(t, hh=hh, kftok_dve=kftok_dve):
                    t.wait_ge(s_dve.h, kftok_dve)
                    bank_wait(t, "pC")
                    last = None
                    for sc_i in range(NCH):
                        last = t.matmul(
                            pC[:, 0:129],
                            kftok[hh][:, 128 * sc_i:128 * (sc_i + 1)],
                            vab[hh][:, 132 * sc_i:132 * sc_i + 129],
                            start=(sc_i == 0), stop=(sc_i == NCH - 1))
                    s_pe.inc(last)
                mem_done = s_pe.v

                @blk.vector
                def _(v, hh=hh, mem_done=mem_done):
                    v.wait_ge(s_pe.h, mem_done)
                    ins = v.tensor_copy(mem_aug[hh][:, 0:129], pC[:, 0:129])
                    s_dve.inc(ins)
                bank_consumed("pC", s_dve)
                memaug_act = s_dve.v

                prev_nrm_pe = 0
                prev_red_dve = 0
                for i6 in range(6):
                    is_q = i6 >= 4
                    tci = i6 - 4 if is_q else i6
                    srcb = q_bf[hh] if is_q else k_bf[hh]

                    @blk.vector
                    def _(v, srcb=srcb, tci=tci, i6=i6,
                          prev_nrm_pe=prev_nrm_pe):
                        if i6 > 0:
                            v.wait_ge(s_pe.h, prev_nrm_pe)
                        ins = v.tensor_tensor(sq_bf[:],
                                              srcb[:, 512 * tci:512 * (tci + 1)],
                                              srcb[:, 512 * tci:512 * (tci + 1)],
                                              ALU.mult)
                        s_dve.inc(ins)
                    sq_dve = s_dve.v

                    @blk.tensor
                    def _(t, sq_dve=sq_dve, i6=i6, prev_red_dve=prev_red_dve):
                        t.wait_ge(s_dve.h, sq_dve)
                        if i6 == 0:
                            bank_wait(t, "pE")
                        else:
                            t.wait_ge(s_dve.h, prev_red_dve)
                        ins = t.matmul(pE[0:1, 0:512], ones_col_b[:], sq_bf[:],
                                       start=True, stop=True)
                        s_pe.inc(ins)
                    nrm_done = s_pe.v
                    prev_nrm_pe = nrm_done

                    @blk.vector
                    def _(v, i6=i6, nrm_done=nrm_done):
                        v.wait_ge(s_pe.h, nrm_done)
                        ins = v.reduce_max(mscr[0:1, i6:i6 + 1], pE[0:1, 0:512],
                                           axis=AX)
                        s_dve.inc(ins)
                    prev_red_dve = s_dve.v
                bank_consumed("pE", s_dve)

                @blk.vector
                def _(v):
                    v.wait_ge(s_dve.h, s_dve.v)
                    vedge(v, v.reduce_max(mscr[0:1, 6:7], mscr[0:1, 0:4],
                                          axis=AX))
                    vedge(v, v.reduce_max(mscr[0:1, 7:8], mscr[0:1, 4:6],
                                          axis=AX))
                    ins = v.tensor_tensor(mscr[0:1, 6:7], mscr[0:1, 6:7],
                                          mscr[0:1, 7:8], ALU.mult)
                    s_dve.inc(ins)
                bnd_dve = s_dve.v

                @blk.scalar
                def _(sc, bnd_dve=bnd_dve):
                    sc.wait_ge(s_dve.h, bnd_dve)
                    aedge(sc, sc.activation(mscr[0:1, 6:7], mscr[0:1, 6:7],
                                            AF.Ln,
                                            scale=float(1.0 / 128.0)))
                    ins = sc.activation(mscr[0:1, 6:7], mscr[0:1, 6:7], AF.Exp,
                                        scale=0.5)
                    s_act.inc(ins)
                bnd_act = s_act.v

                @blk.vector
                def _(v, bnd_act=bnd_act, mbf_free_pe=mbf_free_pe):
                    v.wait_ge(s_act.h, bnd_act)
                    if mbf_free_pe:
                        v.wait_ge(s_pe.h, mbf_free_pe)
                    ins = v.tensor_scalar(mbf[0:1, 0:1], mscr[0:1, 6:7],
                                          -1.02, -0.3, ALU.mult, ALU.add)
                    s_dve.inc(ins)
                mbf_dve = s_dve.v

                @blk.tensor
                def _(t, mbf_dve=mbf_dve):
                    t.wait_ge(s_dve.h, mbf_dve)
                    bank_wait(t, "pE")
                    ins = t.matmul(pE[:, 0:1], ones_row_b[:], mbf[0:1, 0:1],
                                   start=True, stop=True)
                    s_pe.inc(ins)
                mb_done = s_pe.v
                mbf_free_pe = mb_done

                @blk.vector
                def _(v, mb_done=mb_done):
                    v.wait_ge(s_pe.h, mb_done)
                    ins = v.tensor_copy(negM[:], pE[:, 0:1])
                    s_dve.inc(ins)
                bank_consumed("pE", s_dve)
                negM_dve = s_dve.v

                pe0 = s_pe.v
                act0 = s_act.v
                dve0 = s_dve.v
                for qt in range(8):
                    bankPV, bnPV = [(pD, "pD"), (pE, "pE")][qt % 2]
                    bankMO, bnMO = [(pF, "pF"), (pC, "pC")][qt % 2]

                    @blk.tensor
                    def _(t, hh=hh, qt=qt, bankPV=bankPV, bnPV=bnPV,
                          bankMO=bankMO, bnMO=bnMO, negM_dve=negM_dve,
                          memaug_act=memaug_act, pe0=pe0, act0=act0,
                          dve0=dve0):
                        for sc_i in range(NCH):
                            idx = 16 * qt + sc_i
                            bank = [pA, pB][sc_i % 2]
                            if idx == 0:
                                t.wait_ge(s_dve.h, max(kbf_ready[3],
                                                       qbf_ready[1],
                                                       negM_dve))
                                bank_wait(t, "pA")
                            elif idx == 1:
                                bank_wait(t, "pB")
                            else:
                                t.wait_ge(s_act.h, act0 + idx - 1)
                            ins = t.matmul(
                                bank[:, 0:128],
                                k_bf[hh][:, 128 * sc_i:128 * (sc_i + 1)],
                                q_bf[hh][:, 128 * qt:128 * (qt + 1)],
                                start=True, stop=True)
                            s_pe.inc(ins)
                        t.wait_ge(s_act.h, act0 + 16 * (qt + 1))
                        if qt < 2:
                            bank_wait(t, bnPV)
                        else:
                            t.wait_ge(s_dve.h, dve0 + qt - 1)
                        last = None
                        for sc_i in range(NCH):
                            last = t.matmul(
                                bankPV[:, 0:129],
                                ptil[:, 128 * sc_i:128 * (sc_i + 1)],
                                vab[hh][:, 132 * sc_i:132 * sc_i + 129],
                                start=(sc_i == 0), stop=(sc_i == NCH - 1))
                        s_pe.inc(last)
                        if qt < 2:
                            bank_wait(t, bnMO)
                        if qt == 0:
                            t.wait_ge(s_dve.h, memaug_act)
                        ins = t.matmul(bankMO[:, 0:129],
                                       qf_bf[hh][:, 128 * qt:128 * (qt + 1)],
                                       mem_aug[hh][:, 0:129],
                                       start=True, stop=True)
                        s_pe.inc(ins)

                @blk.scalar
                def _(sc, hh=hh, pe0=pe0):
                    for qt in range(8):
                        for sc_i in range(NCH):
                            sc.wait_ge(s_pe.h, pe0 + 18 * qt + sc_i + 1)
                            bank = [pA, pB][sc_i % 2]
                            ins = sc.activation(
                                ptil[:, 128 * sc_i:128 * (sc_i + 1)],
                                bank[:, 0:128], AF.Exp, bias=negM[:],
                                scale=float(1.0 / np.sqrt(128.0)))
                            s_act.inc(ins)
                assert s_act.v == act0 + 128

                for qt in range(8):
                    bankPV = [pD, pE][qt % 2]
                    bankMO = [pF, pC][qt % 2]

                    @blk.vector
                    def _(v, hh=hh, hglob=hglob, qt=qt, bankPV=bankPV,
                          bankMO=bankMO, pe0=pe0):
                        v.wait_ge(s_pe.h, pe0 + 18 * qt + 18)
                        vedge(v, v.reciprocal(rcls[:, 0:1],
                                              bankPV[:, 128:129]))
                        vedge(v, v.reciprocal(rcls[:, 1:2],
                                              bankMO[:, 128:129]))
                        vedge(v, v.tensor_tensor(rcls[:, 2:3],
                                                 rcls[:, 0:1],
                                                 g1m_bc[:, hglob:hglob + 1],
                                                 ALU.mult))
                        vedge(v, v.tensor_tensor(rcls[:, 3:4],
                                                 rcls[:, 1:2],
                                                 g_bc[:, hglob:hglob + 1],
                                                 ALU.mult))
                        vedge(v, v.tensor_scalar(cscr[:], bankPV[:, 0:128],
                                                 rcls[:, 2:3], None,
                                                 ALU.mult))
                        ins = v.scalar_tensor_tensor(
                            combined[:, 2048 * qt + 128 * hglob:
                                     2048 * qt + 128 * hglob + 128],
                            bankMO[:, 0:128], rcls[:, 3:4], cscr[:],
                            ALU.mult, ALU.add)
                        s_dve.inc(ins)
                assert s_dve.v == dve0 + 8
                for bn in ("pA", "pB"):
                    bank_consumed(bn, s_act)
                for bn in ("pD", "pE", "pF", "pC"):
                    bank_consumed(bn, s_dve)
                ptil_free_pe = s_pe.v
        nc.leave_named_scope(f"P{p}", _sid[f"P{p}"], False)

    if debug:
        with nc.Block("dbgB") as blk:
            @blk.sync
            def _(sy):
                sy.wait_ge(s_dve.h, s_dve.v)
                sy.wait_ge(s_act.h, s_act.v)
                for nm, t in (("d_kbf0", k_bf[0]), ("d_kbf1", k_bf[1]),
                              ("d_kfbf0", kf_bf[0]), ("d_qbf0", q_bf[0]),
                              ("d_qfbf0", qf_bf[0]), ("d_vab0", vab[0]),
                              ("d_memaug0", mem_aug[0]),
                              ("d_memaug1", mem_aug[1]),
                              ("d_negM", negM), ("d_mscr", mscr),
                              ("d_ptil", ptil), ("d_combined", combined)):
                    sy.dma_start(out=dbg[nm], in_=t[:]).then_inc(s_dbg.h, 16)
                    s_dbg.v += 16
    dbg_pair_sdbg = s_dbg.v if debug else 0

    esB.close()
    esX.close()

    esFin = ExitStack()
    ySB = esFin.enter_context(
        nc.sbuf_tensor("ySB", [128, NCH * 1024], F32, side="right"))
    esWo = ExitStack()
    combT = esWo.enter_context(
        nc.sbuf_tensor("combT", [128, NCH * 1024], BF16, side="right"))
    woslab = [esWo.enter_context(nc.sbuf_tensor(f"woslab{i}", [128, 2048], BF16))
              for i in range(2)]
    WoT = esWo.enter_context(nc.sbuf_tensor("WoT", [128, NCH * 128], BF16))

    _sid["WO"] = nc.enter_named_scope("WO", False)[0]
    with nc.Block("wo") as blk:
        comb_dve = s_dve.v
        for qt in range(8):
            for g8 in range(2):
                bank, bname = tbank()
                ntr += 1

                @blk.tensor
                def _(t, qt=qt, g8=g8, bank=bank, bname=bname,
                      comb_dve=comb_dve):
                    t.wait_ge(s_dve.h, comb_dve)
                    bank_wait(t, bname)
                    last = None
                    for j in range(8):
                        ic = 8 * g8 + j
                        last = t.transpose(
                            bank[:, 128 * j:128 * (j + 1)],
                            combined[:, 2048 * qt + 128 * ic:
                                      2048 * qt + 128 * (ic + 1)],
                            identb[:])
                    s_pe.inc(last)
                tr_done = s_pe.v

                @blk.vector
                def _(v, qt=qt, g8=g8, bank=bank, tr_done=tr_done):
                    v.wait_ge(s_pe.h, tr_done)
                    dst3 = combT[:].rearrange("p (c f) -> p c f", f=1024)
                    ins = v.tensor_copy(dst3[:, 8 * g8:8 * g8 + 8,
                                             128 * qt:128 * (qt + 1)],
                                        bank[:].rearrange("p (c f) -> p c f",
                                                          f=128))
                    s_dve.inc(ins)
                bank_consumed(bname, s_dve)
        combT_act = s_dve.v

        woslab_free = [(s_pe, 0), (s_pe, 0)]
        for jc in range(NCH):
            slot = jc % 2

            @blk.gpsimd
            def _(g, jc=jc, slot=slot):
                if debug and jc == 0:
                    g.wait_ge(s_dbg.h, dbg_pair_sdbg)
                ctr, val = woslab_free[slot]
                g.wait_ge(ctr.h, val)
                ins = g.dma_start(out=woslab[slot][:],
                                  in_=wo_d[128 * jc:128 * (jc + 1), :])
                s_wl.inc(ins, 16)
            ld_done = s_wl.v

            for g8 in range(2):
                bank, bname = tbank()
                ntr += 1

                @blk.tensor
                def _(t, slot=slot, g8=g8, bank=bank, bname=bname,
                      ld_done=ld_done):
                    t.wait_ge(s_wl.h, ld_done)
                    bank_wait(t, bname)
                    last = None
                    for j in range(8):
                        ic = 8 * g8 + j
                        last = t.transpose(bank[:, 128 * j:128 * (j + 1)],
                                           woslab[slot][:, 128 * ic:128 * (ic + 1)],
                                           identb[:])
                    s_pe.inc(last)
                tr_done = s_pe.v

                @blk.vector
                def _(v, g8=g8, bank=bank, tr_done=tr_done):
                    v.wait_ge(s_pe.h, tr_done)
                    ins = v.tensor_copy(WoT[:, 1024 * g8:1024 * (g8 + 1)],
                                        bank[:])
                    s_dve.inc(ins)
                bank_consumed(bname, s_dve)
            wot_act = s_dve.v
            woslab_free[slot] = (s_pe, s_pe.v)

            for tch in range(2):
                bank, bname = [(pA, "pA"), (pB, "pB")][tch]

                @blk.tensor
                def _(t, jc=jc, tch=tch, bank=bank, bname=bname,
                      wot_act=wot_act, combT_act=combT_act):
                    t.wait_ge(s_dve.h, max(wot_act, combT_act))
                    bank_wait(t, bname)
                    last = None
                    for ic in range(NCH):
                        last = t.matmul(
                            bank[:],
                            WoT[:, 128 * ic:128 * (ic + 1)],
                            combT[:, 1024 * ic + 512 * tch:
                                  1024 * ic + 512 * tch + 512],
                            start=(ic == 0), stop=(ic == NCH - 1))
                    s_pe.inc(last)
                mm_done = s_pe.v

                @blk.vector
                def _(v, jc=jc, tch=tch, bank=bank, mm_done=mm_done):
                    v.wait_ge(s_pe.h, mm_done)
                    ins = v.tensor_copy(ySB[:, 1024 * jc + 512 * tch:
                                            1024 * jc + 512 * tch + 512],
                                        bank[:])
                    s_dve.inc(ins)
                bank_consumed(bname, s_dve)
    nc.leave_named_scope("WO", _sid["WO"], False)
    ySB_wo_act = s_act.v
    ySB_wo_dve = s_dve.v
    if debug:
        with nc.Block("dbgWo") as blk:
            @blk.sync
            def _(sy):
                sy.wait_ge(s_act.h, s_act.v)
                sy.dma_start(out=dbg["d_ySBwo"], in_=ySB[:]).then_inc(s_dbg.h, 16)
                s_dbg.v += 16
    dbg_wo_sdbg = s_dbg.v if debug else 0

    esWo.close()
    esCm.close()
    esW.close()

    esC = ExitStack()
    wbuf = [esC.enter_context(nc.sbuf_tensor(f"wbuf{i}", [128, NCH * 768],
                                             BF16)) for i in range(2)]
    h1sb = esC.enter_context(nc.sbuf_tensor("h1sb", [128, 12 * 512], BF16))
    h2sc = [esC.enter_context(nc.sbuf_tensor(f"h2sc{i}", [128, 512], BF16))
            for i in range(2)]
    cSB = esC.enter_context(nc.sbuf_tensor("cSB", [128, 1024], BF16))
    NIH = 6

    wlist = []
    for ih in range(2):
        wlist.append((sg_d[:, 768 * ih:768 * (ih + 1)],
                      su_d[:, 768 * ih:768 * (ih + 1)],
                      sd_d[768 * ih:768 * (ih + 1), :], False, -1))
    for e in range(E):
        for ih in range(2):
            wlist.append((wg_d[e][:, 768 * ih:768 * (ih + 1)],
                          wu_d[e][:, 768 * ih:768 * (ih + 1)],
                          wd_d[e][768 * ih:768 * (ih + 1), :], True, e))

    def emit_wload(blk, src_ap, buf, kind, wait_pe):
        @blk.gpsimd
        def _(g, src_ap=src_ap, buf=buf, kind=kind, wait_pe=wait_pe):
            if wait_pe:
                g.wait_ge(s_pe.h, wait_pe)
            if kind == "gu":
                for grp in range(2):
                    ins = g.dma_start(
                        out=buf[:, 768 * 8 * grp:768 * 8 * (grp + 1)],
                        in_=src_ap[1024 * grp:1024 * (grp + 1), :].rearrange(
                            "(c p) f -> p c f", p=128))
                    s_wl.inc(ins, 16)
            else:
                ins = g.dma_start(
                    out=buf[:, 0:6 * 2048],
                    in_=src_ap.rearrange("(c p) f -> p c f", p=128))
                s_wl.inc(ins, 16)
        return s_wl.v

    dbg_h_sdbg = 0
    with nc.Block("ffnpre") as blk:
        ld_done = [None] * (3 * len(wlist))
        ld_done[0] = emit_wload(blk, wlist[0][0], wbuf[0], "gu", 0)
        ld_done[1] = emit_wload(blk, wlist[0][1], wbuf[1], "gu", 0)

    for ei in range(len(wlist)):
        srcg, srcu, srcd, routed, eidx = wlist[ei]
        gb = wbuf[ei % 2]
        ub = wbuf[(ei + 1) % 2]
        db = wbuf[ei % 2]

        _sid[f"F{ei}"] = nc.enter_named_scope(f"F{ei}", False)[0]
        with nc.Block(f"ffn{ei}") as blk:
            gate_start_wl = ld_done[3 * ei]
            peG0 = s_pe.v
            actG0 = s_act.v
            for tch in range(2):
                for nIc in range(NIH):
                    bank, bn = [(pA, "pA"), (pB, "pB")][nIc % 2]
                    idx = NIH * tch + nIc

                    @blk.tensor
                    def _(t, tch=tch, nIc=nIc, bank=bank, bn=bn, idx=idx,
                          gate_start_wl=gate_start_wl, gb=gb, actG0=actG0):
                        if idx == 0:
                            t.wait_ge(s_wl.h, gate_start_wl)
                            bank_wait(t, "pA")
                        elif idx == 1:
                            bank_wait(t, "pB")
                        else:
                            t.wait_ge(s_act.h, actG0 + idx - 1)
                        last = None
                        for hc in range(NCH):
                            last = t.matmul(
                                bank[:],
                                gb[:, 768 * hc + 128 * nIc:
                                   768 * hc + 128 * nIc + 128],
                                xnT0[:, 1024 * hc + 512 * tch:
                                     1024 * hc + 512 * tch + 512],
                                start=(hc == 0), stop=(hc == NCH - 1))
                        s_pe.inc(last)

            @blk.scalar
            def _(sc, ei=ei, peG0=peG0, dbg_h=dbg_h_sdbg if debug else 0):
                for tch in range(2):
                    for nIc in range(NIH):
                        idx = NIH * tch + nIc
                        sc.wait_ge(s_pe.h, peG0 + idx + 1)
                        if debug and ei == 1 and idx == 0:
                            sc.wait_ge(s_dbg.h, dbg_h)
                        bank = [pA, pB][nIc % 2]
                        ins = sc.activation(
                            h1sb[:, 512 * idx:512 * idx + 512],
                            bank[:], AF.Silu)
                        s_act.inc(ins)
            assert s_act.v == actG0 + 12
            for bn in ("pA", "pB"):
                bank_consumed(bn, s_act)
            gate_pe_done = s_pe.v
            ld_done[3 * ei + 2] = emit_wload(blk, srcd, db, "d", gate_pe_done)

            up_start_wl = ld_done[3 * ei + 1]
            if routed:
                @blk.vector
                def _(v, eidx=eidx):
                    ins = v.tensor_scalar(sel_sb[:], iota16[:], float(eidx),
                                          None, ALU.is_equal)
                    s_dve.inc(ins)
                sel_dve = s_dve.v
            else:
                sel_dve = 0

            for tch in range(2):
                dveT0 = s_dve.v
                roff = 1 if routed else 0

                @blk.tensor
                def _(t, tch=tch, dveT0=dveT0, roff=roff, routed=routed,
                      sel_dve=sel_dve, up_start_wl=up_start_wl, ub=ub):
                    if routed:
                        if tch == 0:
                            t.wait_ge(s_dve.h, sel_dve)
                            bank_wait(t, "pC")
                        else:
                            t.wait_ge(s_dve.h, dveT0 - roff - NIH + 1)
                        ins = t.matmul(pC[:], sel_sb[:],
                                       combF[0:16, 512 * tch:512 * (tch + 1)],
                                       start=True, stop=True)
                        s_pe.inc(ins)
                    for nIc in range(NIH):
                        bank = [pD, pE][nIc % 2]
                        if tch == 0 and nIc == 0:
                            t.wait_ge(s_wl.h, up_start_wl)
                        if tch == 0 and nIc < 2:
                            bank_wait(t, ["pD", "pE"][nIc])
                        elif nIc < 2:
                            t.wait_ge(s_dve.h, dveT0 - roff - NIH +
                                      (roff + nIc - 2 + 1) + NIH)
                        else:
                            t.wait_ge(s_dve.h, dveT0 + roff + nIc - 1)
                        last = None
                        for hc in range(NCH):
                            last = t.matmul(
                                bank[:],
                                ub[:, 768 * hc + 128 * nIc:
                                   768 * hc + 128 * nIc + 128],
                                xnT0[:, 1024 * hc + 512 * tch:
                                     1024 * hc + 512 * tch + 512],
                                start=(hc == 0), stop=(hc == NCH - 1))
                        s_pe.inc(last)
                peU = s_pe.v

                @blk.vector
                def _(v, tch=tch, routed=routed, peU=peU, roff=roff):
                    nmm = NIH + (1 if routed else 0)
                    base_pe = peU - nmm
                    if routed:
                        v.wait_ge(s_pe.h, base_pe + 1)
                        ins = v.tensor_copy(cSB[:, 512 * tch:512 * (tch + 1)],
                                            pC[:])
                        s_dve.inc(ins)
                    for nIc in range(NIH):
                        bank = [pD, pE][nIc % 2]
                        v.wait_ge(s_pe.h, base_pe + roff + nIc + 1)
                        hsl = h1sb[:, 512 * (NIH * tch + nIc):
                                   512 * (NIH * tch + nIc) + 512]
                        if routed:
                            v.tensor_tensor(hsl, hsl, bank[:], ALU.mult)
                            ins = v.tensor_tensor(
                                hsl, hsl, cSB[:, 512 * tch:512 * (tch + 1)],
                                ALU.mult)
                        else:
                            ins = v.tensor_tensor(hsl, hsl, bank[:], ALU.mult)
                        s_dve.inc(ins)
                assert s_dve.v == dveT0 + roff + NIH
            for bn in ("pD", "pE", "pC"):
                bank_consumed(bn, s_dve)
            up_pe_done = s_pe.v
            h_dve_done = s_dve.v
            if ei == 0 and debug:
                @blk.sync
                def _(sy, h_dve_done=h_dve_done):
                    sy.wait_ge(s_dve.h, h_dve_done)
                    sy.dma_start(out=dbg["d_h1sb"],
                                 in_=h1sb[:]).then_inc(s_dbg.h, 16)
                    s_dbg.v += 16
            dbg_h_sdbg = s_dbg.v if debug else 0
            if ei + 1 < len(wlist):
                ld_done[3 * (ei + 1)] = emit_wload(blk, wlist[ei + 1][0], ub,
                                                   "gu", up_pe_done)

            down_start_wl = ld_done[3 * ei + 2]
            for tch in range(2):
                for jc in range(NCH):
                    bank, bn = [(pF, "pF"), (pA, "pA")][jc % 2]

                    @blk.tensor
                    def _(t, tch=tch, jc=jc, bank=bank, bn=bn,
                          down_start_wl=down_start_wl, h_dve_done=h_dve_done,
                          db=db):
                        if tch == 0 and jc == 0:
                            t.wait_ge(s_wl.h, down_start_wl)
                            t.wait_ge(s_dve.h, h_dve_done)
                        bank_wait(t, bn)
                        last = None
                        for nIc in range(NIH):
                            last = t.matmul(
                                bank[:],
                                db[:, 2048 * nIc + 128 * jc:
                                   2048 * nIc + 128 * jc + 128],
                                h1sb[:, 512 * (NIH * tch + nIc):
                                     512 * (NIH * tch + nIc) + 512],
                                start=(nIc == 0), stop=(nIc == NIH - 1))
                        s_pe.inc(last)
                    dn_done = s_pe.v

                    @blk.vector
                    def _(v, tch=tch, jc=jc, bank=bank, dn_done=dn_done,
                          ei=ei):
                        v.wait_ge(s_pe.h, dn_done)
                        if ei == 0 and tch == 0 and jc == 0:
                            v.wait_ge(s_dve.h, ySB_wo_dve)
                            if debug:
                                v.wait_ge(s_dbg.h, dbg_wo_sdbg)
                        ysl = ySB[:, 1024 * jc + 512 * tch:
                                  1024 * jc + 512 * tch + 512]
                        ins = v.tensor_tensor(ysl, ysl, bank[:], ALU.add)
                        s_dve.inc(ins)
                    bank_consumed(bn, s_dve)
            down_pe_done = s_pe.v
            if ei + 1 < len(wlist):
                ld_done[3 * (ei + 1) + 1] = emit_wload(
                    blk, wlist[ei + 1][1], db, "gu", down_pe_done)
        nc.leave_named_scope(f"F{ei}", _sid[f"F{ei}"], False)
    esC.close()

    esD = ExitStack()
    xfin = [esD.enter_context(nc.sbuf_tensor(f"xfin{i}", [128, H], F32))
            for i in range(2)]
    ofin = [esD.enter_context(nc.sbuf_tensor(f"ofin{i}", [128, H], F32))
            for i in range(2)]
    xfin_free = [(s_dve, 0), (s_dve, 0)]
    ofin_free = [(s_st, 0), (s_st, 0)]
    ySB_full_dve = s_dve.v

    _sid["FIN"] = nc.enter_named_scope("FIN", False)[0]
    with nc.Block("fin") as blk:
        for qt in range(8):
            pp = qt % 2

            @blk.sync
            def _(sy, qt=qt, pp=pp):
                ctr, val = xfin_free[pp]
                sy.wait_ge(ctr.h, val)
                ins = sy.dma_start(out=xfin[pp][:],
                                   in_=x_d[128 * qt:128 * (qt + 1), :])
                s_ldx.inc(ins, 16)
            x_ready = s_ldx.v

            for g4 in range(4):
                bank, bname = [(pD, "pD"), (pE, "pE")][g4 % 2]

                @blk.tensor
                def _(t, qt=qt, g4=g4, bank=bank, bname=bname,
                      ySB_full_dve=ySB_full_dve):
                    if qt == 0 and g4 == 0:
                        t.wait_ge(s_dve.h, ySB_full_dve)
                    bank_wait(t, bname)
                    last = None
                    for j in range(4):
                        jc = 4 * g4 + j
                        last = t.transpose(
                            bank[:, 128 * j:128 * (j + 1)],
                            ySB[:, 1024 * jc + 128 * qt:
                                1024 * jc + 128 * (qt + 1)],
                            identf[:])
                    s_pe.inc(last)
                tr_done = s_pe.v

                @blk.vector
                def _(v, qt=qt, pp=pp, g4=g4, bank=bank, tr_done=tr_done,
                      x_ready=x_ready):
                    v.wait_ge(s_pe.h, tr_done)
                    v.wait_ge(s_ldx.h, x_ready)
                    if g4 == 0:
                        ctr, val = ofin_free[pp]
                        v.wait_ge(ctr.h, val)
                    ins = v.tensor_tensor(ofin[pp][:, 512 * g4:512 * (g4 + 1)],
                                          xfin[pp][:, 512 * g4:512 * (g4 + 1)],
                                          bank[:], ALU.add)
                    s_dve.inc(ins)
                bank_consumed(bname, s_dve)
            add_done = s_dve.v
            xfin_free[pp] = (s_dve, s_dve.v)

            @blk.sync
            def _(sy, qt=qt, pp=pp, add_done=add_done):
                sy.wait_ge(s_dve.h, add_done)
                ins = sy.dma_start(out=out_d[128 * qt:128 * (qt + 1), :],
                                   in_=ofin[pp][:])
                s_st.inc(ins, 16)
            ofin_free[pp] = (s_st, s_st.v)

        @blk.sync
        def _(sy):
            sy.wait_ge(s_st.h, s_st.v)
            if debug:
                sy.wait_ge(s_dbg.h, s_dbg.v)

    nc.leave_named_scope("FIN", _sid["FIN"], False)
    esD.close()
    esFin.close()
    es.close()
    return nc


_NC_CACHE = None


def kernel(L0_attn_hidden_states, L0_attn_Wq, L0_attn_Wk, L0_attn_Wv,
           L0_attn_Wo, L0_attn_gate, L0_ffn_gate_w, L0_ffn_Wg, L0_ffn_Wu,
           L0_ffn_Wd, L0_ffn_Sg, L0_ffn_Su, L0_ffn_Sd):
    global _NC_CACHE
    x = np.ascontiguousarray(np.asarray(L0_attn_hidden_states, dtype=np.float32))
    wq = np.ascontiguousarray(np.asarray(L0_attn_Wq, dtype=np.float32))
    wk = np.ascontiguousarray(np.asarray(L0_attn_Wk, dtype=np.float32))
    wv = np.ascontiguousarray(np.asarray(L0_attn_Wv, dtype=np.float32))
    wo = np.ascontiguousarray(np.asarray(L0_attn_Wo, dtype=np.float32))
    gate = np.ascontiguousarray(
        np.asarray(L0_attn_gate, dtype=np.float32).reshape(1, NH))
    gw = np.ascontiguousarray(np.asarray(L0_ffn_gate_w, dtype=np.float32))
    wg = np.ascontiguousarray(np.asarray(L0_ffn_Wg, dtype=np.float32))
    wu = np.ascontiguousarray(np.asarray(L0_ffn_Wu, dtype=np.float32))
    wd = np.ascontiguousarray(np.asarray(L0_ffn_Wd, dtype=np.float32))
    sg = np.ascontiguousarray(np.asarray(L0_ffn_Sg, dtype=np.float32))
    su = np.ascontiguousarray(np.asarray(L0_ffn_Su, dtype=np.float32))
    sd = np.ascontiguousarray(np.asarray(L0_ffn_Sd, dtype=np.float32))

    if _NC_CACHE is None:
        _NC_CACHE = build()
    nc = _NC_CACHE

    in_maps = []
    for c in range(8):
        b, half = c // 2, c % 2
        xb = x[b]
        if half:
            xb = np.ascontiguousarray(np.roll(xb, -1024, axis=0))
        tbase = np.array([[(512 * t + 1024 * half) % 2048 for t in range(4)]],
                         dtype=np.float32)
        in_maps.append(dict(x=xb, wq=wq, wk=wk, wv=wv, wo=wo, gate=gate,
                            gw=gw, wg=wg, wu=wu, wd=wd, sg=sg, su=su, sd=sd,
                            tbase=tbase))
    res = run_bass_kernel_spmd(nc, in_maps, core_ids=list(range(8)))
    out = np.empty((B, S, H), dtype=np.float32)
    for c in range(8):
        b, half = c // 2, c % 2
        out[b, 1024 * half:1024 * half + 1024] = res.results[c]["out"]
    return out


# version 07
# speedup vs baseline: 1.2584x; 1.1784x over previous
import numpy as np
from contextlib import ExitStack

import concourse.bass as bass
import concourse.mybir as mybir
from concourse.bass_utils import run_bass_kernel_spmd

F32 = mybir.dt.float32
BF16 = mybir.dt.bfloat16
AF = mybir.ActivationFunctionType
ALU = mybir.AluOpType
AX = mybir.AxisListType.X

B, S, H = 4, 2048, 2048
NH, HD = 16, 128
NCH = H // 128
TKV, TQ = 2048, 1024
E, TOPK, I = 16, 4, 1536
NI = I // 128
MAGIC = 12582912.0
PI = float(np.pi)
INV2PI = float(1.0 / (2.0 * np.pi))


class Ctr:
    __slots__ = ("h", "v")

    def __init__(self, h):
        self.h = h
        self.v = 0

    def inc(self, ins, n=1):
        ins.then_inc(self.h, n)
        self.v += n
        return self.v


def build(debug=False):
    _sid = {}
    nc = bass.Bass("TRN2", target_bir_lowering=False,
                   dynamic_dma_scratch_size=16384)

    x_d = nc.dram_tensor("x", [TKV, H], F32, kind="ExternalInput").ap()
    wq_d = nc.dram_tensor("wq", [H, H], F32, kind="ExternalInput").ap()
    wk_d = nc.dram_tensor("wk", [H, H], F32, kind="ExternalInput").ap()
    wv_d = nc.dram_tensor("wv", [H, H], F32, kind="ExternalInput").ap()
    wo_d = nc.dram_tensor("wo", [H, H], F32, kind="ExternalInput").ap()
    gate_d = nc.dram_tensor("gate", [1, NH], F32, kind="ExternalInput").ap()
    gw_d = nc.dram_tensor("gw", [H, E], F32, kind="ExternalInput").ap()
    wg_d = nc.dram_tensor("wg", [E, H, I], F32, kind="ExternalInput").ap()
    wu_d = nc.dram_tensor("wu", [E, H, I], F32, kind="ExternalInput").ap()
    wd_d = nc.dram_tensor("wd", [E, I, H], F32, kind="ExternalInput").ap()
    sg_d = nc.dram_tensor("sg", [H, I], F32, kind="ExternalInput").ap()
    su_d = nc.dram_tensor("su", [H, I], F32, kind="ExternalInput").ap()
    sd_d = nc.dram_tensor("sd", [I, H], F32, kind="ExternalInput").ap()
    tb_d = nc.dram_tensor("tbase", [1, 4], F32, kind="ExternalInput").ap()
    out_d = nc.dram_tensor("out", [TQ, H], F32, kind="ExternalOutput").ap()
    if debug:
        dbg = {
            "d_xnT0": nc.dram_tensor("d_xnT0", [128, NCH * 1024], BF16,
                                     kind="ExternalOutput").ap(),
            "d_xnT1": nc.dram_tensor("d_xnT1", [128, NCH * 1024], BF16,
                                     kind="ExternalOutput").ap(),
            "d_comb": nc.dram_tensor("d_comb", [128, 128], F32,
                                     kind="ExternalOutput").ap(),
            "d_combF": nc.dram_tensor("d_combF", [16, 1024], BF16,
                                      kind="ExternalOutput").ap(),
            "d_kbf0": nc.dram_tensor("d_kbf0", [128, 2048], BF16,
                                     kind="ExternalOutput").ap(),
            "d_kbf1": nc.dram_tensor("d_kbf1", [128, 2048], BF16,
                                     kind="ExternalOutput").ap(),
            "d_kfbf0": nc.dram_tensor("d_kfbf0", [128, 2048], BF16,
                                      kind="ExternalOutput").ap(),
            "d_qbf0": nc.dram_tensor("d_qbf0", [128, 1024], BF16,
                                     kind="ExternalOutput").ap(),
            "d_qfbf0": nc.dram_tensor("d_qfbf0", [128, 1024], BF16,
                                      kind="ExternalOutput").ap(),
            "d_vab0": nc.dram_tensor("d_vab0", [128, NCH * 132], BF16,
                                     kind="ExternalOutput").ap(),
            "d_memaug0": nc.dram_tensor("d_memaug0", [128, 132], BF16,
                                        kind="ExternalOutput").ap(),
            "d_memaug1": nc.dram_tensor("d_memaug1", [128, 132], BF16,
                                        kind="ExternalOutput").ap(),
            "d_negM": nc.dram_tensor("d_negM", [128, 1], F32,
                                     kind="ExternalOutput").ap(),
            "d_mscr": nc.dram_tensor("d_mscr", [1, 8], F32,
                                     kind="ExternalOutput").ap(),
            "d_ptil": nc.dram_tensor("d_ptil", [128, NCH * 256], BF16,
                                     kind="ExternalOutput").ap(),
            "d_combined": nc.dram_tensor("d_combined", [128, 8 * 2048], BF16,
                                         kind="ExternalOutput").ap(),
            "d_ySBwo": nc.dram_tensor("d_ySBwo", [128, NCH * 1024], F32,
                                      kind="ExternalOutput").ap(),
            "d_h1sb": nc.dram_tensor("d_h1sb", [128, 12 * 512], BF16,
                                     kind="ExternalOutput").ap(),
        }

    es = ExitStack()
    sb = lambda n, s, d, **kw: es.enter_context(nc.sbuf_tensor(n, s, d, **kw))
    psn = lambda n, s, d: es.enter_context(nc.psum_tensor(n, s, d))
    mksem = lambda n: Ctr(es.enter_context(nc.semaphore(n)))

    s_pe = mksem("s_pe")
    s_act = mksem("s_act")
    s_dve = mksem("s_dve")
    s_pool = mksem("s_pool")
    s_ldx = mksem("s_ldx")
    s_wl = mksem("s_wl")
    s_st = mksem("s_st")
    s_dbg = mksem("s_dbg")
    s_ve = mksem("s_ve")
    s_ae = mksem("s_ae")

    def vedge(v, ins):
        s_ve.inc(ins)
        v.wait_ge(s_ve.h, s_ve.v)

    def aedge(sc, ins):
        s_ae.inc(ins)
        sc.wait_ge(s_ae.h, s_ae.v)

    xnT0 = sb("xnT0", [128, NCH * 1024], BF16)
    combF = sb("combF", [16, 1024], BF16)
    identf = sb("identf", [128, 128], F32)
    iota16 = sb("iota16", [16, 128], F32)
    sel_sb = sb("sel_sb", [16, 128], BF16)

    esW = ExitStack()
    sbW = lambda n, s, d: esW.enter_context(nc.sbuf_tensor(n, s, d))
    comb_all = sbW("comb_all", [128, 8 * 16], F32)
    g_bc = sbW("g_bc", [128, NH], F32)
    g1m_bc = sbW("g1m_bc", [128, NH], F32)
    identb = sbW("identb", [128, 128], BF16)
    onesb128 = sbW("onesb128", [128, 128], BF16)
    ones_row_b = sbW("ones_row_b", [1, 128], BF16)
    ones_col_b = sbW("ones_col_b", [128, 1], BF16)
    invf_tab = sbW("invf_tab", [128, 8], F32)
    tb_bc = sbW("tb_bc", [128, 4], F32)
    cst = sbW("cst", [128, 4], F32)
    C_EPS, C_HALFPI = 0, 1
    gate_sb = sbW("gate_sb", [1, NH], F32)
    gate_bf = sbW("gate_bf", [1, NH], BF16)
    tb_sb = sbW("tb_sb", [1, 4], F32)
    tb_bf = sbW("tb_bf", [1, 4], BF16)
    negM = sbW("negM", [128, 1], F32)
    mscr = sbW("mscr", [1, 8], F32)
    mbf = sbW("mbf", [1, 1], BF16)

    pA = psn("pA", [128, 512], F32)
    pB = psn("pB", [128, 512], F32)
    pC = psn("pC", [128, 512], F32)
    pD = psn("pD", [128, 512], F32)
    pE = psn("pE", [128, 512], F32)
    pF = psn("pF", [128, 512], F32)
    pT1 = psn("pT1", [128, 1024], BF16)
    pT2 = psn("pT2", [128, 1024], BF16)

    bank_free = {}

    def bank_wait(eng, bank):
        ent = bank_free.get(bank)
        if ent is not None:
            ctr, val = ent
            eng.wait_ge(ctr.h, val)

    def bank_consumed(bank, ctr):
        bank_free[bank] = (ctr, ctr.v)

    with nc.Block("init", no_gpsimd_drain=True) as blk:
        @blk.sync
        def _(sy):
            sy.dma_start(out=gate_sb[:], in_=gate_d).then_inc(s_ldx.h, 16)
            sy.dma_start(out=tb_sb[:], in_=tb_d).then_inc(s_ldx.h, 16)
            s_ldx.v += 32

        @blk.vector
        def _(v):
            v.memset(ones_row_b[:], 1.0)
            v.memset(ones_col_b[:], 1.0)
            v.memset(onesb128[:], 1.0)
            v.memset(cst[:, C_EPS:C_EPS + 1], 1e-5)
            ins = v.memset(cst[:, C_HALFPI:C_HALFPI + 1], PI / 2.0)
            s_dve.inc(ins)

        @blk.gpsimd
        def _(g):
            g.wait_ge(s_dve.h, 1)
            g.affine_select(identb[:], onesb128[:], [[1, 128]], ALU.is_equal,
                            0.0, base=0, channel_multiplier=-1)
            g.iota(invf_tab[:], [[128, 8]], base=0, channel_multiplier=1,
                   allow_small_or_imprecise_dtypes=True)
            ins = g.iota(iota16[:], [[0, 128]], base=0, channel_multiplier=1,
                         allow_small_or_imprecise_dtypes=True)
            s_pool.inc(ins)

        @blk.scalar
        def _(sc):
            sc.wait_ge(s_pool.h, 1)
            sc.copy(identf[:], identb[:])
            sc.activation(invf_tab[:], invf_tab[:], AF.Exp,
                          scale=float(-np.log(10000.0) / 1024.0))
            sc.wait_ge(s_ldx.h, 32)
            aedge(sc, sc.activation(gate_sb[:], gate_sb[:], AF.Sigmoid))
            sc.copy(gate_bf[:], gate_sb[:])
            ins = sc.copy(tb_bf[:], tb_sb[:])
            s_act.inc(ins)

        @blk.tensor
        def _(t):
            t.wait_ge(s_act.h, 1)
            t.matmul(pC[:, 0:16], ones_row_b[:], gate_bf[:], start=True, stop=True)
            ins = t.matmul(pC[:, 16:20], ones_row_b[:], tb_bf[:],
                           start=True, stop=True)
            s_pe.inc(ins)

        @blk.scalar
        def _(sc):
            sc.wait_ge(s_pe.h, 1)
            sc.copy(g_bc[:], pC[:, 0:16])
            ins = sc.copy(tb_bc[:], pC[:, 16:20])
            s_act.inc(ins)

        @blk.vector
        def _(v):
            v.wait_ge(s_act.h, 2)
            ins = v.tensor_scalar(g1m_bc[:], g_bc[:], -1.0, 1.0, ALU.mult, ALU.add)
            s_dve.inc(ins)
        bank_consumed("pC", s_act)

    esCm = ExitStack()
    combined = esCm.enter_context(
        nc.sbuf_tensor("combined", [128, 8 * 2048], BF16))
    esX = ExitStack()
    xnT1 = esX.enter_context(nc.sbuf_tensor("xnT1", [128, NCH * 1024], BF16))
    iota512 = esX.enter_context(nc.sbuf_tensor("iota512", [128, 512], F32))
    gw_sb = esX.enter_context(nc.sbuf_tensor("gw_sb", [128, NCH * E], F32))

    esA = ExitStack()
    sbA = lambda n, s, d: esA.enter_context(nc.sbuf_tensor(n, s, d))
    xt = [sbA(f"xt{i}", [128, H], F32) for i in range(2)]
    xnf = [sbA(f"xnf{i}", [128, H], F32) for i in range(2)]
    xnb = [sbA(f"xnb{i}", [128, H], BF16) for i in range(2)]
    sqscr = sbA("sqscr", [128, H], F32)
    xnTf = sbA("xnTf", [128, NCH * 128], F32)
    lncol = sbA("lncol", [128, 8], F32)
    CL_SUM, CL_NEGMEAN, CL_M2, CL_VAR, CL_RSTD = 0, 1, 2, 3, 4
    CL_SQP = [5, 6]
    Ltile = sbA("Ltile", [128, 16], F32)
    topv = sbA("topv", [128, 8], F32)
    texp = sbA("texp", [128, 4], F32)
    probs = sbA("probs", [128, 4], F32)
    pcc = sbA("pcc", [128, 2], F32)
    maskt = sbA("maskt", [128, 16], F32)
    combsc = [sbA(f"combsc{i}", [128, 16], F32) for i in range(2)]

    with nc.Block("ginit", no_gpsimd_drain=True) as blk:
        @blk.sync
        def _(sy):
            ins = sy.dma_start(out=gw_sb[:],
                               in_=gw_d.rearrange("(c p) f -> p c f", p=128))
            s_ldx.inc(ins, 16)

        @blk.gpsimd
        def _(g):
            ins = g.iota(iota512[:], [[1, 512]], base=0, channel_multiplier=0,
                         allow_small_or_imprecise_dtypes=True)
            s_pool.inc(ins)

    xt_free = [(s_dve, 0), (s_dve, 0)]
    xn_free = [(s_pe, 0), (s_pe, 0)]
    gw_ready = s_ldx.v
    iota_ready = s_pool.v

    for tt in range(16):
        pp = tt % 2
        dst_xnT = xnT0 if tt < 8 else xnT1
        _sid[f"A{tt}"] = nc.enter_named_scope(f"A{tt}", False)[0]
        toff = 128 * (tt % 8)
        with nc.Block(f"ln{tt}", no_gpsimd_drain=True) as blk:
            @blk.sync
            def _(sy):
                ctr, val = xt_free[pp]
                sy.wait_ge(ctr.h, val)
                ins = sy.dma_start(out=xt[pp][:],
                                   in_=x_d[128 * tt:128 * (tt + 1), :])
                s_ldx.inc(ins, 16)
            x_ready = s_ldx.v

            @blk.scalar
            def _(sc, pp=pp):
                sc.wait_ge(s_ldx.h, x_ready)
                sq = CL_SQP[pp]
                ins = sc.activation(sqscr[:], xt[pp][:], AF.Square,
                                    accum_out=lncol[:, sq:sq + 1])
                s_act.inc(ins)
            sq_ready = s_act.v

            @blk.vector
            def _(v):
                v.wait_ge(s_ldx.h, x_ready)
                vedge(v, v.reduce_sum(lncol[:, CL_SUM:CL_SUM + 1], xt[pp][:],
                                      axis=AX))
                vedge(v, v.tensor_scalar(lncol[:, CL_NEGMEAN:CL_NEGMEAN + 1],
                                         lncol[:, CL_SUM:CL_SUM + 1],
                                         float(-1.0 / H), None, ALU.mult))
                vedge(v, v.tensor_tensor(lncol[:, CL_M2:CL_M2 + 1],
                                         lncol[:, CL_NEGMEAN:CL_NEGMEAN + 1],
                                         lncol[:, CL_NEGMEAN:CL_NEGMEAN + 1],
                                         ALU.mult))
                v.wait_ge(s_act.h, sq_ready)
                sq = CL_SQP[pp]
                ins = v.scalar_tensor_tensor(lncol[:, CL_VAR:CL_VAR + 1],
                                             lncol[:, sq:sq + 1],
                                             float(1.0 / H),
                                             lncol[:, CL_M2:CL_M2 + 1],
                                             ALU.mult, ALU.subtract)
                s_dve.inc(ins)
            var_ready = s_dve.v

            @blk.scalar
            def _(sc):
                sc.wait_ge(s_dve.h, var_ready)
                aedge(sc, sc.activation(lncol[:, CL_VAR:CL_VAR + 1],
                                        lncol[:, CL_VAR:CL_VAR + 1], AF.Ln,
                                        bias=cst[:, C_EPS:C_EPS + 1]))
                ins = sc.activation(lncol[:, CL_RSTD:CL_RSTD + 1],
                                    lncol[:, CL_VAR:CL_VAR + 1], AF.Exp,
                                    scale=-0.5)
                s_act.inc(ins)
            rstd_ready = s_act.v

            @blk.vector
            def _(v):
                v.wait_ge(s_act.h, rstd_ready)
                ctr, val = xn_free[pp]
                v.wait_ge(ctr.h, val)
                v.tensor_scalar(xnf[pp][:], xt[pp][:],
                                lncol[:, CL_NEGMEAN:CL_NEGMEAN + 1],
                                lncol[:, CL_RSTD:CL_RSTD + 1],
                                ALU.add, ALU.mult)
                ins = v.tensor_copy(xnb[pp][:], xnf[pp][:])
                s_dve.inc(ins)
            xn_ready = s_dve.v
            xt_free[pp] = (s_dve, s_dve.v)

            for g4 in range(4):
                bank = [pT1, pT2][g4 % 2]
                bname = ["pT1", "pT2"][g4 % 2]

                @blk.tensor
                def _(t, g4=g4, bank=bank, bname=bname):
                    t.wait_ge(s_dve.h, xn_ready)
                    bank_wait(t, bname)
                    last = None
                    for j in range(4):
                        ic = 4 * g4 + j
                        last = t.transpose(bank[:, 128 * j:128 * (j + 1)],
                                           xnb[pp][:, 128 * ic:128 * (ic + 1)],
                                           identb[:])
                    s_pe.inc(last)
                grp_done = s_pe.v

                @blk.vector
                def _(v, g4=g4, bank=bank, grp_done=grp_done):
                    v.wait_ge(s_pe.h, grp_done)
                    dst3 = dst_xnT[:].rearrange("p (c f) -> p c f", f=1024)
                    ins = v.tensor_copy(dst3[:, 4 * g4:4 * g4 + 4,
                                             toff:toff + 128],
                                        bank[:, 0:512].rearrange(
                                            "p (c f) -> p c f", f=128))
                    s_dve.inc(ins)
                bank_consumed(bname, s_dve)

            if tt < 8:
                for g4 in range(4):
                    bank = [pD, pE][g4 % 2]
                    bname = ["pD", "pE"][g4 % 2]

                    @blk.tensor
                    def _(t, g4=g4, bank=bank, bname=bname):
                        t.wait_ge(s_dve.h, xn_ready)
                        bank_wait(t, bname)
                        last = None
                        for j in range(4):
                            ic = 4 * g4 + j
                            last = t.transpose(bank[:, 128 * j:128 * (j + 1)],
                                               xnf[pp][:, 128 * ic:128 * (ic + 1)],
                                               identf[:])
                        s_pe.inc(last)
                    grp_done = s_pe.v

                    @blk.vector
                    def _(v, g4=g4, bank=bank, grp_done=grp_done):
                        v.wait_ge(s_pe.h, grp_done)
                        ins = v.tensor_copy(xnTf[:, 512 * g4:512 * (g4 + 1)],
                                            bank[:])
                        s_dve.inc(ins)
                    bank_consumed(bname, s_dve)
                xnTf_ready = s_dve.v
                xn_free[pp] = (s_dve, s_dve.v)

                @blk.tensor
                def _(t):
                    t.wait_ge(s_dve.h, xnTf_ready)
                    t.wait_ge(s_ldx.h, gw_ready)
                    bank_wait(t, "pC")
                    last = None
                    for ic in range(NCH):
                        last = t.matmul(pC[:, 0:16],
                                        xnTf[:, 128 * ic:128 * (ic + 1)],
                                        gw_sb[:, 16 * ic:16 * (ic + 1)],
                                        start=(ic == 0), stop=(ic == NCH - 1))
                    s_pe.inc(last)
                logits_done = s_pe.v

                @blk.vector
                def _(v):
                    v.wait_ge(s_pe.h, logits_done)
                    ins = v.tensor_copy(Ltile[:], pC[:, 0:16])
                    s_dve.inc(ins)
                bank_consumed("pC", s_dve)

                @blk.vector
                def _(v):
                    v.wait_ge(s_dve.h, s_dve.v)
                    vedge(v, v.max(out=topv[:], in_=Ltile[:]))
                    ins = v.tensor_scalar(texp[:], topv[:, 0:4], topv[:, 0:1],
                                          None, ALU.subtract)
                    s_dve.inc(ins)
                texp_ready = s_dve.v

                @blk.scalar
                def _(sc):
                    sc.wait_ge(s_dve.h, texp_ready)
                    ins = sc.activation(texp[:], texp[:], AF.Exp)
                    s_act.inc(ins)
                exp_ready = s_act.v

                @blk.vector
                def _(v):
                    v.wait_ge(s_act.h, exp_ready)
                    vedge(v, v.reduce_sum(pcc[:, 0:1], texp[:], axis=AX))
                    vedge(v, v.reciprocal(pcc[:, 1:2], pcc[:, 0:1]))
                    vedge(v, v.tensor_scalar(probs[:], texp[:], pcc[:, 1:2],
                                             None, ALU.mult))
                    v.memset(combsc[1][:], 0.0)
                    for r in range(TOPK):
                        vedge(v, v.tensor_scalar(maskt[:], Ltile[:],
                                                 topv[:, r:r + 1],
                                                 None, ALU.is_equal))
                        vedge(v, v.reduce_sum(pcc[:, 0:1], maskt[:], axis=AX))
                        vedge(v, v.reciprocal(pcc[:, 1:2], pcc[:, 0:1]))
                        vedge(v, v.tensor_tensor(pcc[:, 0:1], probs[:, r:r + 1],
                                                 pcc[:, 1:2], ALU.mult))
                        vedge(v, v.scalar_tensor_tensor(combsc[r % 2][:],
                                                        maskt[:],
                                                        pcc[:, 0:1],
                                                        combsc[1 - r % 2][:],
                                                        ALU.mult, ALU.add))
                    ins = v.tensor_copy(comb_all[:, 16 * tt:16 * (tt + 1)],
                                        combsc[(TOPK - 1) % 2][:])
                    s_dve.inc(ins)
                comb_ready = s_dve.v

                @blk.tensor
                def _(t):
                    t.wait_ge(s_dve.h, comb_ready)
                    bank_wait(t, "pD")
                    ins = t.transpose(pD[0:16, 0:128],
                                      comb_all[:, 16 * tt:16 * (tt + 1)],
                                      identf[:])
                    s_pe.inc(ins)
                combT_done = s_pe.v

                @blk.vector
                def _(v):
                    v.wait_ge(s_pe.h, combT_done)
                    ins = v.tensor_copy(combF[0:16, 128 * tt:128 * (tt + 1)],
                                        pD[0:16, 0:128])
                    s_dve.inc(ins)
                bank_consumed("pD", s_dve)
            else:
                xn_free[pp] = (s_dve, s_dve.v)
        nc.leave_named_scope(f"A{tt}", _sid[f"A{tt}"], False)

    if debug:
        with nc.Block("dbgA", no_gpsimd_drain=True) as blk:
            @blk.sync
            def _(sy):
                sy.wait_ge(s_dve.h, s_dve.v)
                sy.wait_ge(s_act.h, s_act.v)
                sy.dma_start(out=dbg["d_xnT0"], in_=xnT0[:]).then_inc(s_dbg.h, 16)
                sy.dma_start(out=dbg["d_xnT1"], in_=xnT1[:]).then_inc(s_dbg.h, 16)
                sy.dma_start(out=dbg["d_comb"], in_=comb_all[:]).then_inc(s_dbg.h, 16)
                sy.dma_start(out=dbg["d_combF"], in_=combF[:]).then_inc(s_dbg.h, 16)
                s_dbg.v += 64

    esA.close()

    esB = ExitStack()
    sbB = lambda n, s, d: esB.enter_context(nc.sbuf_tensor(n, s, d))
    wslab = [sbB(f"wslab{i}", [128, 2048], BF16) for i in range(2)]
    WTq = sbB("WTq", [128, NCH * 256], BF16)
    WTk = sbB("WTk", [128, NCH * 256], BF16)
    WTv = sbB("WTv", [128, NCH * 256], BF16)
    WT = {"q": WTq, "k": WTk, "v": WTv}
    ropeo = [sbB(f"ropeo{i}", [128, 512], F32) for i in range(2)]
    eluscr = [sbB(f"eluscr{i}", [128, 512], F32) for i in range(2)]
    ang = sbB("ang", [128, 512], F32)
    tscr = sbB("tscr", [128, 512], F32)
    tscr2 = sbB("tscr2", [128, 512], F32)
    sinb = sbB("sinb", [128, 512], F32)
    cosb = sbB("cosb", [128, 512], F32)
    k_bf = [sbB(f"k_bf{i}", [128, 2048], BF16) for i in range(2)]
    kf_bf = [sbB(f"kf_bf{i}", [128, 2048], BF16) for i in range(2)]
    q_bf = [sbB(f"q_bf{i}", [128, 1024], BF16) for i in range(2)]
    qf_bf = [sbB(f"qf_bf{i}", [128, 1024], BF16) for i in range(2)]
    v_stage = sbB("v_stage", [128, 512], BF16)
    sq3 = sbB("sq3", [128, 3 * 512], BF16)
    vab = [sbB(f"vab{i}", [128, NCH * 132], BF16) for i in range(2)]
    kftok = [sbB(f"kftok{i}", [128, NCH * 128], BF16) for i in range(2)]
    ptil = sbB("ptil", [128, NCH * 256], BF16)
    mem_aug = [sbB(f"mem_aug{i}", [128, 132], BF16) for i in range(2)]
    cscr = sbB("cscr", [128, 128], F32)
    rcls = sbB("rcls", [128, 4], F32)

    slab_free = [(s_pe, 0)] * 2
    vst_free = (s_pe, 0)
    nload = 0
    nproj = 0
    ntr = 0
    ptil_free_pe = 0
    mbf_free_pe = 0

    def tbank():
        return [(pT1, "pT1"), (pT2, "pT2")][ntr % 2]

    def qslice(ic, tc):
        src = xnT0 if tc < 2 else xnT1
        return src[:, 1024 * ic + 512 * (tc % 2):1024 * ic + 512 * (tc % 2) + 512]

    W_SRC = {"q": wq_d, "k": wk_d, "v": wv_d}

    for p in range(8):
        _sid[f"P{p}"] = nc.enter_named_scope(f"P{p}", False)[0]
        with nc.Block(f"pair{p}", no_gpsimd_drain=True) as blk:
            for wi, w in enumerate(("q", "k", "v")):
                for hh in range(2):
                    li = nload
                    nload += 1
                    slot = li % 2
                    hd = p + 8 * hh

                    @blk.gpsimd
                    def _(g, li=li, slot=slot, hd=hd, w=w):
                        ctr, val = slab_free[slot]
                        g.wait_ge(ctr.h, val)
                        ins = g.dma_start(out=wslab[slot][:],
                                          in_=W_SRC[w][128 * hd:128 * (hd + 1), :])
                        s_wl.inc(ins, 16)
                    ld_done = s_wl.v

                    for g8 in range(2):
                        bank, bname = tbank()
                        ntr += 1

                        @blk.tensor
                        def _(t, slot=slot, g8=g8, bank=bank, bname=bname,
                              ld_done=ld_done):
                            t.wait_ge(s_wl.h, ld_done)
                            bank_wait(t, bname)
                            last = None
                            for j in range(8):
                                ic = 8 * g8 + j
                                last = t.transpose(
                                    bank[:, 128 * j:128 * (j + 1)],
                                    wslab[slot][:, 128 * ic:128 * (ic + 1)],
                                    identb[:])
                            s_pe.inc(last)
                        tr_done = s_pe.v

                        @blk.vector
                        def _(v, w=w, hh=hh, g8=g8, bank=bank,
                              tr_done=tr_done):
                            v.wait_ge(s_pe.h, tr_done)
                            dst3 = WT[w][:].rearrange("p (c f) -> p c f", f=256)
                            ins = v.tensor_copy(
                                dst3[:, 8 * g8:8 * g8 + 8,
                                     128 * hh:128 * hh + 128],
                                bank[:].rearrange("p (c f) -> p c f", f=128))
                            s_dve.inc(ins)
                        bank_consumed(bname, s_dve)
                    slab_free[slot] = (s_pe, s_pe.v)

            wt_ready_dve = s_dve.v

            @blk.vector
            def _(v):
                v3a = vab[0][:].rearrange("p (c w) -> p c w", w=132)
                v3b = vab[1][:].rearrange("p (c w) -> p c w", w=132)
                v.memset(v3a[:, :, 128:129], 1.0)
                ins = v.memset(v3b[:, :, 128:129], 1.0)
                s_dve.inc(ins)

            kbf_ready = [0, 0, 0, 0]
            qbf_ready = [0, 0]

            for tc in range(4):
                d0 = s_dve.v
                a0 = s_act.v

                @blk.vector
                def _(v, tc=tc, p=p):
                    vedge(v, v.tensor_scalar(ang[:], iota512[:],
                                             tb_bc[:, tc:tc + 1],
                                             invf_tab[:, p:p + 1],
                                             ALU.add, ALU.mult))
                    vedge(v, v.tensor_scalar(tscr[:], ang[:], INV2PI, None,
                                             ALU.mult))
                    vedge(v, v.tensor_scalar(tscr[:], tscr[:], MAGIC, MAGIC,
                                             ALU.add, ALU.subtract))
                    v.scalar_tensor_tensor(tscr[:], tscr[:], float(-2 * PI),
                                           ang[:], ALU.mult, ALU.add)
                    vedge(v, v.tensor_scalar(tscr2[:], ang[:], INV2PI, 0.25,
                                             ALU.mult, ALU.add))
                    vedge(v, v.tensor_scalar(tscr2[:], tscr2[:], MAGIC, MAGIC,
                                             ALU.add, ALU.subtract))
                    ins = v.scalar_tensor_tensor(tscr2[:], tscr2[:],
                                                 float(-2 * PI), ang[:],
                                                 ALU.mult, ALU.add)
                    s_dve.inc(ins)
                assert s_dve.v == d0 + 1

                @blk.scalar
                def _(sc, tc=tc, d0=d0):
                    sc.wait_ge(s_dve.h, d0 + 1)
                    sc.activation(sinb[:], tscr[:], AF.Sin)
                    ins = sc.activation(cosb[:], tscr2[:], AF.Sin,
                                        bias=cst[:, C_HALFPI:C_HALFPI + 1])
                    s_act.inc(ins)
                    sc.wait_ge(s_dve.h, d0 + 7)
                    sc.activation(eluscr[0][:], eluscr[0][:], AF.Exp)
                    ins = sc.activation(eluscr[1][:], eluscr[1][:], AF.Exp)
                    s_act.inc(ins)
                    if tc < 2:
                        sc.wait_ge(s_dve.h, d0 + 10)
                        sc.activation(eluscr[0][:], eluscr[0][:], AF.Exp)
                        ins = sc.activation(eluscr[1][:], eluscr[1][:], AF.Exp)
                        s_act.inc(ins)
                assert s_act.v == a0 + (3 if tc < 2 else 2)

                for hh in range(2):
                    bank, bname = [(pA, "pA"), (pB, "pB")][nproj % 2]
                    nproj += 1

                    @blk.tensor
                    def _(t, hh=hh, tc=tc, bank=bank, bname=bname,
                          wt_ready_dve=wt_ready_dve):
                        if tc == 0 and hh == 0:
                            t.wait_ge(s_dve.h, wt_ready_dve)
                        bank_wait(t, bname)
                        last = None
                        for ic in range(NCH):
                            last = t.matmul(
                                bank[:],
                                WTv[:, 256 * ic + 128 * hh:256 * ic + 128 * hh + 128],
                                qslice(ic, tc),
                                start=(ic == 0), stop=(ic == NCH - 1))
                        s_pe.inc(last)
                    mm_done = s_pe.v

                    @blk.vector
                    def _(v, bank=bank, mm_done=mm_done):
                        ctr, val = vst_free
                        v.wait_ge(ctr.h, val)
                        v.wait_ge(s_pe.h, mm_done)
                        ins = v.tensor_copy(v_stage[:], bank[:])
                        s_dve.inc(ins)
                    bank_consumed(bname, s_dve)
                    vst_dve = s_dve.v

                    bank2, bname2 = tbank()
                    ntr += 1

                    @blk.tensor
                    def _(t, bank2=bank2, bname2=bname2, vst_dve=vst_dve):
                        t.wait_ge(s_dve.h, vst_dve)
                        bank_wait(t, bname2)
                        last = None
                        for j in range(4):
                            last = t.transpose(bank2[:, 128 * j:128 * (j + 1)],
                                               v_stage[:, 128 * j:128 * (j + 1)],
                                               identb[:])
                        s_pe.inc(last)
                    vtr_done = s_pe.v
                    vst_free = (s_pe, s_pe.v)

                    @blk.vector
                    def _(v, hh=hh, tc=tc, bank2=bank2, vtr_done=vtr_done):
                        v.wait_ge(s_pe.h, vtr_done)
                        dst3 = vab[hh][:].rearrange("p (c w) -> p c w", w=132)
                        ins = v.tensor_copy(dst3[:, 4 * tc:4 * tc + 4, 0:128],
                                            bank2[:, 0:512].rearrange(
                                                "p (c f) -> p c f", f=128))
                        s_dve.inc(ins)
                    bank_consumed(bname2, s_dve)
                assert s_dve.v == d0 + 5

                kmm = []
                for hh in range(2):
                    bank, bname = [(pA, "pA"), (pB, "pB")][nproj % 2]
                    nproj += 1

                    @blk.tensor
                    def _(t, hh=hh, tc=tc, bank=bank, bname=bname):
                        bank_wait(t, bname)
                        last = None
                        for ic in range(NCH):
                            last = t.matmul(
                                bank[:],
                                WTk[:, 256 * ic + 128 * hh:256 * ic + 128 * hh + 128],
                                qslice(ic, tc),
                                start=(ic == 0), stop=(ic == NCH - 1))
                        s_pe.inc(last)
                    kmm.append((bank, bname, s_pe.v))

                @blk.vector
                def _(v, tc=tc, kmm=kmm, a0=a0):
                    v.wait_ge(s_pe.h, kmm[1][2])
                    v.wait_ge(s_act.h, a0 + 1)
                    kb0, kb1 = kmm[0][0], kmm[1][0]
                    v.tensor_tensor(tscr[:], kb0[:], cosb[:], ALU.mult)
                    v.tensor_tensor(tscr2[:], kb1[:], sinb[:], ALU.mult)
                    v.tensor_tensor(ropeo[0][:], tscr[:], tscr2[:],
                                    ALU.subtract)
                    v.tensor_tensor(tscr[:], kb1[:], cosb[:], ALU.mult)
                    v.tensor_tensor(tscr2[:], kb0[:], sinb[:], ALU.mult)
                    ins = v.tensor_tensor(ropeo[1][:], tscr[:], tscr2[:],
                                          ALU.add)
                    s_dve.inc(ins)
                    for hh in range(2):
                        v.tensor_copy(k_bf[hh][:, 512 * tc:512 * (tc + 1)],
                                      ropeo[hh][:])
                    v.tensor_scalar(eluscr[0][:], ropeo[0][:], 0.0, None,
                                    ALU.min)
                    v.tensor_scalar(eluscr[1][:], ropeo[1][:], 0.0, None,
                                    ALU.min)
                    v.tensor_scalar(tscr[:], ropeo[0][:], 0.0, None, ALU.max)
                    ins = v.tensor_scalar(tscr2[:], ropeo[1][:], 0.0, None,
                                          ALU.max)
                    s_dve.inc(ins)
                for bank, bname, _v in kmm:
                    bank_free[bname] = (s_dve, d0 + 6)

                @blk.vector
                def _(v, tc=tc, a0=a0):
                    v.wait_ge(s_act.h, a0 + 2)
                    v.tensor_tensor(kf_bf[0][:, 512 * tc:512 * (tc + 1)],
                                    eluscr[0][:], tscr[:], ALU.add)
                    ins = v.tensor_tensor(kf_bf[1][:, 512 * tc:512 * (tc + 1)],
                                          eluscr[1][:], tscr2[:], ALU.add)
                    s_dve.inc(ins)
                kbf_ready[tc] = s_dve.v
                assert s_dve.v == d0 + 8

                if tc < 2:
                    qmm = []
                    for hh in range(2):
                        bank, bname = [(pA, "pA"), (pB, "pB")][nproj % 2]
                        nproj += 1

                        @blk.tensor
                        def _(t, hh=hh, tc=tc, bank=bank, bname=bname):
                            bank_wait(t, bname)
                            last = None
                            for ic in range(NCH):
                                last = t.matmul(
                                    bank[:],
                                    WTq[:, 256 * ic + 128 * hh:256 * ic + 128 * hh + 128],
                                    qslice(ic, tc),
                                    start=(ic == 0), stop=(ic == NCH - 1))
                            s_pe.inc(last)
                        qmm.append((bank, bname, s_pe.v))

                    @blk.vector
                    def _(v, tc=tc, qmm=qmm, a0=a0):
                        v.wait_ge(s_pe.h, qmm[1][2])
                        qb0, qb1 = qmm[0][0], qmm[1][0]
                        v.tensor_tensor(tscr[:], qb0[:], cosb[:], ALU.mult)
                        v.tensor_tensor(tscr2[:], qb1[:], sinb[:],
                                        ALU.mult)
                        v.tensor_tensor(ropeo[0][:], tscr[:], tscr2[:],
                                        ALU.subtract)
                        v.tensor_tensor(tscr[:], qb1[:], cosb[:], ALU.mult)
                        v.tensor_tensor(tscr2[:], qb0[:], sinb[:],
                                        ALU.mult)
                        ins = v.tensor_tensor(ropeo[1][:], tscr[:], tscr2[:],
                                              ALU.add)
                        s_dve.inc(ins)
                        for hh in range(2):
                            v.tensor_copy(q_bf[hh][:, 512 * tc:512 * (tc + 1)],
                                          ropeo[hh][:])
                        v.tensor_scalar(eluscr[0][:], ropeo[0][:], 0.0, None,
                                        ALU.min)
                        v.tensor_scalar(eluscr[1][:], ropeo[1][:], 0.0, None,
                                        ALU.min)
                        v.tensor_scalar(tscr[:], ropeo[0][:], 0.0, None,
                                        ALU.max)
                        ins = v.tensor_scalar(tscr2[:], ropeo[1][:], 0.0,
                                              None, ALU.max)
                        s_dve.inc(ins)
                    for bank, bname, _v in qmm:
                        bank_free[bname] = (s_dve, d0 + 9)

                    @blk.vector
                    def _(v, tc=tc, a0=a0):
                        v.wait_ge(s_act.h, a0 + 3)
                        v.tensor_tensor(qf_bf[0][:, 512 * tc:512 * (tc + 1)],
                                        eluscr[0][:], tscr[:], ALU.add)
                        ins = v.tensor_tensor(
                            qf_bf[1][:, 512 * tc:512 * (tc + 1)],
                            eluscr[1][:], tscr2[:], ALU.add)
                        s_dve.inc(ins)
                    qbf_ready[tc] = s_dve.v
                    assert s_dve.v == d0 + 11

            for hh in range(2):
                hglob = p + 8 * hh

                for g8 in range(2):
                    bank, bname = tbank()
                    ntr += 1

                    @blk.tensor
                    def _(t, hh=hh, g8=g8, bank=bank, bname=bname):
                        t.wait_ge(s_dve.h, kbf_ready[3])
                        bank_wait(t, bname)
                        last = None
                        for j in range(8):
                            sc_i = 8 * g8 + j
                            last = t.transpose(
                                bank[:, 128 * j:128 * (j + 1)],
                                kf_bf[hh][:, 128 * sc_i:128 * (sc_i + 1)],
                                identb[:])
                        s_pe.inc(last)
                    tr_done = s_pe.v

                    @blk.vector
                    def _(v, hh=hh, g8=g8, bank=bank, tr_done=tr_done):
                        v.wait_ge(s_pe.h, tr_done)
                        ins = v.tensor_copy(
                            kftok[hh][:, 1024 * g8:1024 * (g8 + 1)], bank[:])
                        s_dve.inc(ins)
                    bank_consumed(bname, s_dve)
                kftok_dve = s_dve.v

                @blk.tensor
                def _(t, hh=hh, kftok_dve=kftok_dve):
                    t.wait_ge(s_dve.h, kftok_dve)
                    bank_wait(t, "pC")
                    last = None
                    for sc_i in range(NCH):
                        last = t.matmul(
                            pC[:, 0:129],
                            kftok[hh][:, 128 * sc_i:128 * (sc_i + 1)],
                            vab[hh][:, 132 * sc_i:132 * sc_i + 129],
                            start=(sc_i == 0), stop=(sc_i == NCH - 1))
                    s_pe.inc(last)
                mem_done = s_pe.v

                @blk.vector
                def _(v, hh=hh, mem_done=mem_done):
                    v.wait_ge(s_pe.h, mem_done)
                    ins = v.tensor_copy(mem_aug[hh][:, 0:129], pC[:, 0:129])
                    s_dve.inc(ins)
                bank_consumed("pC", s_dve)
                memaug_act = s_dve.v

                srcs = [(k_bf[hh], 0), (k_bf[hh], 1), (k_bf[hh], 2),
                        (k_bf[hh], 3), (q_bf[hh], 0), (q_bf[hh], 1)]
                nbank = [(pD, "pD"), (pE, "pE"), (pF, "pF")]
                for rnd in range(2):
                    @blk.vector
                    def _(v, rnd=rnd):
                        last = None
                        for i in range(3):
                            srcb, tci = srcs[3 * rnd + i]
                            last = v.tensor_tensor(
                                sq3[:, 512 * i:512 * (i + 1)],
                                srcb[:, 512 * tci:512 * (tci + 1)],
                                srcb[:, 512 * tci:512 * (tci + 1)], ALU.mult)
                        s_dve.inc(last)
                    sq_dve = s_dve.v

                    @blk.tensor
                    def _(t, rnd=rnd, sq_dve=sq_dve):
                        t.wait_ge(s_dve.h, sq_dve)
                        last = None
                        for i in range(3):
                            bank, bn = nbank[i]
                            if rnd == 0 and hh == 0:
                                bank_wait(t, bn)
                            last = t.matmul(bank[0:1, 0:512], ones_col_b[:],
                                            sq3[:, 512 * i:512 * (i + 1)],
                                            start=True, stop=True)
                            s_pe.inc(last)
                    nrm_pe = s_pe.v

                    @blk.vector
                    def _(v, rnd=rnd, nrm_pe=nrm_pe):
                        v.wait_ge(s_pe.h, nrm_pe)
                        last = None
                        for i in range(3):
                            bank, bn = nbank[i]
                            last = v.reduce_max(
                                mscr[0:1, 3 * rnd + i:3 * rnd + i + 1],
                                bank[0:1, 0:512], axis=AX)
                        s_dve.inc(last)
                for bn in ("pD", "pE", "pF"):
                    bank_consumed(bn, s_dve)

                @blk.vector
                def _(v):
                    v.wait_ge(s_dve.h, s_dve.v)
                    vedge(v, v.reduce_max(mscr[0:1, 6:7], mscr[0:1, 0:4],
                                          axis=AX))
                    vedge(v, v.reduce_max(mscr[0:1, 7:8], mscr[0:1, 4:6],
                                          axis=AX))
                    ins = v.tensor_tensor(mscr[0:1, 6:7], mscr[0:1, 6:7],
                                          mscr[0:1, 7:8], ALU.mult)
                    s_dve.inc(ins)
                bnd_dve = s_dve.v

                @blk.scalar
                def _(sc, bnd_dve=bnd_dve):
                    sc.wait_ge(s_dve.h, bnd_dve)
                    aedge(sc, sc.activation(mscr[0:1, 6:7], mscr[0:1, 6:7],
                                            AF.Ln,
                                            scale=float(1.0 / 128.0)))
                    ins = sc.activation(mscr[0:1, 6:7], mscr[0:1, 6:7], AF.Exp,
                                        scale=0.5)
                    s_act.inc(ins)
                bnd_act = s_act.v

                @blk.vector
                def _(v, bnd_act=bnd_act, mbf_free_pe=mbf_free_pe):
                    v.wait_ge(s_act.h, bnd_act)
                    if mbf_free_pe:
                        v.wait_ge(s_pe.h, mbf_free_pe)
                    ins = v.tensor_scalar(mbf[0:1, 0:1], mscr[0:1, 6:7],
                                          -1.02, -0.3, ALU.mult, ALU.add)
                    s_dve.inc(ins)
                mbf_dve = s_dve.v

                @blk.tensor
                def _(t, mbf_dve=mbf_dve):
                    t.wait_ge(s_dve.h, mbf_dve)
                    bank_wait(t, "pC")
                    ins = t.matmul(pC[:, 0:1], ones_row_b[:], mbf[0:1, 0:1],
                                   start=True, stop=True)
                    s_pe.inc(ins)
                mb_done = s_pe.v
                mbf_free_pe = mb_done

                @blk.vector
                def _(v, mb_done=mb_done):
                    v.wait_ge(s_pe.h, mb_done)
                    ins = v.tensor_copy(negM[:], pC[:, 0:1])
                    s_dve.inc(ins)
                bank_consumed("pC", s_dve)
                negM_dve = s_dve.v

                pe0 = s_pe.v
                act0 = s_act.v
                dve0 = s_dve.v
                for qg in range(4):
                    @blk.tensor
                    def _(t, hh=hh, qg=qg, negM_dve=negM_dve,
                          memaug_act=memaug_act, pe0=pe0, act0=act0,
                          dve0=dve0):
                        for sc_i in range(NCH):
                            idx = 16 * qg + sc_i
                            bank = [pA, pB][sc_i % 2]
                            if idx == 0:
                                t.wait_ge(s_dve.h, max(kbf_ready[3],
                                                       qbf_ready[1],
                                                       negM_dve))
                                bank_wait(t, "pA")
                            elif idx == 1:
                                bank_wait(t, "pB")
                            else:
                                t.wait_ge(s_act.h, act0 + idx - 1)
                            ins = t.matmul(
                                bank[:, 0:256],
                                k_bf[hh][:, 128 * sc_i:128 * (sc_i + 1)],
                                q_bf[hh][:, 256 * qg:256 * (qg + 1)],
                                start=True, stop=True)
                            s_pe.inc(ins)
                        t.wait_ge(s_act.h, act0 + 16 * (qg + 1))
                        for qsub in range(2):
                            qt = 2 * qg + qsub
                            bankPV, bnPV = [(pD, "pD"), (pE, "pE")][qsub]
                            bankMO, bnMO = [(pF, "pF"), (pC, "pC")][qsub]
                            if qg == 0 and hh == 0:
                                bank_wait(t, bnPV)
                            elif qg > 0:
                                t.wait_ge(s_dve.h, dve0 + 2 * (qg - 1)
                                          + qsub + 1)
                            last = None
                            for sc_i in range(NCH):
                                last = t.matmul(
                                    bankPV[:, 0:129],
                                    ptil[:, 256 * sc_i + 128 * qsub:
                                         256 * sc_i + 128 * qsub + 128],
                                    vab[hh][:, 132 * sc_i:132 * sc_i + 129],
                                    start=(sc_i == 0), stop=(sc_i == NCH - 1))
                            s_pe.inc(last)
                            if qg == 0 and hh == 0:
                                bank_wait(t, bnMO)
                            if qg == 0 and qsub == 0:
                                t.wait_ge(s_dve.h, memaug_act)
                            ins = t.matmul(bankMO[:, 0:129],
                                           qf_bf[hh][:, 128 * qt:128 * (qt + 1)],
                                           mem_aug[hh][:, 0:129],
                                           start=True, stop=True)
                            s_pe.inc(ins)

                @blk.scalar
                def _(sc, hh=hh, pe0=pe0):
                    for qg in range(4):
                        for sc_i in range(NCH):
                            sc.wait_ge(s_pe.h, pe0 + 20 * qg + sc_i + 1)
                            bank = [pA, pB][sc_i % 2]
                            ins = sc.activation(
                                ptil[:, 256 * sc_i:256 * (sc_i + 1)],
                                bank[:, 0:256], AF.Exp, bias=negM[:],
                                scale=float(1.0 / np.sqrt(128.0)))
                            s_act.inc(ins)
                assert s_act.v == act0 + 64
                assert s_pe.v == pe0 + 80

                for qg in range(4):
                    for qsub in range(2):
                        qt = 2 * qg + qsub
                        bankPV = [pD, pE][qsub]
                        bankMO = [pF, pC][qsub]

                        @blk.vector
                        def _(v, hh=hh, hglob=hglob, qg=qg, qsub=qsub, qt=qt,
                              bankPV=bankPV, bankMO=bankMO, pe0=pe0):
                            v.wait_ge(s_pe.h, pe0 + 20 * qg + 18 + 2 * qsub)
                            vedge(v, v.reciprocal(rcls[:, 0:1],
                                                  bankPV[:, 128:129]))
                            vedge(v, v.reciprocal(rcls[:, 1:2],
                                                  bankMO[:, 128:129]))
                            vedge(v, v.tensor_tensor(rcls[:, 2:3],
                                                     rcls[:, 0:1],
                                                     g1m_bc[:, hglob:hglob + 1],
                                                     ALU.mult))
                            vedge(v, v.tensor_tensor(rcls[:, 3:4],
                                                     rcls[:, 1:2],
                                                     g_bc[:, hglob:hglob + 1],
                                                     ALU.mult))
                            vedge(v, v.tensor_scalar(cscr[:], bankPV[:, 0:128],
                                                     rcls[:, 2:3], None,
                                                     ALU.mult))
                            ins = v.scalar_tensor_tensor(
                                combined[:, 2048 * qt + 128 * hglob:
                                         2048 * qt + 128 * hglob + 128],
                                bankMO[:, 0:128], rcls[:, 3:4], cscr[:],
                                ALU.mult, ALU.add)
                            s_dve.inc(ins)
                assert s_dve.v == dve0 + 8
                for bn in ("pA", "pB"):
                    bank_consumed(bn, s_act)
                for bn in ("pD", "pE", "pF", "pC"):
                    bank_consumed(bn, s_dve)
                ptil_free_pe = s_pe.v
        nc.leave_named_scope(f"P{p}", _sid[f"P{p}"], False)
        if p == 7:
            pairs_proj_pe = s_pe.v
            pairs_trig_dve = s_dve.v

    if debug:
        with nc.Block("dbgB", no_gpsimd_drain=True) as blk:
            @blk.sync
            def _(sy):
                sy.wait_ge(s_dve.h, s_dve.v)
                sy.wait_ge(s_act.h, s_act.v)
                for nm, t in (("d_kbf0", k_bf[0]), ("d_kbf1", k_bf[1]),
                              ("d_kfbf0", kf_bf[0]), ("d_qbf0", q_bf[0]),
                              ("d_qfbf0", qf_bf[0]), ("d_vab0", vab[0]),
                              ("d_memaug0", mem_aug[0]),
                              ("d_memaug1", mem_aug[1]),
                              ("d_negM", negM), ("d_mscr", mscr),
                              ("d_ptil", ptil), ("d_combined", combined)):
                    sy.dma_start(out=dbg[nm], in_=t[:]).then_inc(s_dbg.h, 16)
                    s_dbg.v += 16
    dbg_pair_sdbg = s_dbg.v if debug else 0

    esB.close()
    esX.close()

    esFin = ExitStack()
    ySB = esFin.enter_context(
        nc.sbuf_tensor("ySB", [128, NCH * 1024], F32, side="right"))
    esWo = ExitStack()
    combT = esWo.enter_context(
        nc.sbuf_tensor("combT", [128, NCH * 1024], BF16, side="right"))
    woslab = [esWo.enter_context(nc.sbuf_tensor(f"woslab{i}", [128, 2048], BF16))
              for i in range(2)]
    WoT = esWo.enter_context(nc.sbuf_tensor("WoT", [128, NCH * 128], BF16))

    _sid["WO"] = nc.enter_named_scope("WO", False)[0]
    with nc.Block("wo", no_gpsimd_drain=True) as blk:
        comb_dve = s_dve.v
        for qt in range(8):
            for g8 in range(2):
                bank, bname = tbank()
                ntr += 1

                @blk.tensor
                def _(t, qt=qt, g8=g8, bank=bank, bname=bname,
                      comb_dve=comb_dve):
                    t.wait_ge(s_dve.h, comb_dve)
                    bank_wait(t, bname)
                    last = None
                    for j in range(8):
                        ic = 8 * g8 + j
                        last = t.transpose(
                            bank[:, 128 * j:128 * (j + 1)],
                            combined[:, 2048 * qt + 128 * ic:
                                      2048 * qt + 128 * (ic + 1)],
                            identb[:])
                    s_pe.inc(last)
                tr_done = s_pe.v

                @blk.vector
                def _(v, qt=qt, g8=g8, bank=bank, tr_done=tr_done):
                    v.wait_ge(s_pe.h, tr_done)
                    dst3 = combT[:].rearrange("p (c f) -> p c f", f=1024)
                    ins = v.tensor_copy(dst3[:, 8 * g8:8 * g8 + 8,
                                             128 * qt:128 * (qt + 1)],
                                        bank[:].rearrange("p (c f) -> p c f",
                                                          f=128))
                    s_dve.inc(ins)
                bank_consumed(bname, s_dve)
        combT_act = s_dve.v

        woslab_free = [(s_pe, 0), (s_pe, 0)]
        for jc in range(NCH):
            slot = jc % 2

            @blk.gpsimd
            def _(g, jc=jc, slot=slot):
                if debug and jc == 0:
                    g.wait_ge(s_dbg.h, dbg_pair_sdbg)
                if jc == 0:
                    g.wait_ge(s_pe.h, pairs_proj_pe)
                    g.wait_ge(s_dve.h, pairs_trig_dve)
                ctr, val = woslab_free[slot]
                g.wait_ge(ctr.h, val)
                ins = g.dma_start(out=woslab[slot][:],
                                  in_=wo_d[128 * jc:128 * (jc + 1), :])
                s_wl.inc(ins, 16)
            ld_done = s_wl.v

            for g8 in range(2):
                bank, bname = tbank()
                ntr += 1

                @blk.tensor
                def _(t, slot=slot, g8=g8, bank=bank, bname=bname,
                      ld_done=ld_done):
                    t.wait_ge(s_wl.h, ld_done)
                    bank_wait(t, bname)
                    last = None
                    for j in range(8):
                        ic = 8 * g8 + j
                        last = t.transpose(bank[:, 128 * j:128 * (j + 1)],
                                           woslab[slot][:, 128 * ic:128 * (ic + 1)],
                                           identb[:])
                    s_pe.inc(last)
                tr_done = s_pe.v

                @blk.vector
                def _(v, g8=g8, bank=bank, tr_done=tr_done):
                    v.wait_ge(s_pe.h, tr_done)
                    ins = v.tensor_copy(WoT[:, 1024 * g8:1024 * (g8 + 1)],
                                        bank[:])
                    s_dve.inc(ins)
                bank_consumed(bname, s_dve)
            wot_act = s_dve.v
            woslab_free[slot] = (s_pe, s_pe.v)

            for tch in range(2):
                bank, bname = [(pA, "pA"), (pB, "pB")][tch]

                @blk.tensor
                def _(t, jc=jc, tch=tch, bank=bank, bname=bname,
                      wot_act=wot_act, combT_act=combT_act):
                    t.wait_ge(s_dve.h, max(wot_act, combT_act))
                    bank_wait(t, bname)
                    last = None
                    for ic in range(NCH):
                        last = t.matmul(
                            bank[:],
                            WoT[:, 128 * ic:128 * (ic + 1)],
                            combT[:, 1024 * ic + 512 * tch:
                                  1024 * ic + 512 * tch + 512],
                            start=(ic == 0), stop=(ic == NCH - 1))
                    s_pe.inc(last)
                mm_done = s_pe.v

                @blk.vector
                def _(v, jc=jc, tch=tch, bank=bank, mm_done=mm_done):
                    v.wait_ge(s_pe.h, mm_done)
                    ins = v.tensor_copy(ySB[:, 1024 * jc + 512 * tch:
                                            1024 * jc + 512 * tch + 512],
                                        bank[:])
                    s_dve.inc(ins)
                bank_consumed(bname, s_dve)
    nc.leave_named_scope("WO", _sid["WO"], False)
    ySB_wo_act = s_act.v
    ySB_wo_dve = s_dve.v
    if debug:
        with nc.Block("dbgWo", no_gpsimd_drain=True) as blk:
            @blk.sync
            def _(sy):
                sy.wait_ge(s_act.h, s_act.v)
                sy.dma_start(out=dbg["d_ySBwo"], in_=ySB[:]).then_inc(s_dbg.h, 16)
                s_dbg.v += 16
    dbg_wo_sdbg = s_dbg.v if debug else 0

    esWo.close()
    esCm.close()
    esW.close()

    esC = ExitStack()
    wbuf = [esC.enter_context(nc.sbuf_tensor(f"wbuf{i}", [128, NCH * 768],
                                             BF16)) for i in range(3)]
    h1sb = esC.enter_context(nc.sbuf_tensor("h1sb", [128, 12 * 512], BF16))
    h2sc = [esC.enter_context(nc.sbuf_tensor(f"h2sc{i}", [128, 512], BF16))
            for i in range(2)]
    cSB = esC.enter_context(nc.sbuf_tensor("cSB", [128, 1024], BF16))
    NIH = 6

    wlist = []
    for ih in range(2):
        wlist.append((sg_d[:, 768 * ih:768 * (ih + 1)],
                      su_d[:, 768 * ih:768 * (ih + 1)],
                      sd_d[768 * ih:768 * (ih + 1), :], False, -1))
    for e in range(E):
        for ih in range(2):
            wlist.append((wg_d[e][:, 768 * ih:768 * (ih + 1)],
                          wu_d[e][:, 768 * ih:768 * (ih + 1)],
                          wd_d[e][768 * ih:768 * (ih + 1), :], True, e))

    def emit_wload(blk, src_ap, buf, kind, wait_pe):
        @blk.gpsimd
        def _(g, src_ap=src_ap, buf=buf, kind=kind, wait_pe=wait_pe):
            if wait_pe:
                g.wait_ge(s_pe.h, wait_pe)
            if kind == "gu":
                for grp in range(2):
                    ins = g.dma_start(
                        out=buf[:, 768 * 8 * grp:768 * 8 * (grp + 1)],
                        in_=src_ap[1024 * grp:1024 * (grp + 1), :].rearrange(
                            "(c p) f -> p c f", p=128))
                    s_wl.inc(ins, 16)
            else:
                ins = g.dma_start(
                    out=buf[:, 0:6 * 2048],
                    in_=src_ap.rearrange("(c p) f -> p c f", p=128))
                s_wl.inc(ins, 16)
        return s_wl.v

    dbg_h_sdbg = 0
    wo_done_pe = s_pe.v
    with nc.Block("ffnpre", no_gpsimd_drain=True) as blk:
        ld_done = [None] * (3 * len(wlist))
        ld_done[0] = emit_wload(blk, wlist[0][0], wbuf[0], "gu", wo_done_pe)
        ld_done[1] = emit_wload(blk, wlist[0][1], wbuf[1], "gu", wo_done_pe)
        ld_done[2] = emit_wload(blk, wlist[0][2], wbuf[2], "d", wo_done_pe)
    step_done_pe = [0] * (3 * len(wlist))

    for ei in range(len(wlist)):
        srcg, srcu, srcd, routed, eidx = wlist[ei]
        gb = wbuf[(3 * ei) % 3]
        ub = wbuf[(3 * ei + 1) % 3]
        db = wbuf[(3 * ei + 2) % 3]

        _sid[f"F{ei}"] = nc.enter_named_scope(f"F{ei}", False)[0]
        with nc.Block(f"ffn{ei}", no_gpsimd_drain=True) as blk:
            gate_start_wl = ld_done[3 * ei]
            peG0 = s_pe.v
            actG0 = s_act.v
            for tch in range(2):
                for nIc in range(NIH):
                    bank, bn = [(pA, "pA"), (pB, "pB")][nIc % 2]
                    idx = NIH * tch + nIc

                    @blk.tensor
                    def _(t, tch=tch, nIc=nIc, bank=bank, bn=bn, idx=idx,
                          gate_start_wl=gate_start_wl, gb=gb, actG0=actG0):
                        if idx == 0:
                            t.wait_ge(s_wl.h, gate_start_wl)
                            bank_wait(t, "pA")
                        elif idx == 1:
                            bank_wait(t, "pB")
                        else:
                            t.wait_ge(s_act.h, actG0 + idx - 1)
                        last = None
                        for hc in range(NCH):
                            last = t.matmul(
                                bank[:],
                                gb[:, 768 * hc + 128 * nIc:
                                   768 * hc + 128 * nIc + 128],
                                xnT0[:, 1024 * hc + 512 * tch:
                                     1024 * hc + 512 * tch + 512],
                                start=(hc == 0), stop=(hc == NCH - 1))
                        s_pe.inc(last)

            @blk.scalar
            def _(sc, ei=ei, peG0=peG0, dbg_h=dbg_h_sdbg if debug else 0):
                for tch in range(2):
                    for nIc in range(NIH):
                        idx = NIH * tch + nIc
                        sc.wait_ge(s_pe.h, peG0 + idx + 1)
                        if debug and ei == 1 and idx == 0:
                            sc.wait_ge(s_dbg.h, dbg_h)
                        bank = [pA, pB][nIc % 2]
                        ins = sc.activation(
                            h1sb[:, 512 * idx:512 * idx + 512],
                            bank[:], AF.Silu)
                        s_act.inc(ins)
            assert s_act.v == actG0 + 12
            for bn in ("pA", "pB"):
                bank_consumed(bn, s_act)
            gate_pe_done = s_pe.v
            step_done_pe[3 * ei] = gate_pe_done
            if ei + 1 < len(wlist):
                ld_done[3 * (ei + 1)] = emit_wload(
                    blk, wlist[ei + 1][0], wbuf[(3 * ei + 3) % 3], "gu",
                    gate_pe_done)

            up_start_wl = ld_done[3 * ei + 1]
            if routed:
                @blk.vector
                def _(v, eidx=eidx):
                    ins = v.tensor_scalar(sel_sb[:], iota16[:], float(eidx),
                                          None, ALU.is_equal)
                    s_dve.inc(ins)
                sel_dve = s_dve.v
            else:
                sel_dve = 0

            for tch in range(2):
                dveT0 = s_dve.v
                roff = 1 if routed else 0

                @blk.tensor
                def _(t, tch=tch, dveT0=dveT0, roff=roff, routed=routed,
                      sel_dve=sel_dve, up_start_wl=up_start_wl, ub=ub):
                    if routed:
                        if tch == 0:
                            t.wait_ge(s_dve.h, sel_dve)
                            bank_wait(t, "pC")
                        else:
                            t.wait_ge(s_dve.h, dveT0 - roff - NIH + 1)
                        ins = t.matmul(pC[:], sel_sb[:],
                                       combF[0:16, 512 * tch:512 * (tch + 1)],
                                       start=True, stop=True)
                        s_pe.inc(ins)
                    for nIc in range(NIH):
                        bank = [pD, pE][nIc % 2]
                        if tch == 0 and nIc == 0:
                            t.wait_ge(s_wl.h, up_start_wl)
                        if tch == 0 and nIc < 2:
                            bank_wait(t, ["pD", "pE"][nIc])
                        elif nIc < 2:
                            t.wait_ge(s_dve.h, dveT0 - roff - NIH +
                                      (roff + nIc - 2 + 1) + NIH)
                        else:
                            t.wait_ge(s_dve.h, dveT0 + roff + nIc - 1)
                        last = None
                        for hc in range(NCH):
                            last = t.matmul(
                                bank[:],
                                ub[:, 768 * hc + 128 * nIc:
                                   768 * hc + 128 * nIc + 128],
                                xnT0[:, 1024 * hc + 512 * tch:
                                     1024 * hc + 512 * tch + 512],
                                start=(hc == 0), stop=(hc == NCH - 1))
                        s_pe.inc(last)
                peU = s_pe.v

                @blk.vector
                def _(v, tch=tch, routed=routed, peU=peU, roff=roff):
                    nmm = NIH + (1 if routed else 0)
                    base_pe = peU - nmm
                    if routed:
                        v.wait_ge(s_pe.h, base_pe + 1)
                        ins = v.tensor_copy(cSB[:, 512 * tch:512 * (tch + 1)],
                                            pC[:])
                        s_dve.inc(ins)
                    for nIc in range(NIH):
                        bank = [pD, pE][nIc % 2]
                        v.wait_ge(s_pe.h, base_pe + roff + nIc + 1)
                        hsl = h1sb[:, 512 * (NIH * tch + nIc):
                                   512 * (NIH * tch + nIc) + 512]
                        if routed:
                            v.tensor_tensor(hsl, hsl, bank[:], ALU.mult)
                            ins = v.tensor_tensor(
                                hsl, hsl, cSB[:, 512 * tch:512 * (tch + 1)],
                                ALU.mult)
                        else:
                            ins = v.tensor_tensor(hsl, hsl, bank[:], ALU.mult)
                        s_dve.inc(ins)
                assert s_dve.v == dveT0 + roff + NIH
            for bn in ("pD", "pE", "pC"):
                bank_consumed(bn, s_dve)
            up_pe_done = s_pe.v
            step_done_pe[3 * ei + 1] = up_pe_done
            h_dve_done = s_dve.v
            if ei == 0 and debug:
                @blk.sync
                def _(sy, h_dve_done=h_dve_done):
                    sy.wait_ge(s_dve.h, h_dve_done)
                    sy.dma_start(out=dbg["d_h1sb"],
                                 in_=h1sb[:]).then_inc(s_dbg.h, 16)
                    s_dbg.v += 16
            dbg_h_sdbg = s_dbg.v if debug else 0
            if ei + 1 < len(wlist):
                ld_done[3 * (ei + 1) + 1] = emit_wload(
                    blk, wlist[ei + 1][1], wbuf[(3 * ei + 4) % 3], "gu",
                    up_pe_done)

            down_start_wl = ld_done[3 * ei + 2]
            for tch in range(2):
                for jc in range(NCH):
                    bank, bn = [(pF, "pF"), (pA, "pA")][jc % 2]

                    @blk.tensor
                    def _(t, tch=tch, jc=jc, bank=bank, bn=bn,
                          down_start_wl=down_start_wl, h_dve_done=h_dve_done,
                          db=db):
                        if tch == 0 and jc == 0:
                            t.wait_ge(s_wl.h, down_start_wl)
                            t.wait_ge(s_dve.h, h_dve_done)
                        bank_wait(t, bn)
                        last = None
                        for nIc in range(NIH):
                            last = t.matmul(
                                bank[:],
                                db[:, 2048 * nIc + 128 * jc:
                                   2048 * nIc + 128 * jc + 128],
                                h1sb[:, 512 * (NIH * tch + nIc):
                                     512 * (NIH * tch + nIc) + 512],
                                start=(nIc == 0), stop=(nIc == NIH - 1))
                        s_pe.inc(last)
                    dn_done = s_pe.v

                    @blk.vector
                    def _(v, tch=tch, jc=jc, bank=bank, dn_done=dn_done,
                          ei=ei):
                        v.wait_ge(s_pe.h, dn_done)
                        if ei == 0 and tch == 0 and jc == 0:
                            v.wait_ge(s_dve.h, ySB_wo_dve)
                            if debug:
                                v.wait_ge(s_dbg.h, dbg_wo_sdbg)
                        ysl = ySB[:, 1024 * jc + 512 * tch:
                                  1024 * jc + 512 * tch + 512]
                        ins = v.tensor_tensor(ysl, ysl, bank[:], ALU.add)
                        s_dve.inc(ins)
                    bank_consumed(bn, s_dve)
            down_pe_done = s_pe.v
            step_done_pe[3 * ei + 2] = down_pe_done
            if ei + 1 < len(wlist):
                ld_done[3 * (ei + 1) + 2] = emit_wload(
                    blk, wlist[ei + 1][2], wbuf[(3 * ei + 5) % 3], "d",
                    down_pe_done)
        nc.leave_named_scope(f"F{ei}", _sid[f"F{ei}"], False)
    ffn_done_pe = s_pe.v
    esC.close()

    esD = ExitStack()
    xfin = [esD.enter_context(nc.sbuf_tensor(f"xfin{i}", [128, H], F32))
            for i in range(2)]
    ofin = [esD.enter_context(nc.sbuf_tensor(f"ofin{i}", [128, H], F32))
            for i in range(2)]
    xfin_free = [(s_dve, 0), (s_dve, 0)]
    ofin_free = [(s_st, 0), (s_st, 0)]
    ySB_full_dve = s_dve.v

    _sid["FIN"] = nc.enter_named_scope("FIN", False)[0]
    with nc.Block("fin", no_gpsimd_drain=True) as blk:
        for qt in range(8):
            pp = qt % 2

            @blk.sync
            def _(sy, qt=qt, pp=pp):
                if qt < 2:
                    sy.wait_ge(s_pe.h, ffn_done_pe)
                ctr, val = xfin_free[pp]
                sy.wait_ge(ctr.h, val)
                ins = sy.dma_start(out=xfin[pp][:],
                                   in_=x_d[128 * qt:128 * (qt + 1), :])
                s_ldx.inc(ins, 16)
            x_ready = s_ldx.v

            for g4 in range(4):
                bank, bname = [(pD, "pD"), (pE, "pE")][g4 % 2]

                @blk.tensor
                def _(t, qt=qt, g4=g4, bank=bank, bname=bname,
                      ySB_full_dve=ySB_full_dve):
                    if qt == 0 and g4 == 0:
                        t.wait_ge(s_dve.h, ySB_full_dve)
                    bank_wait(t, bname)
                    last = None
                    for j in range(4):
                        jc = 4 * g4 + j
                        last = t.transpose(
                            bank[:, 128 * j:128 * (j + 1)],
                            ySB[:, 1024 * jc + 128 * qt:
                                1024 * jc + 128 * (qt + 1)],
                            identf[:])
                    s_pe.inc(last)
                tr_done = s_pe.v

                @blk.vector
                def _(v, qt=qt, pp=pp, g4=g4, bank=bank, tr_done=tr_done,
                      x_ready=x_ready):
                    v.wait_ge(s_pe.h, tr_done)
                    v.wait_ge(s_ldx.h, x_ready)
                    if g4 == 0:
                        ctr, val = ofin_free[pp]
                        v.wait_ge(ctr.h, val)
                    ins = v.tensor_tensor(ofin[pp][:, 512 * g4:512 * (g4 + 1)],
                                          xfin[pp][:, 512 * g4:512 * (g4 + 1)],
                                          bank[:], ALU.add)
                    s_dve.inc(ins)
                bank_consumed(bname, s_dve)
            add_done = s_dve.v
            xfin_free[pp] = (s_dve, s_dve.v)

            @blk.sync
            def _(sy, qt=qt, pp=pp, add_done=add_done):
                sy.wait_ge(s_dve.h, add_done)
                ins = sy.dma_start(out=out_d[128 * qt:128 * (qt + 1), :],
                                   in_=ofin[pp][:])
                s_st.inc(ins, 16)
            ofin_free[pp] = (s_st, s_st.v)

        @blk.sync
        def _(sy):
            sy.wait_ge(s_st.h, s_st.v)
            if debug:
                sy.wait_ge(s_dbg.h, s_dbg.v)

    nc.leave_named_scope("FIN", _sid["FIN"], False)
    esD.close()
    esFin.close()
    es.close()
    return nc


_NC_CACHE = None


def kernel(L0_attn_hidden_states, L0_attn_Wq, L0_attn_Wk, L0_attn_Wv,
           L0_attn_Wo, L0_attn_gate, L0_ffn_gate_w, L0_ffn_Wg, L0_ffn_Wu,
           L0_ffn_Wd, L0_ffn_Sg, L0_ffn_Su, L0_ffn_Sd):
    global _NC_CACHE
    x = np.ascontiguousarray(np.asarray(L0_attn_hidden_states, dtype=np.float32))
    wq = np.ascontiguousarray(np.asarray(L0_attn_Wq, dtype=np.float32))
    wk = np.ascontiguousarray(np.asarray(L0_attn_Wk, dtype=np.float32))
    wv = np.ascontiguousarray(np.asarray(L0_attn_Wv, dtype=np.float32))
    wo = np.ascontiguousarray(np.asarray(L0_attn_Wo, dtype=np.float32))
    gate = np.ascontiguousarray(
        np.asarray(L0_attn_gate, dtype=np.float32).reshape(1, NH))
    gw = np.ascontiguousarray(np.asarray(L0_ffn_gate_w, dtype=np.float32))
    wg = np.ascontiguousarray(np.asarray(L0_ffn_Wg, dtype=np.float32))
    wu = np.ascontiguousarray(np.asarray(L0_ffn_Wu, dtype=np.float32))
    wd = np.ascontiguousarray(np.asarray(L0_ffn_Wd, dtype=np.float32))
    sg = np.ascontiguousarray(np.asarray(L0_ffn_Sg, dtype=np.float32))
    su = np.ascontiguousarray(np.asarray(L0_ffn_Su, dtype=np.float32))
    sd = np.ascontiguousarray(np.asarray(L0_ffn_Sd, dtype=np.float32))

    if _NC_CACHE is None:
        _NC_CACHE = build()
    nc = _NC_CACHE

    in_maps = []
    for c in range(8):
        b, half = c // 2, c % 2
        xb = x[b]
        if half:
            xb = np.ascontiguousarray(np.roll(xb, -1024, axis=0))
        tbase = np.array([[(512 * t + 1024 * half) % 2048 for t in range(4)]],
                         dtype=np.float32)
        in_maps.append(dict(x=xb, wq=wq, wk=wk, wv=wv, wo=wo, gate=gate,
                            gw=gw, wg=wg, wu=wu, wd=wd, sg=sg, su=su, sd=sd,
                            tbase=tbase))
    res = run_bass_kernel_spmd(nc, in_maps, core_ids=list(range(8)))
    out = np.empty((B, S, H), dtype=np.float32)
    for c in range(8):
        b, half = c // 2, c % 2
        out[b, 1024 * half:1024 * half + 1024] = res.results[c]["out"]
    return out
